# version 05
# baseline (speedup 1.0000x reference)
import contextlib
import numpy as np
import ml_dtypes

import concourse.bass as bass
import concourse.mybir as mybir
from concourse.bass_utils import run_bass_kernel_spmd

F32 = mybir.dt.float32
BF16 = mybir.dt.bfloat16
I32 = mybir.dt.int32
AF = mybir.ActivationFunctionType
ALU = mybir.AluOpType
AX = mybir.AxisListType

P = 128
S = 2048
D = 1024
NCH = D // P
NT = S // P
QG = 512
NG = S // QG
NH = 16
HD = 64
HIDP = 2816
NEG = -1048576.0
EPS = 1e-5

ENGINES = ("sync", "tensor", "vector", "scalar", "gpsimd")
BLOCKS = ("L0A", "L0F", "L1A", "L1F")


class Prog:
    def __init__(self, nc, es):
        self.nc = nc
        self.es = es
        self.q = {e: [] for e in ENGINES}
        self.meta = {e: [] for e in ENGINES}
        self.sems = {}
        self.cnt = {}
        self.last_wait = {e: {} for e in ENGINES}

    def sem(self, name):
        self.sems[name] = self.es.enter_context(self.nc.semaphore(name))
        self.cnt[name] = 0

    def now(self, name):
        return self.cnt[name]

    def on(self, eng, builder, waits=(), inc=None, inc_n=1):
        own = {"vector": "vv", "scalar": "ss"}.get(eng)
        if own is not None and self.cnt[own]:
            waits = list(waits) + [(own, self.cnt[own])]
        ww = []
        lw = self.last_wait[eng]
        for s, v in waits:
            if v > 0 and lw.get(s, 0) < v:
                ww.append((s, v))
                lw[s] = v
        post = None
        if inc is not None:
            self.cnt[inc] += inc_n
            post = self.cnt[inc]
        sems = self.sems

        def run(engine, ww=tuple(ww), builder=builder, inc=inc, inc_n=inc_n):
            for s, v in ww:
                engine.wait_ge(sems[s], v)
            try:
                ins = builder(engine)
            except Exception as ex:
                co = getattr(builder, "__code__", None)
                raise RuntimeError(
                    f"emit at {co.co_filename}:{co.co_firstlineno}: {ex}") from ex
            if inc is not None:
                ins.then_inc(sems[inc], inc_n)
        self.q[eng].append(run)
        self.meta[eng].append((tuple(ww), ((inc, inc_n),) if inc else ()))
        return post

    def fence(self, eng, sem):
        self.on(eng, lambda e: e.nop(), waits=[(sem, self.now(sem))])

    def barrier(self):
        for e in ENGINES:
            self.on(e, lambda eng: eng.nop(), inc="ph")
        v1 = self.now("ph")
        for e in ENGINES:
            self.on(e, lambda eng: eng.nop(), waits=[("ph", v1)])
        return v1

    def check(self):
        final = dict(self.cnt)
        for e in ENGINES:
            for ww, _ in self.meta[e]:
                for s, v in ww:
                    assert v <= final[s], \
                        f"wait {s}>={v} unreachable (final {final[s]}) on {e}"
        pos = {e: 0 for e in ENGINES}
        val = {s: 0 for s in self.cnt}
        progress = True
        while progress:
            progress = False
            for e in ENGINES:
                while pos[e] < len(self.meta[e]):
                    ww, incs = self.meta[e][pos[e]]
                    if all(val[s] >= v for s, v in ww):
                        for s, n in incs:
                            val[s] += n
                        pos[e] += 1
                        progress = True
                    else:
                        break
        stuck = {e: (pos[e], len(self.meta[e])) for e in ENGINES
                 if pos[e] < len(self.meta[e])}
        if stuck:
            detail = {e: self.meta[e][pos[e]][0] for e in stuck}
            raise RuntimeError(f"deadlock: {stuck}; waits {detail}")


class Arena:
    def __init__(self, nc, es, name, dtype, words):
        self.t = es.enter_context(nc.sbuf_tensor(name, [P, words], dtype))
        self.words = words
        self.off = 0
        self.peak = 0
        self.marks = []

    def take(self, n, tag=""):
        n = (n + 3) & ~3
        assert self.off + n <= self.words, \
            f"arena overflow {tag}: {self.off}+{n}>{self.words}"
        v = self.t[:, self.off:self.off + n]
        self.map = getattr(self, "map", {})
        self.map[tag] = (self.off, n)
        self.off += n
        self.peak = max(self.peak, self.off)
        return v

    def push(self):
        self.marks.append(self.off)

    def pop(self):
        self.off = self.marks.pop()


def _rotary_tables():
    n, dh, window = S, HD, 512
    half = np.arange(0, dh, 2, dtype=np.float64)
    inv_freq = 1.0 / 10000.0 ** (half / dh)
    t = np.arange(n, dtype=np.float64)
    freqs = t[:, None] * inv_freq[None, :]
    freqs = np.concatenate([freqs, freqs], axis=-1)
    base = (half + 0.4 * dh) / (1.4 * dh)
    power = (t - n // 2) / float(window // 2)
    scale = base[None, :] ** power[:, None]
    scale = np.concatenate([scale, scale], axis=-1)
    cos, sin = np.cos(freqs), np.sin(freqs)
    sgn = np.concatenate([-np.ones(32), np.ones(32)])[None, :]
    qc = (cos * scale * 8.0).astype(np.float32)
    qs = (sin * scale * 8.0 * sgn).astype(np.float32)
    kc = (cos / scale).astype(np.float32)
    ks = (sin / scale * sgn).astype(np.float32)
    return qc, qs, kc, ks


def _consts():
    bf = ml_dtypes.bfloat16
    c = {}
    c["identity"] = np.eye(P, dtype=np.float32).astype(bf)
    c["ones_row"] = np.ones((1, P), dtype=np.float32).astype(bf)
    c["ones_row_f"] = np.ones((1, P), dtype=np.float32)
    c["ones_col"] = np.ones((P, 1), dtype=np.float32).astype(bf)
    c["idrow16"] = np.eye(16, dtype=np.float32)
    hm = np.zeros((P, NCH * NH), dtype=np.float32)
    for ch in range(NCH):
        hm[0:64, ch * NH + 2 * ch] = 1.0
        hm[64:128, ch * NH + 2 * ch + 1] = 1.0
    c["headmask16"] = hm.astype(bf)
    ki = np.arange(P)[:, None]
    qi = np.arange(P)[None, :]
    c["mask_m4"] = (np.where(ki < qi, NEG, 0.0)).astype(bf)
    c["mask_m0"] = (np.where(ki > qi, NEG, 0.0)).astype(bf)
    st = np.zeros((P, NT * 32), dtype=np.float32)
    for kt in range(NT):
        kabs = kt * P + np.arange(P)[:, None]
        j = np.arange(16)[None, :]
        st[:, kt * 32:kt * 32 + 16] = (kabs <= 128 * j - 128)
        st[:, kt * 32 + 16:kt * 32 + 32] = (kabs >= 128 * j + 128)
    c["lr_step"] = st.astype(bf)
    sel = np.zeros((32, S), dtype=np.float32)
    for j in range(NT):
        sel[j, j * P:(j + 1) * P] = 1.0
        sel[16 + j, j * P:(j + 1) * P] = 1.0
    c["selmask"] = sel.astype(bf)
    e2 = np.zeros((2, 32), dtype=np.float32)
    e2[0, 0:16] = 1.0
    e2[1, 16:32] = 1.0
    c["edge2"] = e2.astype(bf)
    c["c2p65"] = np.full((16, 1), 2.0 ** 65, np.float32)
    c["ones_2col"] = np.ones((P, 2), np.float32).astype(bf)
    c["identity_f"] = np.eye(P, dtype=np.float32)
    tr_ = (np.arange(P)[None, :] < np.arange(P)[:, None]).astype(np.float32)
    c["tri"] = tr_.astype(bf)
    c["trin"] = (-tr_).astype(bf)
    ps2 = np.zeros((2, P), np.float32)
    ps2[0, 0:64] = 1.0
    ps2[1, 64:128] = 1.0
    c["pairsel2"] = ps2.astype(bf)
    perm = np.zeros((P, P), np.float32)
    for blk in range(2):
        for d in range(32):
            perm[blk * 64 + d + 32, blk * 64 + d] = 1.0
            perm[blk * 64 + d, blk * 64 + d + 32] = 1.0
    c["rotperm"] = perm.astype(bf)
    qc, qs, kc, ks = _rotary_tables()
    for nme, tab in (("fm_qc", qc), ("fm_qs", qs), ("fm_kc", kc), ("fm_ks", ks)):
        fm = np.tile(tab.T, (2, 1))
        c[nme] = np.ascontiguousarray(fm).astype(bf)
    qc, qs, kc, ks = _rotary_tables()
    c["rot_qc"], c["rot_qs"], c["rot_kc"], c["rot_ks"] = qc, qs, kc, ks
    return c


def _cols(v, pad_to=None):
    v = np.asarray(v, dtype=np.float32).ravel()
    if pad_to is not None and v.size < pad_to:
        v = np.concatenate([v, np.zeros(pad_to - v.size, np.float32)])
    return np.ascontiguousarray(v.reshape(v.size // P, P).T)


def stage_inputs(inputs):
    f = lambda a: np.ascontiguousarray(np.asarray(a, dtype=np.float32))
    cm = {}
    cm["l0a_wqkv"] = np.ascontiguousarray(f(inputs["L0_attn_qkv_w"]).T)
    cm["l0a_bqkv_c"] = _cols(inputs["L0_attn_qkv_b"])
    cm["l0a_wout"] = np.ascontiguousarray(f(inputs["L0_attn_out_w"]).T)
    cm["l0a_bout_c"] = _cols(inputs["L0_attn_out_b"])
    w1 = f(inputs["L0_ffn_w1"])
    w2 = f(inputs["L0_ffn_w2"])
    b1 = f(inputs["L0_ffn_b1"])
    HID = 2730
    for l in range(2):
        cm[f"l0f{l}_wqkv"] = f(inputs["L0_ffn_wqkv"][l])
        cm[f"l0f{l}_wo"] = f(inputs["L0_ffn_wo"][l])
        w1a = np.zeros((D, HIDP), np.float32); w1a[:, :HID] = w1[l][:, :HID]
        w1g = np.zeros((D, HIDP), np.float32); w1g[:, :HID] = w1[l][:, HID:]
        cm[f"l0f{l}_w1a"] = w1a
        cm[f"l0f{l}_w1g"] = w1g
        w2p = np.zeros((HIDP, D), np.float32); w2p[:HID] = w2[l]
        cm[f"l0f{l}_w2"] = w2p
        cm[f"l0f{l}_b1a_c"] = _cols(b1[l][:HID], HIDP)
        cm[f"l0f{l}_b1g_c"] = _cols(b1[l][HID:], HIDP)
        cm[f"l0f{l}_b2_c"] = _cols(inputs["L0_ffn_b2"][l])
        cm[f"l0f{l}_lng_c"] = _cols(inputs["L0_ffn_ln_g"][l])
        cm[f"l0f{l}_lnb_c"] = _cols(inputs["L0_ffn_ln_b"][l])
        cm[f"l0f{l}_qscale"] = f(inputs["L0_ffn_q_scale"][l]).reshape(1, HD)
        cm[f"l0f{l}_kscale"] = f(inputs["L0_ffn_k_scale"][l]).reshape(1, HD)
        cm[f"l0f{l}_qscale_c"] = np.ascontiguousarray(
            np.tile(f(inputs["L0_ffn_q_scale"][l]).reshape(1, HD), (1, 2)).reshape(P, 1))
        cm[f"l0f{l}_kscale_c"] = np.ascontiguousarray(
            np.tile(f(inputs["L0_ffn_k_scale"][l]).reshape(1, HD), (1, 2)).reshape(P, 1))
    cm["l1a_wq"] = np.ascontiguousarray(f(inputs["L1_attn_Wq"]).T)
    cm["l1a_wk"] = np.ascontiguousarray(f(inputs["L1_attn_Wk"]).T)
    cm["l1a_wv"] = np.ascontiguousarray(f(inputs["L1_attn_Wv"]).T)
    cm["l1a_wo"] = np.ascontiguousarray(f(inputs["L1_attn_Wo"]).T)
    cm["l1a_bq_c"] = _cols(inputs["L1_attn_bq"])
    cm["l1a_bk_c"] = _cols(inputs["L1_attn_bk"])
    cm["l1a_bv_c"] = _cols(inputs["L1_attn_bv"])
    cm["l1a_bo_c"] = _cols(inputs["L1_attn_bo"])
    _rkT = np.ascontiguousarray(f(inputs["L1_attn_rel_k_emb"]).T)
    cm["l1a_relkT"] = np.concatenate([_rkT, _rkT], axis=0)
    rv = f(inputs["L1_attn_rel_v_emb"])
    rvm = np.zeros((256, HD), np.float32)
    rvm[:255] = rv[1:256][::-1]
    cm["l1a_relv_mid"] = np.concatenate([rvm[0:128], rvm[128:256]], axis=1)
    cm["l1a_relv_edge"] = np.ascontiguousarray(np.stack([rv[256], rv[0]]))
    cm["l1a_rv0"] = np.ascontiguousarray(rv[0:1, :])
    cm["l1f_wqkv"] = f(inputs["L1_ffn_W_qkv"])
    cm["l1f_wproj"] = f(inputs["L1_ffn_W_proj"])
    cm["l1f_bproj_c"] = _cols(inputs["L1_ffn_b_proj"])
    cm["l1f_qg"] = np.ascontiguousarray(f(inputs["L1_ffn_g_info"][0, 0]).reshape(NH, HD))
    cm.update(_consts())
    x = np.asarray(inputs["L0_attn_x"], dtype=np.float32)
    mixes = []
    for b in range(8):
        mx = np.zeros((8, P, P), np.float32)
        kA, hA = b // 2, b % 2
        for d in range(HD):
            mx[kA, hA * HD + d, d] = 1.0
            mx[4 + kA, hA * HD + d, HD + d] = 1.0
        mixes.append(np.ascontiguousarray(mx.reshape(8 * P, P)))
    return [dict(cm, x_in=np.ascontiguousarray(x[b].T), l1a_mix=mixes[b])
            for b in range(8)]


class Ctx:
    DBG = False


def dbg(c, ap, waits, slot, n, cast=False):
    if not getattr(c, "DBG", False):
        return
    q = "gpsimd" if cast else "sync"
    c.pr.on(q, lambda e: e.dma_start(
        out=c.dbg_out[0:ap.shape[0], slot:slot + n], in_=ap,
        allow_slow_non_contiguous=True),
        waits=waits, inc="so", inc_n=16)


def bw(c, *keys):
    out = []
    for k in keys:
        sem, cnt = c.bank[k]
        if cnt:
            out.append((sem, cnt))
    return out


def br(c, key, sem, cnt):
    c.bank[key] = (sem, cnt)


def mmv(c, ps_ap, lhsT, rhs, start, stop, waits=(), inc=None, tp=None):
    def b(eng, ps_ap=ps_ap, lhsT=lhsT, rhs=rhs, start=start, stop=stop, tp=tp):
        return eng.matmul(ps_ap, lhsT, rhs, start=start, stop=stop,
                          tile_position=tp)
    return c.pr.on("tensor", b, waits=waits, inc=inc)


def ins(c, eng, opname, *args, waits=(), inc=None, inc_n=1, **kw):
    def b(e, opname=opname, args=args, kw=kw):
        return getattr(e, opname)(*args, **kw)
    return c.pr.on(eng, b, waits=waits, inc=inc, inc_n=inc_n)


def vin(c, opname, *args, waits=(), inc="vv", **kw):
    return ins(c, "vector", opname, *args, waits=waits, inc=inc, **kw)


def sin(c, opname, *args, waits=(), inc="ss", **kw):
    return ins(c, "scalar", opname, *args, waits=waits, inc=inc, **kw)


def dma(c, qname, out, in_, waits=(), inc=None, **kw):
    return ins(c, qname, "dma_start", waits=waits, inc=inc, inc_n=16,
               out=out, in_=in_, **kw)


def load_w(c, dst, src_ap, waits=()):
    return c.pr.on("gpsimd", lambda e: e.dma_start(out=dst, in_=src_ap),
                   waits=waits, inc="wd", inc_n=16)


def vop(c, builder, waits=(), inc="vv"):
    return c.pr.on("vector", builder, waits=waits, inc=inc)


def sop(c, builder, waits=(), inc="ss"):
    return c.pr.on("scalar", builder, waits=waits, inc=inc)


def bc_row16(c, row_b16, n, out_b16, waits=()):
    ps = c.ps_f[:, 0:n]
    t = mmv(c, ps, c.cst["ones_row"][0:1, :], row_b16, True, True,
            waits=list(waits) + bw(c, "f"), inc="tt")
    cnt = vin(c, "tensor_copy", out_b16, ps, waits=[("tt", t)])
    br(c, "f", "vv", cnt)
    return cnt


def bc_rowf(c, row_f32, n, out_f32, waits=()):
    ps = c.ps_f[:, 0:n]
    t = mmv(c, ps, c.cst["ones_row_f"][0:1, :], row_f32, True, True,
            waits=list(waits) + bw(c, "f"), inc="tt")
    cnt = vin(c, "tensor_copy", out_f32, ps, waits=[("tt", t)])
    br(c, "f", "vv", cnt)
    return cnt


def ln_rows(c, xb_of, rows, waits, n=QG):
    onescol = c.cst["ones_col"]
    sq2 = c.lnsq
    mcnt = qcnt = None
    prev = [0, 0]
    for ch in range(NCH):
        w = list(waits) + (bw(c, "a") if ch == 0 else [])
        mcnt = mmv(c, c.ps_a[0:1, 0:n], onescol[:], xb_of(ch),
                   start=(ch == 0), stop=(ch == NCH - 1), waits=w, inc="tt")
        if ch == 0:
            br(c, "a", "tt", mcnt)
        scnt = sin(c, "activation", sq2[ch % 2][:, 0:n], xb_of(ch), AF.Square,
                   waits=w + ([("tt", prev[ch % 2])] if prev[ch % 2] else []))
        qcnt = mmv(c, c.ps_a[0:1, QG:QG + n], onescol[:], sq2[ch % 2][:, 0:n],
                   start=(ch == 0), stop=(ch == NCH - 1),
                   waits=[("ss", scnt)], inc="tt")
        prev[ch % 2] = qcnt
    mu, rstd, tmp = rows["mu"], rows["rstd"], rows["tmp"]
    vin(c, "tensor_scalar", mu, c.ps_a[0:1, 0:n], 1.0 / D, None, ALU.mult,
        waits=[("tt", mcnt)])
    vin(c, "tensor_tensor", tmp, mu, mu, ALU.mult)
    vin(c, "scalar_tensor_tensor", tmp, c.ps_a[0:1, QG:QG + n], 1.0 / D, tmp,
        ALU.mult, ALU.subtract, waits=[("tt", qcnt)])
    v1 = vin(c, "tensor_scalar", tmp, tmp, EPS, None, ALU.add)
    s1 = sin(c, "activation", tmp, tmp, AF.Sqrt, waits=[("vv", v1)])
    cnt = vin(c, "reciprocal", rstd, tmp, waits=[("ss", s1)])
    br(c, "a", "tt", qcnt)
    br(c, "a", "vv", cnt)
    return cnt


def pow2_upper_sqrt(c, col, nrows, waits):
    ci = col.bitcast(I32)
    vin(c, "tensor_scalar", ci, ci, 1, None, ALU.logical_shift_right,
        waits=waits)
    return vin(c, "tensor_tensor", col, col, c.cst["c2p65"][0:nrows, 0:1],
               ALU.mult)


def neg_bias(c, q2, k2, scale, bias_out, nh, waits):
    t = c.nbcol[0:nh, 0:1]
    v0 = vin(c, "tensor_tensor", t, q2, k2, ALU.mult,
             waits=list(waits) + [("tt", c.bank["nb"][1])])
    s0 = sin(c, "activation", t, t, AF.Ln, waits=[("vv", v0)])
    s1 = sin(c, "activation", t, t, AF.Exp, scale=0.5)
    v1 = vin(c, "tensor_scalar", t, t, -scale * 1.01, None, ALU.mult,
             waits=[("ss", s1)])
    rowp = c.ps_f[0:1, 0:nh]
    t1 = mmv(c, rowp, t, c.cst["idrow16"][0:nh, 0:nh], True, True,
             waits=[("vv", v1)] + bw(c, "f"), inc="tt")
    row = c.nbrow[0:1, 0:nh]
    v2 = vin(c, "tensor_copy", row, rowp, waits=[("tt", t1)])
    ps = c.ps_f[:, 64:64 + nh]
    t2 = mmv(c, ps, c.cst["ones_row_f"][0:1, :], row, True, True,
             waits=[("vv", v2)], inc="tt")
    c.bank["nb"] = ("tt", t2)
    cnt = vin(c, "tensor_copy", bias_out, ps, waits=[("tt", t2)])
    br(c, "f", "vv", cnt)
    return cnt


def pre_ln(c, xln, waits):
    ab, af = c.ab, c.af
    af.push(); ab.push()
    rows = {k: af.take(512, k)[0:1, :] for k in ("mu", "rstd", "tmp")}
    mubc = af.take(QG, "mubc")
    rstdbc = af.take(QG, "rstdbc")
    tbuf = af.take(QG, "t")
    xb2 = [ab.take(QG, "xb0"), ab.take(QG, "xb1")]
    last = 0
    last_q = [0, 0]
    last_c = [0, 0]
    last_s = [0, 0]
    for g in range(NG):
        gsl = slice(g * QG, (g + 1) * QG)
        onescol = c.cst["ones_col"]
        sq2 = c.lnsq
        mcnt = qcnt = None
        for ch in range(NCH):
            cnt = vin(c, "tensor_copy", xb2[ch % 2][:], c.XD[ch][:, gsl],
                      waits=list(waits) +
                      ([("tt", last_q[ch % 2])] if last_q[ch % 2] else []) +
                      ([("vv", last_c[ch % 2])] if last_c[ch % 2] else []))
            last_c[ch % 2] = cnt
            w = [("vv", cnt)] + (bw(c, "a") if ch == 0 and g == 0 else [])
            mcnt = mmv(c, c.ps_a[0:1, 0:QG], onescol[:], xb2[ch % 2][:],
                       start=(ch == 0), stop=(ch == NCH - 1), waits=w, inc="tt")
            if ch == 0 and g == 0:
                br(c, "a", "tt", mcnt)
            scnt = sin(c, "activation", sq2[ch % 2][:, 0:QG], xb2[ch % 2][:],
                       AF.Square,
                       waits=w + ([("tt", last_q[ch % 2])] if last_q[ch % 2] else [])
                       + ([("ss", last_s[ch % 2])] if last_s[ch % 2] else []))
            last_s[ch % 2] = scnt
            qcnt = mmv(c, c.ps_a[0:1, QG:QG + QG], onescol[:], sq2[ch % 2][:, 0:QG],
                       start=(ch == 0), stop=(ch == NCH - 1),
                       waits=[("ss", scnt)], inc="tt")
            last_q[ch % 2] = qcnt
        mu, rstd, tmp = rows["mu"], rows["rstd"], rows["tmp"]
        vin(c, "tensor_scalar", mu, c.ps_a[0:1, 0:QG], 1.0 / D, None, ALU.mult,
            waits=[("tt", mcnt)])
        vin(c, "tensor_tensor", tmp, mu, mu, ALU.mult)
        vin(c, "scalar_tensor_tensor", tmp, c.ps_a[0:1, QG:2 * QG], 1.0 / D,
            tmp, ALU.mult, ALU.subtract, waits=[("tt", qcnt)])
        v1 = vin(c, "tensor_scalar", tmp, tmp, EPS, None, ALU.add)
        s1 = sin(c, "activation", tmp, tmp, AF.Sqrt, waits=[("vv", v1)])
        vcn = vin(c, "reciprocal", rstd, tmp, waits=[("ss", s1)])
        br(c, "a", "vv", vcn)
        b1 = bc_rowf(c, mu, QG, mubc, waits=[("vv", vcn)])
        b2 = bc_rowf(c, rstd, QG, rstdbc, waits=[("vv", vcn)])
        if g == 0:
            dbg(c, c.XD[0][:, 0:512], list(waits), 4608, 512)
            dbg(c, mubc[:, 0:512], [("vv", b1)], 5120, 512)
            dbg(c, rstdbc[:, 0:512], [("vv", b2)], 5632, 512)
        for ch in range(NCH):
            vin(c, "tensor_tensor", tbuf, c.XD[ch][:, gsl], mubc, ALU.subtract,
                waits=[("vv", b2)])
            last = vin(c, "tensor_tensor", xln[ch][:, gsl], tbuf, rstdbc,
                       ALU.mult)
    af.pop(); ab.pop()
    return last


def post_ln(c, yb_g, g, waits):
    ab, af = c.ab, c.af
    af.push(); ab.push()
    rows = {k: af.take(512, k)[0:1, :] for k in ("mu", "rstd", "tmp")}
    mubc16 = ab.take(QG, "mubc16")
    mur16 = ab.take(512, "mur16")[0:1, :]
    rstdbc = af.take(QG, "rstdbc")
    tbuf = af.take(QG, "t")
    gsl = slice(g * QG, (g + 1) * QG)
    lcnt = ln_rows(c, lambda ch: yb_g[ch][:], rows, list(waits))
    m16 = vin(c, "tensor_copy", mur16, rows["mu"], waits=[("vv", lcnt)])
    b1 = bc_row16(c, mur16, QG, mubc16, waits=[("vv", m16)])
    b2 = bc_rowf(c, rows["rstd"], QG, rstdbc, waits=[("vv", lcnt)])
    last = 0
    for ch in range(NCH):
        vin(c, "tensor_tensor", tbuf, yb_g[ch][:], mubc16, ALU.subtract,
            waits=[("vv", b2)])
        vin(c, "tensor_tensor", tbuf, tbuf, rstdbc, ALU.mult)
        last = vin(c, "tensor_tensor", c.XD[ch][:, gsl], c.XD[ch][:, gsl],
                   tbuf, ALU.add)
    af.pop(); ab.pop()
    return last


def emit_l0a(c, xln, xln_ready):
    pr, ab, af = c.pr, c.ab, c.af
    ab.push(); af.push()
    WQKV = c.dram["l0a_wqkv"]
    ofm = [ab.take(S, f"ofm{i}") for i in range(NCH)]
    wout = [ab.take(D, f"wo{i}") for i in range(NCH)]
    wout_cnt = max(load_w(c, wout[di], c.dram["l0a_wout"][di * P:(di + 1) * P, :])
                   for di in range(NCH))
    pr.fence("gpsimd", "wd")
    ab.push()
    kpair = ab.take(S, "kpair")
    vfm = ab.take(S, "vfm")
    v65 = ab.take(NT * 130, "v65")
    v65t = [v65[:, kt * 130:kt * 130 + 130] for kt in range(NT)]
    v65h = v65.rearrange("p (t h e) -> p t h e", h=2, e=65)
    qpair = ab.take(QG, "qpair")
    exps = [ab.take(1024, "exps0"), ab.take(1024, "exps1")]
    wpair = [ab.take(384, f"wp{di}") for di in range(NCH)]
    sqs = [ab.take(QG, "sqs0"), ab.take(QG, "sqs1")]
    stage64 = ab.take(QG, "stage64")
    rr16 = ab.take(512, "rr16")[0:1, :]
    bcq = ab.take(QG, "bcq")
    rr = af.take(QG, "rr")[0:1, :]
    bias2 = af.take(4, "bias2")
    k2col = af.take(4, "k2col")[0:2, 0:1]
    q2col = af.take(4, "q2col")[0:2, 0:1]
    odd_dma = 0
    v65_read = 0
    av_done = [0, 0]
    exp_done = [0, 0]
    kv_evac = 0

    wpair_read = 0
    for pair in range(NH // 2):
        h0 = 2 * pair
        wcnts = []
        for di in range(NCH):
            src = WQKV[di * P:(di + 1) * P, :]
            for slot, base in ((0, h0 * HD), (1, D + h0 * HD), (2, 2 * D + h0 * HD)):
                wcnts.append(load_w(c, wpair[di][:, slot * 128:slot * 128 + 128],
                                    src[:, base:base + 128],
                                    waits=[("tt", wpair_read)] if wpair_read else []))
        pr.fence("gpsimd", "wd")
        wready = ("wd", max(wcnts))
        kev = []
        for g in range(NG):
            gsl = slice(g * QG, (g + 1) * QG)
            reg = [c.ps_a, c.ps_b][g % 2]
            ps = reg[:, 0:QG]
            w0 = [wready] + xln_ready + [("ss", exp_done[g % 2])] + \
                ([("vv", kev[g - 2])] if g >= 2 else [])
            cnt = None
            for di in range(NCH):
                cnt = mmv(c, ps, wpair[di][:, 128:256], xln[di][:, gsl],
                          start=(di == 0), stop=(di == NCH - 1),
                          waits=w0 if di == 0 else [],
                          inc="tt" if di == NCH - 1 else None)
            kev.append(vin(c, "tensor_scalar", kpair[:, gsl], ps,
                           c.cst["l0a_bqkv_c"][:, 8 + pair:9 + pair], None,
                           ALU.add, waits=[("tt", cnt), ("ss", c.kpair_ss)]))
        nrm_mm = 0
        for g in range(NG):
            gsl = slice(g * QG, (g + 1) * QG)
            scnt = sin(c, "activation", sqs[g % 2][:], kpair[:, gsl], AF.Square,
                       waits=[("vv", kev[g])] + ([("tt", nrm_mm)] if g >= 2 else []))
            c.kpair_ss = scnt
            nrm_mm = mmv(c, c.ps_e[0:2, 0:QG], c.cst["headmask16"][:, 0:2],
                         sqs[g % 2][:], start=(g == 0), stop=(g == NG - 1),
                         waits=[("ss", scnt)] + (bw(c, "e") if g == 0 else []),
                         inc="tt")
            if g == 0:
                br(c, "e", "tt", nrm_mm)
        kn = vin(c, "reduce_max", k2col, c.ps_e[0:2, 0:QG], axis=AX.X,
                 waits=[("tt", nrm_mm)])
        br(c, "e", "vv", kn)
        vev = []
        for g in range(NG):
            gsl = slice(g * QG, (g + 1) * QG)
            reg = [c.ps_a, c.ps_b][g % 2]
            ps = reg[:, QG:2 * QG]
            w0 = [wready] + xln_ready + [("ss", exp_done[g % 2])] + \
                ([("vv", vev[g - 2])] if g >= 2 else [])
            cnt = None
            for di in range(NCH):
                cnt = mmv(c, ps, wpair[di][:, 256:384], xln[di][:, gsl],
                          start=(di == 0), stop=(di == NCH - 1),
                          waits=w0 if di == 0 else [],
                          inc="tt" if di == NCH - 1 else None)
            vev.append(vin(c, "tensor_scalar", vfm[:, gsl], ps,
                           c.cst["l0a_bqkv_c"][:, 16 + pair:17 + pair], None,
                           ALU.add, waits=[("tt", cnt)]))
        vone = vin(c, "memset", v65h[:, :, :, 64:65], 1.0,
                   waits=[("tt", v65_read)])
        vtr = []
        for kt in range(NT):
            tcnt = ins(c, "tensor", "transpose", c.ps_d[:, 0:P],
                       vfm[:, kt * P:(kt + 1) * P], c.cst["identity"][:],
                       waits=[("vv", vev[kt // 4]), ("tt", v65_read)] +
                             bw(c, "d") +
                             ([("vv", vtr[kt - 1])] if kt >= 1 else []),
                       inc="tt")
            vtr.append(vin(c, "tensor_copy", v65h[:, kt, :, 0:64],
                           c.ps_d[:, 0:P].rearrange("p (h e) -> p h e", e=64),
                           waits=[("tt", tcnt)]))
        br(c, "d", "vv", max(vtr))
        vready = [("vv", max(vtr + [vone]))]

        if pair == 0:
            dbg(c, xln[0][:, 0:512], xln_ready, 0, 512, cast=True)
            dbg(c, kpair[:, 0:512], [("vv", kev[0])], 512, 512, cast=True)
            dbg(c, vfm[:, 0:512], [("vv", vev[0])], 1024, 512, cast=True)
        for g in range(NG):
            gsl = slice(g * QG, (g + 1) * QG)
            psq = c.ps_c[:, 0:QG]
            w0 = [wready] + xln_ready + bw(c, "c")
            cnt = None
            for di in range(NCH):
                cnt = mmv(c, psq, wpair[di][:, 0:128], xln[di][:, gsl],
                          start=(di == 0), stop=(di == NCH - 1),
                          waits=w0 if di == 0 else [],
                          inc="tt" if di == NCH - 1 else None)
            qev = vin(c, "tensor_scalar", qpair[:], psq,
                      c.cst["l0a_bqkv_c"][:, pair:pair + 1], None, ALU.add,
                      waits=[("tt", cnt), ("ss", c.qpair_ss)])
            br(c, "c", "vv", qev)
            scnt = sin(c, "activation", sqs[0][:], qpair[:], AF.Square,
                       waits=[("vv", qev), ("tt", nrm_mm)])
            c.qpair_ss = scnt
            qmm = mmv(c, c.ps_e[0:2, 0:QG], c.cst["headmask16"][:, 0:2],
                      sqs[0][:], True, True,
                      waits=[("ss", scnt)] + bw(c, "e"), inc="tt")
            qn = vin(c, "reduce_max", q2col, c.ps_e[0:2, 0:QG], axis=AX.X,
                     waits=[("tt", qmm)])
            br(c, "e", "vv", qn)
            bias_cnt = neg_bias(c, q2col, k2col, 0.125, bias2[:, 0:2], 2,
                                waits=[("vv", qn), ("vv", kn),
                                       ("ss", c.bias_read)])
            if pair == 0 and g == 0:
                dbg(c, qpair[:, 0:512], [("vv", qev)], 1536, 512, cast=True)
                dbg(c, bias2[:, 0:2], [("vv", bias_cnt)], 2048, 2)
                dbg(c, q2col, [("vv", qn)], 2052, 1)
                dbg(c, k2col, [("vv", kn)], 2056, 1)
            for hh in range(2):
                h = h0 + hh
                hp = slice(hh * HD, (hh + 1) * HD)
                avps = [c.ps_c, c.ps_e][hh][0:65, 0:QG]
                avkey = "c" if hh == 0 else "e"
                for ri in range(8):
                    k0, k1 = 2 * ri, 2 * ri + 2
                    sreg = [c.ps_a, c.ps_b][ri % 2]
                    qkc = None
                    for kt in range(k0, k1):
                        j = kt - k0
                        qkc = mmv(c, sreg[:, j * QG:(j + 1) * QG],
                                  kpair[hp, kt * P:(kt + 1) * P],
                                  qpair[hp, :], True, True,
                                  waits=([("vv", qev),
                                          ("ss", exp_done[ri % 2])] +
                                         (bw(c, avkey) if ri == 0 else []) +
                                         (vready if ri == 0 else []))
                                  if kt == k0 else [],
                                  inc="tt" if kt == k1 - 1 else None)
                    ecnt = sin(c, "activation", exps[ri % 2][:],
                               sreg[:, 0:1024], AF.Exp,
                               bias=bias2[:, hh:hh + 1], scale=0.125,
                               waits=[("tt", qkc), ("vv", bias_cnt)])
                    c.bias_read = ecnt
                    exp_done[ri % 2] = ecnt
                    if pair == 0 and g == 0 and hh == 0 and ri == 0:
                        dbg(c, exps[0][:, 0:512], [("ss", ecnt)], 2560, 512,
                            cast=True)
                    for kt in range(k0, k1):
                        j = kt - k0
                        mmv(c, avps, v65t[kt][:, 65 * hh:65 * hh + 65],
                            exps[ri % 2][:, j * QG:(j + 1) * QG],
                            start=(kt == 0), stop=(kt == NT - 1),
                            waits=[("ss", ecnt)] +
                                  (bw(c, avkey) if kt == 0 else []),
                            inc="tt" if kt == NT - 1 else None)
                avc = pr.now("tt")
                v65_read = avc
                v1 = vin(c, "tensor_copy", rr, avps[64:65, :],
                         waits=[("tt", avc)])
                if pair == 0 and g == 0 and hh == 0:
                    dbg(c, rr[0:1, :], [("vv", v1)], 3072, 512)
                vin(c, "reciprocal", rr, rr)
                v3 = vin(c, "tensor_copy", rr16, rr)
                bcr = mmv(c, c.ps_f[0:64, 0:QG], c.cst["ones_row"][0:1, 0:64],
                          rr16, True, True,
                          waits=[("vv", v3)] + bw(c, "f"), inc="tt")
                v4 = vin(c, "tensor_copy", bcq[0:64, :], c.ps_f[0:64, 0:QG],
                         waits=[("tt", bcr)])
                br(c, "f", "vv", v4)
                if hh == 0:
                    av_done[0] = vin(c, "tensor_tensor", ofm[pair][0:64, gsl],
                                     avps[0:64, :], bcq[0:64, :], ALU.mult,
                                     waits=[("vv", v4), ("bd", odd_dma)])
                    br(c, "c", "vv", av_done[0])
                else:
                    st = vin(c, "tensor_tensor", stage64[0:64, :],
                             avps[0:64, :], bcq[0:64, :], ALU.mult,
                             waits=[("vv", v4), ("bd", odd_dma)])
                    av_done[1] = st
                    br(c, "e", "vv", st)
                    odd_dma = dma(c, "sync", ofm[pair][64:128, gsl],
                                  stage64[0:64, :], waits=[("vv", st)],
                                  inc="bd")
                    pr.fence("sync", "bd")
        wpair_read = pr.now("tt")
    c.bank["c"] = ("vv", av_done[0])
    c.bank["e"] = ("vv", av_done[1])
    ab.pop()
    yb = [ab.take(QG, f"yb{i}") for i in range(NCH)]

    first = [("wd", wout_cnt), ("bd", odd_dma),
             ("ss", exp_done[0]), ("ss", exp_done[1]),
             ("vv", max(av_done))]
    for g in range(NG):
        gsl = slice(g * QG, (g + 1) * QG)
        ev = []
        for oc in range(NCH):
            reg = [c.ps_a, c.ps_b][oc % 2]
            ps = reg[:, (oc // 2 % 2) * QG:(oc // 2 % 2) * QG + QG]
            cnt = None
            for di in range(NCH):
                cnt = mmv(c, ps, wout[di][:, oc * P:(oc + 1) * P],
                          ofm[di][:, gsl],
                          start=(di == 0), stop=(di == NCH - 1),
                          waits=(first + bw(c, "a", "b") +
                                 ([("vv", ev[oc - 4])] if oc >= 4 else []))
                          if di == 0 else [],
                          inc="tt" if di == NCH - 1 else None)
            ev.append(vin(c, "tensor_scalar", yb[oc][:], ps,
                          c.cst["l0a_bout_c"][:, oc:oc + 1], None, ALU.add,
                          waits=[("tt", cnt)]))
        if g == 0:
            dbg(c, ofm[0][:, 0:512], [("vv", max(ev))], 3584, 512, cast=True)
            dbg(c, yb[0][:], [("vv", max(ev))], 4096, 512, cast=True)
        first = [("vv", max(ev))]
        br(c, "a", "vv", max(ev))
        br(c, "b", "vv", max(ev))
        post_ln(c, yb, g, [("vv", max(ev))])
    ab.pop(); af.pop()



def emit_l0f(c, xi, xi_ready):
    pr, ab, af = c.pr, c.ab, c.af

    def xsh(di, g):
        off = 1 if di < 4 else 0
        return xi[di][:, off + g * QG: off + g * QG + QG]

    def xdat(di, g):
        return xi[di][:, 1 + g * QG: 1 + g * QG + QG]

    xw = list(xi_ready)
    c.l0f_wread = getattr(c, "l0f_wread", 0)
    c.l0f_bd = getattr(c, "l0f_bd", pr.now("bd"))
    for l in range(2):
        lw_ = f"l0f{l}_"
        ab.push(); af.push()
        ofm = [ab.take(S, f"f_ofm{i}") for i in range(4)]
        wo_res = [ab.take(D, f"f_wo{i}") for i in range(4)]
        v65 = ab.take(NT * 130, "f_v65")
        v65t = [v65[:, kt * 130:kt * 130 + 130] for kt in range(NT)]
        v65h = v65.rearrange("p (t h e) -> p t h e", h=2, e=65)
        wpair = [ab.take(384, f"f_wp{i}") for i in range(NCH)]
        qfm = ab.take(S, "f_qfm")
        kfm = ab.take(S, "f_kfm")
        qpfm = ab.take(S, "f_qpfm")
        kpfm = ab.take(S, "f_kpfm")
        vfm = ab.take(S, "f_vfm")
        tabc = ab.take(S, "f_tabc")
        tabs = ab.take(S, "f_tabs")
        rnb = ab.take(S, "f_rnb")[0:2, :]
        exps = ab.take(640, "f_exps")
        stage64 = ab.take(QG, "f_stage64")
        rr16 = ab.take(512, "f_rr16")[0:1, :]
        bcq = ab.take(QG, "f_bcq")
        sqs = [ab.take(QG, "f_sqs0"), ab.take(QG, "f_sqs1")]
        rn2 = af.take(2048, "f_rn2")[0:2, :]
        tmp1 = af.take(QG, "f_tmp1")
        tmp2 = af.take(QG, "f_tmp2")
        rsbc = af.take(QG, "f_rsbc")
        negb = af.take(4, "f_negb")[:, 0:1]
        rr = af.take(QG, "f_rr")[0:1, :]
        bq = af.take(4, "f_bq")[0:1, 0:1]
        bk = af.take(4, "f_bk")[0:1, 0:1]

        wo_cnt = max(load_w(c, wo_res[i], c.dram[lw_ + "wo"][i * P:(i + 1) * P, :],
                            waits=[("tt", c.l0f_wread)] if c.l0f_wread else [])
                     for i in range(4))
        pr.fence("gpsimd", "wd")
        vin(c, "reduce_max", bq, c.cst[lw_ + "qscale"][:], axis=AX.X,
            apply_absolute_value=True, waits=c.ready)
        vin(c, "reduce_max", bk, c.cst[lw_ + "kscale"][:], axis=AX.X,
            apply_absolute_value=True)
        v0 = vin(c, "tensor_tensor", bq, bq, bk, ALU.mult)
        t2b = mmv(c, c.ps_f[:, 0:1], c.cst["ones_row_f"][0:1, :], bq, True, True,
                  waits=[("vv", v0)] + bw(c, "f"), inc="tt")
        nbc = vin(c, "tensor_scalar", negb, c.ps_f[:, 0:1], -8.0 * 1.02, None,
                  ALU.mult, waits=[("tt", t2b)])
        br(c, "f", "vv", nbc)

        av_done = [0, 0]
        exp_done = 0
        exps_read = 0
        odd_dma = c.l0f_bd
        tab_read = 0
        v65_read = 0
        for pp in range(4):
            wcnts = []
            for di in range(NCH):
                src = c.dram[lw_ + "wqkv"][di * P:(di + 1) * P, :]
                for slot, base in ((0, pp * P), (1, 512 + pp * P),
                                   (2, 1024 + pp * P)):
                    wcnts.append(load_w(
                        c, wpair[di][:, slot * P:(slot + 1) * P],
                        src[:, base:base + P],
                        waits=[("tt", c.l0f_wread)] if c.l0f_wread else []))
            pr.fence("gpsimd", "wd")
            wready = ("wd", max(wcnts))
            evs = {}
            sq_read = 0
            for ti, (tgt, slot) in enumerate(((qfm, 0), (kfm, 1), (vfm, 2))):
                for g in range(NG):
                    gsl = slice(g * QG, (g + 1) * QG)
                    reg = [c.ps_a, c.ps_b][(ti * NG + g) % 2]
                    ps = reg[:, 0:QG]
                    w0 = [wready, ("ss", exp_done), ("ss", sq_read)] + xw +                         ([("vv", evs[ti * NG + g - 2])]
                         if ti * NG + g - 2 in evs else [])
                    cnt = None
                    for di in range(NCH):
                        cnt = mmv(c, ps, wpair[di][:, slot * P:(slot + 1) * P],
                                  xsh(di, g),
                                  start=(di == 0), stop=(di == NCH - 1),
                                  waits=w0 if di == 0 else [],
                                  inc="tt" if di == NCH - 1 else None)
                    evs[ti * NG + g] = vin(c, "tensor_copy", tgt[:, gsl], ps,
                                           waits=[("tt", cnt)])
            qkv_ev = pr.now("vv")
            vone = vin(c, "memset", v65h[:, :, :, 64:65], 1.0,
                       waits=[("tt", v65_read), ("vv", qkv_ev)])
            vtr = []
            for kt in range(NT):
                tcnt = ins(c, "tensor", "transpose", c.ps_d[:, 0:P],
                           vfm[:, kt * P:(kt + 1) * P], c.cst["identity"][:],
                           waits=[("vv", qkv_ev), ("tt", v65_read)] + bw(c, "d") +
                                 ([("vv", vtr[kt - 1])] if kt >= 1 else []),
                           inc="tt")
                vtr.append(vin(c, "tensor_copy", v65h[:, kt, :, 0:64],
                               c.ps_d[:, 0:P].rearrange("p (h e) -> p h e", e=64),
                               waits=[("tt", tcnt)]))
            br(c, "d", "vv", max(vtr))
            vready = ("vv", max(vtr + [vone]))
            rnb_read = 0
            for ti, src_ in enumerate((qfm, kfm)):
                nprev = 0
                for g in range(NG):
                    gsl = slice(g * QG, (g + 1) * QG)
                    scnt = sin(c, "activation", sqs[g % 2][:], src_[:, gsl],
                               AF.Square, waits=[("vv", qkv_ev)])
                    sq_read = scnt
                    nmm = mmv(c, c.ps_e[0:2, 0:QG], c.cst["headmask16"][:, 0:2],
                              sqs[g % 2][:], True, True,
                              waits=[("ss", scnt)] + bw(c, "e"), inc="tt")
                    br(c, "e", "tt", nmm)
                    lncnt = sin(c, "activation", rn2[:, gsl], c.ps_e[0:2, 0:QG],
                                AF.Ln, waits=[("tt", nmm)])
                    ecnt = sin(c, "activation", rn2[:, gsl], rn2[:, gsl],
                               AF.Exp, scale=-0.5)
                    br(c, "e", "ss", ecnt)
                    nprev = vin(c, "tensor_copy", rnb[:, gsl], rn2[:, gsl],
                                waits=[("ss", ecnt), ("tt", rnb_read)])
                tgt = (qpfm, kpfm)[ti]
                tc_ = load_w(c, tabc, c.dram[("fm_qc", "fm_kc")[ti]],
                             waits=[("vv", tab_read)] if tab_read else [])
                ts_ = load_w(c, tabs, c.dram[("fm_qs", "fm_ks")[ti]])
                pr.fence("gpsimd", "wd")
                scol = c.cst[lw_ + ("qscale_c", "kscale_c")[ti]]
                for g in range(NG):
                    gsl = slice(g * QG, (g + 1) * QG)
                    bmm = mmv(c, c.ps_f[:, 0:QG], c.cst["pairsel2"][:],
                              rnb[:, gsl], True, True,
                              waits=[("vv", nprev)] + bw(c, "f"), inc="tt")
                    rnb_read = bmm
                    rsb = vin(c, "tensor_scalar", rsbc, c.ps_f[:, 0:QG],
                              scol[:, 0:1], None, ALU.mult, waits=[("tt", bmm)])
                    br(c, "f", "vv", rsb)
                    rmm = mmv(c, c.ps_e[:, 0:QG], c.cst["rotperm"][:],
                              src_[:, gsl], True, True, waits=bw(c, "e"),
                              inc="tt")
                    u1 = vin(c, "tensor_tensor", tmp1, src_[:, gsl],
                             tabc[:, gsl], ALU.mult, waits=[("wd", ts_)])
                    u2 = vin(c, "tensor_tensor", tmp2, c.ps_e[:, 0:QG],
                             tabs[:, gsl], ALU.mult, waits=[("tt", rmm)])
                    br(c, "e", "vv", u2)
                    vin(c, "tensor_tensor", tmp1, tmp1, tmp2, ALU.add)
                    tab_read = vin(c, "tensor_tensor", tgt[:, gsl], tmp1, rsbc,
                                   ALU.mult)
            rot_done = pr.now("vv")
            for hh in range(2):
                hp = slice(hh * HD, (hh + 1) * HD)
                avp = [c.ps_c, c.ps_e][hh][0:65, 0:QG]
                avkey = ("c", "e")[hh]
                for w in range(4):
                    first_av = True
                    for qt in range(4 * w, 4 * w + 4):
                        kts = [k for k in range(qt - 4, qt + 1) if k >= 0]
                        reg = [c.ps_a, c.ps_b][qt % 2]
                        qkc = None
                        nk = len(kts)
                        nb0 = min(nk, 4)
                        for ki, kt in enumerate(kts):
                            st_ = (ki == 0 or ki == 4)
                            sp_ = (ki == nb0 - 1 or ki == nk - 1)
                            qkc = mmv(c, reg[:, ki * P:(ki + 1) * P],
                                      kpfm[hp, kt * P:(kt + 1) * P],
                                      qpfm[hp, qt * P:(qt + 1) * P],
                                      start=st_, stop=sp_,
                                      waits=([("vv", rot_done),
                                              ("ss", exp_done)] +
                                             (bw(c, avkey) if first_av else []))
                                      if ki == 0 else [],
                                      inc="tt" if ki == len(kts) - 1 else None)
                        mw = [("tt", qkc)]
                        for ki, kt in enumerate(kts):
                            msk = None
                            if kt == qt - 4:
                                msk = "mask_m4"
                            elif kt == qt:
                                msk = "mask_m0"
                            if msk:
                                mcnt = vin(c, "tensor_tensor",
                                           reg[:, ki * P:(ki + 1) * P],
                                           reg[:, ki * P:(ki + 1) * P],
                                           c.cst[msk][:], ALU.add, waits=mw)
                                mw = [("vv", mcnt)]
                        ecnt = sin(c, "activation", exps[:, 0:nk * P],
                                   reg[:, 0:nk * P], AF.Exp,
                                   bias=negb, scale=1.0,
                                   waits=mw + [("vv", nbc), ("tt", exps_read)])
                        exp_done = ecnt
                        for ki, kt in enumerate(kts):
                            avc = mmv(c, avp[:, (qt % 4) * P:(qt % 4 + 1) * P],
                                      v65t[kt][:, 65 * hh:65 * hh + 65],
                                      exps[:, ki * P:(ki + 1) * P],
                                      start=first_av,
                                      stop=(qt % 4 == 3 and ki == len(kts) - 1),
                                      waits=[("ss", ecnt), vready] +
                                            (bw(c, avkey) if first_av else []),
                                      inc="tt" if ki == len(kts) - 1 else None)
                            first_av = False
                        exps_read = avc
                    v1 = vin(c, "tensor_copy", rr, avp[64:65, :],
                             waits=[("tt", avc)])
                    vin(c, "reciprocal", rr, rr)
                    v3 = vin(c, "tensor_copy", rr16, rr)
                    bcr = mmv(c, c.ps_f[0:64, 0:QG], c.cst["ones_row"][0:1, 0:64],
                              rr16, True, True, waits=[("vv", v3)] + bw(c, "f"),
                              inc="tt")
                    v4 = vin(c, "tensor_copy", bcq[0:64, :], c.ps_f[0:64, 0:QG],
                             waits=[("tt", bcr)])
                    br(c, "f", "vv", v4)
                    wsl = slice(w * QG, (w + 1) * QG)
                    if hh == 0:
                        av_done[0] = vin(c, "tensor_tensor", ofm[pp][0:64, wsl],
                                         avp[0:64, :], bcq[0:64, :], ALU.mult,
                                         waits=[("vv", v4), ("bd", odd_dma)])
                        br(c, "c", "vv", av_done[0])
                    else:
                        st = vin(c, "tensor_tensor", stage64[0:64, :],
                                 avp[0:64, :], bcq[0:64, :], ALU.mult,
                                 waits=[("vv", v4), ("bd", odd_dma)])
                        av_done[1] = st
                        br(c, "e", "vv", st)
                        odd_dma = dma(c, "sync", ofm[pp][64:128, wsl],
                                      stage64[0:64, :], waits=[("vv", st)],
                                      inc="bd")
                        pr.fence("sync", "bd")
            v65_read = pr.now("tt")
        c.l0f_bd = odd_dma
        first = [("wd", wo_cnt), ("bd", odd_dma), ("ss", exp_done),
                 ("vv", max(av_done))]
        xev = []
        for g in range(NG):
            gsl = slice(g * QG, (g + 1) * QG)
            ev = []
            for oc in range(NCH):
                reg = [c.ps_a, c.ps_b][oc % 2]
                ps = reg[:, (oc // 2 % 2) * QG:(oc // 2 % 2) * QG + QG]
                cnt = None
                for di in range(4):
                    cnt = mmv(c, ps, wo_res[di][:, oc * P:(oc + 1) * P],
                              ofm[di][:, gsl],
                              start=(di == 0), stop=(di == 3),
                              waits=(first + ([("vv", ev[oc - 4])] if oc >= 4
                                              else [])) if di == 0 else [],
                              inc="tt" if di == 3 else None)
                ev.append(vin(c, "tensor_tensor", xdat(oc, g), ps, xdat(oc, g),
                              ALU.add, waits=[("tt", cnt)]))
            first = [("vv", max(ev))]
            br(c, "a", "vv", max(ev))
            br(c, "b", "vv", max(ev))
            xev.append(max(ev))
        xw = [("vv", max(xev))]
        c.l0f_wread = pr.now("tt")
        ab.pop(); af.pop()
        pr.barrier()

        ab.push(); af.push()
        xiln = [ab.take(S, f"f_xiln{i}") for i in range(NCH)]
        hbuf = [ab.take(QG, f"f_hb{j}") for j in range(2)]
        yacc = [ab.take(S, f"f_ya{i}") for i in range(NCH)]
        yaccv = [y.rearrange("p (g t) -> p g t", t=QG) for y in yacc]
        rows = {k: af.take(512, k)[0:1, :] for k in ("mu", "rstd", "tmp")}
        mubc16 = ab.take(QG, "f_mubc16")
        ha = mubc16
        mur16 = ab.take(512, "f_mur16")[0:1, :]
        rstdbc = af.take(QG, "f_rstdbc")
        tmp1 = af.take(QG, "f_tmp1b")
        xil_ev = []
        for g in range(NG):
            gsl = slice(g * QG, (g + 1) * QG)
            lcnt = ln_rows(c, lambda ch, g=g: xsh(ch, g), rows, xw)
            m16 = vin(c, "tensor_copy", mur16, rows["mu"], waits=[("vv", lcnt)])
            b1 = bc_row16(c, mur16, QG, mubc16, waits=[("vv", m16)])
            b2 = bc_rowf(c, rows["rstd"], QG, rstdbc, waits=[("vv", lcnt)])
            for ch in range(NCH):
                vin(c, "tensor_tensor", tmp1, xsh(ch, g), mubc16, ALU.subtract,
                    waits=[("vv", b2), ("vv", b1)])
                vin(c, "tensor_tensor", tmp1, tmp1, rstdbc, ALU.mult)
                xil_ev.append(vin(
                    c, "tensor_scalar", xiln[ch][:, gsl], tmp1,
                    c.cst[lw_ + "lng_c"][:, ch:ch + 1],
                    c.cst[lw_ + "lnb_c"][:, ch:ch + 1], ALU.mult, ALU.add))
        xil_ready = ("vv", max(xil_ev))
        JG = [(j0, min(2, 22 - j0)) for j0 in range(0, 22, 2)]
        w1a_s = [ab.take(256, f"f_w1a{i}") for i in range(NCH)]
        w1g_s = [ab.take(256, f"f_w1g{i}") for i in range(NCH)]
        w2_s = [ab.take(D, f"f_w2{i}") for i in range(2)]
        wread = 0
        ha_ev = 0
        hg_ss = 0
        yev = {}
        for gi, (j0, jn) in enumerate(JG):
            wcnts = []
            for di in range(NCH):
                wcnts.append(load_w(c, w1a_s[di][:, 0:jn * P],
                                    c.dram[lw_ + "w1a"][di * P:(di + 1) * P,
                                                        j0 * P:(j0 + jn) * P],
                                    waits=[("tt", wread)] if wread else []))
                wcnts.append(load_w(c, w1g_s[di][:, 0:jn * P],
                                    c.dram[lw_ + "w1g"][di * P:(di + 1) * P,
                                                        j0 * P:(j0 + jn) * P]))
            for j in range(jn):
                wcnts.append(load_w(c, w2_s[j],
                                    c.dram[lw_ + "w2"][(j0 + j) * P:(j0 + j + 1) * P, :]))
            pr.fence("gpsimd", "wd")
            wready = ("wd", max(wcnts))
            for g in range(NG):
                gsl = slice(g * QG, (g + 1) * QG)
                hev = []
                for j in range(jn):
                    jj = j0 + j
                    aps = c.ps_a[:, 0:QG]
                    gps = c.ps_a[:, QG:2 * QG]
                    w0 = [wready, xil_ready, ("vv", ha_ev), ("ss", hg_ss)] +                         bw(c, "a")
                    acnt = gcnt = None
                    for di in range(NCH):
                        acnt = mmv(c, aps, w1a_s[di][:, j * P:(j + 1) * P],
                                   xiln[di][:, gsl], start=(di == 0),
                                   stop=(di == NCH - 1),
                                   waits=w0 if di == 0 else [],
                                   inc="tt" if di == NCH - 1 else None)
                    for di in range(NCH):
                        gcnt = mmv(c, gps, w1g_s[di][:, j * P:(j + 1) * P],
                                   xiln[di][:, gsl], start=(di == 0),
                                   stop=(di == NCH - 1),
                                   waits=[] if di else [("ss", hg_ss)],
                                   inc="tt" if di == NCH - 1 else None)
                    ha_ev = vin(c, "tensor_scalar", ha, aps,
                                c.cst[lw_ + "b1a_c"][:, jj:jj + 1], None,
                                ALU.add, waits=[("tt", acnt)])
                    hg_ss = sin(c, "activation", hbuf[j][:], gps, AF.Gelu,
                                bias=c.cst[lw_ + "b1g_c"][:, jj:jj + 1],
                                waits=[("tt", gcnt)])
                    hev.append(vin(c, "tensor_tensor", hbuf[j][:], ha,
                                   hbuf[j][:], ALU.mult,
                                   waits=[("vv", ha_ev), ("ss", hg_ss)]))
                br(c, "a", "vv", max(hev))
                for oc in range(NCH):
                    ps = c.ps_b[:, (oc % 2) * QG:(oc % 2) * QG + QG]
                    cnt = None
                    slot_prev = c.w2slot.get(oc % 2, 0)
                    for j in range(jn):
                        cnt = mmv(c, ps, w2_s[j][:, oc * P:(oc + 1) * P],
                                  hbuf[j][:], start=(j == 0), stop=(j == jn - 1),
                                  waits=([("vv", max(hev))] +
                                         ([("vv", yev[(g, oc)])]
                                          if (g, oc) in yev else []) +
                                         ([("vv", slot_prev)] if slot_prev else []) +
                                         (bw(c, "b") if gi == 0 and oc < 2 else []))
                                  if j == 0 else [],
                                  inc="tt" if j == jn - 1 else None)
                    if gi == 0:
                        yev[(g, oc)] = vin(c, "tensor_copy", yaccv[oc][:, g, :],
                                           ps, waits=[("tt", cnt)])
                    else:
                        yev[(g, oc)] = vin(c, "tensor_tensor", yaccv[oc][:, g, :],
                                           ps, yaccv[oc][:, g, :], ALU.add,
                                           waits=[("tt", cnt)])
                    c.w2slot[oc % 2] = yev[(g, oc)]
            wread = pr.now("tt")
        xev = []
        for g in range(NG):
            for oc in range(NCH):
                xev.append(vin(
                    c, "scalar_tensor_tensor", xdat(oc, g), yaccv[oc][:, g, :],
                    c.cst[lw_ + "b2_c"][:, oc:oc + 1], xdat(oc, g),
                    ALU.add, ALU.add, waits=[("vv", yev[(g, oc)])]))
        xw = [("vv", max(xev))]
        c.l0f_wread = pr.now("tt")
        ab.pop(); af.pop()
        pr.barrier()
    return xw


def emit_l1f(c, xln, xln_ready):
    pr, ab, af = c.pr, c.ab, c.af
    ab.push(); af.push()
    WQKV = c.dram["l1f_wqkv"]
    ofm = [ab.take(S, f"n_ofm{i}") for i in range(NCH)]
    rowA = ab.take(S, "n_rowA")[0:1, :]
    rowB = ab.take(S, "n_rowB")[0:1, :]
    ab.push()
    kpair = ab.take(S, "n_kpair")
    vfm = ab.take(S, "n_vfm")
    v65p = ab.take(NT * 130, "n_v65p")
    v65n = ab.take(NT * 130, "n_v65n")
    v65pt = [v65p[:, kt * 130:kt * 130 + 130] for kt in range(NT)]
    v65nt = [v65n[:, kt * 130:kt * 130 + 130] for kt in range(NT)]
    v65ph = v65p.rearrange("p (t h e) -> p t h e", h=2, e=65)
    v65nh = v65n.rearrange("p (t h e) -> p t h e", h=2, e=65)
    qpair = ab.take(QG, "n_qpair")
    qfull = ab.take(S, "n_qfull")
    exps = [ab.take(1024, "n_exps0"), ab.take(1024, "n_exps1")]
    wpair = [ab.take(384, f"n_wp{i}") for i in range(NCH)]
    sqs = [ab.take(QG, "n_sqs0"), ab.take(QG, "n_sqs1")]
    stage64 = ab.take(QG, "n_st64")
    stageB = ab.take(QG, "n_stB")
    rr16 = ab.take(512, "n_rr16")[0:2, :]
    rr16b = ab.take(512, "n_rr16b")[0:1, :]
    bcq = ab.take(QG, "n_bcq")
    bcn = ab.take(QG, "n_bcn")
    qgbig = ab.take(16, "n_qgbig")
    gh16 = ab.take(64, "n_gh16")
    ghT = ab.take(16, "n_ghT")
    mposc = af.take(16, "n_mposc")
    mnegc = af.take(16, "n_mnegc")
    rr = af.take(QG, "n_rr")[0:2, :]
    rrb = af.take(QG, "n_rrb")[0:1, :]
    negb = af.take(4, "n_negb")[:, 0:1]
    gq2 = af.take(4, "n_gq2")[0:1, 0:1]
    k2p = af.take(4, "n_k2p")[0:1, 0:1]
    t11 = af.take(4, "n_t11")[0:1, 0:1]
    qgn = af.take(64, "n_qgn")
    tau = af.take(4, "n_tau")[0:1, 0:1]
    ccst = af.take(4, "n_cc")[0:1, 0:2]
    af.push()
    dot2 = af.take(QG, "n_dot2")[0:2, :]
    simrow = af.take(2048, "n_sim")[0:1, :]
    simcol = af.take(16, "n_simcol")

    qg = c.cst["l1f_qg"]
    v0 = vin(c, "tensor_tensor", qgn[0:16, 0:HD], qg[:], qg[:], ALU.mult,
             waits=c.ready)
    v1 = vin(c, "reduce_sum", qgn[0:16, 0:1], qgn[0:16, 0:HD], axis=AX.X)
    s0 = sin(c, "activation", qgn[0:16, 0:1], qgn[0:16, 0:1], AF.Ln,
             waits=[("vv", v1)])
    s1 = sin(c, "activation", qgn[0:16, 0:1], qgn[0:16, 0:1], AF.Exp,
             scale=-0.5)
    v3 = vin(c, "tensor_scalar", gh16[0:16, 0:HD], qg[:], qgn[0:16, 0:1],
             None, ALU.mult, waits=[("ss", s1)])
    tq = ins(c, "tensor", "transpose", c.ps_d[0:64, 0:16], gh16[0:16, 0:HD],
             c.cst["identity"][0:16, 0:16], waits=[("vv", v3)] + bw(c, "d"),
             inc="tt")
    v4 = vin(c, "tensor_copy", ghT[0:64, 0:16], c.ps_d[0:64, 0:16],
             waits=[("tt", tq)])
    br(c, "d", "vv", v4)
    qgz = vin(c, "memset", qgbig[:], 0.0)
    ghT2 = ghT[0:64, 0:16].rearrange("p (a b) -> p a b", b=2)
    qgb2lo = qgbig[0:64, :].rearrange("p (a b) -> p a b", b=2)
    qgb2hi = qgbig[64:128, :].rearrange("p (a b) -> p a b", b=2)
    ev_ = vin(c, "tensor_copy", qgb2lo[:, :, 0:1], ghT2[:, :, 0:1],
              waits=[("vv", v4), ("vv", qgz)])
    od_ = dma(c, "sync", qgb2hi[:, :, 1:2], ghT2[:, :, 1:2],
              waits=[("vv", ev_)], inc="bd", allow_slow_non_contiguous=True)
    pr.fence("sync", "bd")
    qg_ready = ("bd", od_)

    wpair_read = 0
    gq_ev = 0
    for pair in range(NH // 2):
        h0 = 2 * pair
        wcnts = [load_w(c, wpair[di][:, 0:128],
                        WQKV[di * P:(di + 1) * P, h0 * HD:h0 * HD + 128],
                        waits=[("tt", wpair_read)] if wpair_read else [])
                 for di in range(NCH)]
        pr.fence("gpsimd", "wd")
        wready = ("wd", max(wcnts))
        qev = []
        for g in range(NG):
            gsl = slice(g * QG, (g + 1) * QG)
            key = "a" if g % 2 == 0 else "b"
            ps = [c.ps_a, c.ps_b][g % 2][:, 0:QG]
            w0 = [wready] + xln_ready + bw(c, key) +                 ([("vv", qev[g - 2])] if g >= 2 else [])
            cnt = None
            for di in range(NCH):
                cnt = mmv(c, ps, wpair[di][:, 0:128], xln[di][:, gsl],
                          start=(di == 0), stop=(di == NCH - 1),
                          waits=w0 if di == 0 else [],
                          inc="tt" if di == NCH - 1 else None)
            qev.append(vin(c, "tensor_copy", qfull[:, gsl], ps,
                           waits=[("tt", cnt)]))
            br(c, key, "vv", qev[-1])
        for g in range(NG):
            gsl = slice(g * QG, (g + 1) * QG)
            scnt = sin(c, "activation", sqs[g % 2][:], qfull[:, gsl],
                       AF.Square, waits=[("vv", qev[g])])
            nmm = mmv(c, c.ps_e[0:2, 0:QG], c.cst["headmask16"][:, 0:2],
                      sqs[g % 2][:], True, True,
                      waits=[("ss", scnt)] + bw(c, "e"), inc="tt")
            br(c, "e", "tt", nmm)
            bmm = mmv(c, c.ps_e[0:1, QG:2 * QG] if False else
                      c.ps_c[0:1, 0:QG], c.cst["ones_col"][:],
                      sqs[g % 2][:], True, True,
                      waits=[("ss", scnt)] + bw(c, "c"), inc="tt")
            br(c, "c", "tt", bmm)
            mx = vin(c, "reduce_max", t11, c.ps_c[0:1, 0:QG], axis=AX.X,
                     waits=[("tt", bmm), ("ss", getattr(c, "t11_ss", 0))])
            br(c, "c", "vv", mx)
            if pair == 0 and g == 0:
                gq_ev = vin(c, "tensor_copy", gq2, t11)
            else:
                gq_ev = vin(c, "tensor_tensor", gq2, gq2, t11, ALU.max)
            dmm = mmv(c, c.ps_f[0:2, 0:QG], qgbig[:, h0:h0 + 2],
                      qfull[:, gsl], True, True,
                      waits=[qg_ready, ("vv", qev[g])] + bw(c, "f"), inc="tt")
            lcn = sin(c, "activation", rr[0:2, :], c.ps_e[0:2, 0:QG], AF.Ln,
                      waits=[("tt", nmm)])
            ecn = sin(c, "activation", rr[0:2, :], rr[0:2, :], AF.Exp,
                      scale=-0.5)
            br(c, "e", "ss", ecn)
            dcn = vin(c, "tensor_tensor", dot2[0:2, :],
                      c.ps_f[0:2, 0:QG], rr[0:2, :], ALU.mult,
                      waits=[("ss", ecn), ("tt", dmm)])
            br(c, "f", "vv", dcn)
            d16 = vin(c, "tensor_copy", sqs[g % 2][0:2, :], dot2[0:2, :])
            smm = mmv(c, c.ps_c[0:1, 0:QG], c.cst["ones_col"][0:2, :],
                      sqs[g % 2][0:2, :], True, True,
                      waits=[("vv", d16)] + bw(c, "c"), inc="tt")
            if pair == 0:
                sev = vin(c, "tensor_scalar", simrow[0:1, gsl],
                          c.ps_c[0:1, 0:QG], 1.0 / 16.0, None, ALU.mult,
                          waits=[("tt", smm)])
            else:
                sev = vin(c, "scalar_tensor_tensor", simrow[0:1, gsl],
                          c.ps_c[0:1, 0:QG], 1.0 / 16.0, simrow[0:1, gsl],
                          ALU.mult, ALU.add, waits=[("tt", smm)])
            br(c, "c", "vv", sev)
        wpair_read = pr.now("tt")

    mn1 = vin(c, "tensor_reduce", ccst[0:1, 0:1], simrow[0:1, 5:S], AX.X,
              ALU.min)
    vin(c, "tensor_scalar", ccst[0:1, 0:1], ccst[0:1, 0:1], -1.0, None,
        ALU.mult)
    mx1 = vin(c, "reduce_max", ccst[0:1, 1:2], simrow[0:1, 5:S], axis=AX.X)
    cin = dma(c, "gpsimd", c.cc_in.ap(), ccst[0:1, 0:2], waits=[("vv", mx1)],
              inc="wd")
    pr.fence("gpsimd", "wd")
    ccc = pr.on("gpsimd", lambda e: e.collective_compute(
        "AllReduce", ALU.max, ins=[c.cc_in.ap()], outs=[c.cc_out.ap()],
        replica_groups=[list(range(8))]), waits=[("wd", cin)], inc="gg")
    cout = dma(c, "gpsimd", ccst[0:1, 0:2], c.cc_out.ap(),
               waits=[("gg", ccc)], inc="wd")
    pr.fence("gpsimd", "wd")
    v5 = vin(c, "tensor_scalar", tau, ccst[0:1, 1:2], 0.9, None, ALU.mult,
             waits=[("wd", cout)])
    v6 = vin(c, "scalar_tensor_tensor", tau, ccst[0:1, 0:1], 0.1, tau,
             ALU.mult, ALU.subtract)
    v7 = vin(c, "tensor_scalar", tau, tau, -1.0, None, ALU.mult)
    sd = vin(c, "tensor_scalar", simrow[0:1, :], simrow[0:1, :], tau[0:1, 0:1],
             None, ALU.subtract, waits=[("vv", v7)])
    trs = []
    for kt in range(NT):
        tcn = mmv(c, c.ps_f[:, kt:kt + 1],
                  simrow[0:1, kt * P:(kt + 1) * P],
                  c.cst["ones_row_f"][0:1, 0:1], True, True,
                  waits=[("vv", sd)] + (bw(c, "f") if kt == 0 else []),
                  inc="tt")
        trs.append(tcn)
    v8 = vin(c, "tensor_copy", simcol[:, 0:16], c.ps_f[:, 0:16],
             waits=[("tt", trs[-1])])
    br(c, "f", "vv", v8)
    p1 = vin(c, "tensor_single_scalar", mposc[:, 0:16], simcol[:, 0:16], 0.0,
             ALU.is_gt)
    p2 = vin(c, "memset", mposc[0:5, 0:1], 1.0)
    p3 = vin(c, "tensor_scalar", mnegc[:, 0:16], mposc[:, 0:16], -1.0, 1.0,
             ALU.mult, ALU.add)
    p4 = vin(c, "memset", mnegc[0:5, 0:1], 1.0)
    r1 = vin(c, "tensor_single_scalar", rowA[0:1, :], simrow[0:1, :], 0.0,
             ALU.is_gt, waits=[("vv", sd)])
    r2 = vin(c, "tensor_scalar", rowB[0:1, :], rowA[0:1, :], -1.0, 1.0,
             ALU.mult, ALU.add)
    r3 = vin(c, "memset", rowA[0:1, 0:5], 0.5)
    r4 = vin(c, "memset", rowB[0:1, 0:5], 0.5)
    mask_ready = pr.now("vv")
    af.pop()

    odd_dma = pr.now("bd")
    av_done = [0, 0]
    exp_done = [0, 0]
    v65_read = 0
    kv_ss = 0
    for pair in range(NH // 2):
        h0 = 2 * pair
        wcnts = []
        for di in range(NCH):
            src = WQKV[di * P:(di + 1) * P, :]
            for slot, base in ((0, h0 * HD), (1, D + h0 * HD),
                               (2, 2 * D + h0 * HD)):
                wcnts.append(load_w(c, wpair[di][:, slot * 128:slot * 128 + 128],
                                    src[:, base:base + 128],
                                    waits=[("tt", wpair_read)]))
        pr.fence("gpsimd", "wd")
        wready = ("wd", max(wcnts))
        kev = []
        for g in range(NG):
            gsl = slice(g * QG, (g + 1) * QG)
            key = "a" if g % 2 == 0 else "b"
            ps = [c.ps_a, c.ps_b][g % 2][:, 0:QG]
            w0 = [wready, ("ss", exp_done[g % 2]), ("ss", kv_ss)] +                 xln_ready + bw(c, key) +                 ([("vv", kev[g - 2])] if g >= 2 else [])
            cnt = None
            for di in range(NCH):
                cnt = mmv(c, ps, wpair[di][:, 128:256], xln[di][:, gsl],
                          start=(di == 0), stop=(di == NCH - 1),
                          waits=w0 if di == 0 else [],
                          inc="tt" if di == NCH - 1 else None)
            kev.append(vin(c, "tensor_copy", kpair[:, gsl], ps,
                           waits=[("tt", cnt)]))
        k2mm = 0
        for g in range(NG):
            gsl = slice(g * QG, (g + 1) * QG)
            scnt = sin(c, "activation", sqs[g % 2][:], kpair[:, gsl],
                       AF.Square, waits=[("vv", kev[g])] +
                       ([("tt", k2mm)] if k2mm else []))
            kv_ss = scnt
            k2mm = mmv(c, c.ps_c[0:1, 0:QG], c.cst["ones_col"][:],
                       sqs[g % 2][:], True, True,
                       waits=[("ss", scnt)] + bw(c, "c"), inc="tt")
            mx = vin(c, "reduce_max", t11, c.ps_c[0:1, 0:QG], axis=AX.X,
                     waits=[("tt", k2mm), ("ss", getattr(c, "t11_ss", 0))])
            br(c, "c", "vv", mx)
            if g == 0:
                kc_ev = vin(c, "tensor_copy", k2p, t11)
            else:
                kc_ev = vin(c, "tensor_tensor", k2p, k2p, t11, ALU.max)
        b0 = vin(c, "tensor_tensor", t11, gq2, k2p, ALU.mult,
                 waits=[("vv", kc_ev), ("vv", gq_ev)])
        b1 = sin(c, "activation", t11, t11, AF.Ln, waits=[("vv", b0)])
        b2 = sin(c, "activation", t11, t11, AF.Exp, scale=0.5)
        c.t11_ss = b2
        b3 = mmv(c, c.ps_f[:, 0:1], c.cst["ones_row_f"][0:1, :], t11, True,
                 True, waits=[("ss", b2)] + bw(c, "f"), inc="tt")
        nbc = vin(c, "tensor_scalar", negb, c.ps_f[:, 0:1], -0.125 * 1.01,
                  None, ALU.mult, waits=[("tt", b3)])
        br(c, "f", "vv", nbc)
        vev = []
        for g in range(NG):
            gsl = slice(g * QG, (g + 1) * QG)
            key = "a" if g % 2 == 0 else "b"
            ps = [c.ps_a, c.ps_b][g % 2][:, 0:QG]
            w0 = [wready, ("ss", exp_done[g % 2]), ("ss", kv_ss)] + bw(c, key) +                 ([("vv", vev[g - 2])] if g >= 2 else [])
            cnt = None
            for di in range(NCH):
                cnt = mmv(c, ps, wpair[di][:, 256:384], xln[di][:, gsl],
                          start=(di == 0), stop=(di == NCH - 1),
                          waits=w0 if di == 0 else [],
                          inc="tt" if di == NCH - 1 else None)
            vev.append(vin(c, "tensor_copy", vfm[:, gsl], ps,
                           waits=[("tt", cnt)]))
        mposb = mposc[:, 0:16].unsqueeze(2).broadcast_to([P, 16, 2])
        mnegb = mnegc[:, 0:16].unsqueeze(2).broadcast_to([P, 16, 2])
        vone = vin(c, "tensor_copy",
                   v65ph[:, :, :, 64:65].rearrange("p a b e -> p a (b e)"),
                   mposb, waits=[("vv", mask_ready), ("tt", v65_read)])
        none1 = vin(c, "tensor_copy",
                    v65nh[:, :, :, 64:65].rearrange("p a b e -> p a (b e)"),
                    mnegb, waits=[("vv", mask_ready), ("tt", v65_read)])
        vtr = []
        for kt in range(NT):
            tcnt = ins(c, "tensor", "transpose", c.ps_d[:, 0:P],
                       vfm[:, kt * P:(kt + 1) * P], c.cst["identity"][:],
                       waits=[("vv", vev[kt // 4]), ("tt", v65_read)] +
                             bw(c, "d") +
                             ([("vv", vtr[kt - 1])] if kt >= 1 else []),
                       inc="tt")
            vp = vin(c, "tensor_scalar",
                     v65ph[:, kt, :, 0:64],
                     c.ps_d[:, 0:P].rearrange("p (h e) -> p h e", e=64),
                     mposc[:, kt:kt + 1], None, ALU.mult,
                     waits=[("tt", tcnt), ("vv", mask_ready)])
            vn = vin(c, "tensor_scalar",
                     v65nh[:, kt, :, 0:64],
                     c.ps_d[:, 0:P].rearrange("p (h e) -> p h e", e=64),
                     mnegc[:, kt:kt + 1], None, ALU.mult)
            vtr.append(vn)
        br(c, "d", "vv", max(vtr))
        vready = ("vv", max(vtr + [vone, none1]))
        for g in range(NG):
            gsl = slice(g * QG, (g + 1) * QG)
            psq = c.ps_c[:, 0:QG]
            w0 = [wready] + xln_ready + bw(c, "c")
            cnt = None
            for di in range(NCH):
                cnt = mmv(c, psq, wpair[di][:, 0:128], xln[di][:, gsl],
                          start=(di == 0), stop=(di == NCH - 1),
                          waits=w0 if di == 0 else [],
                          inc="tt" if di == NCH - 1 else None)
            qev = vin(c, "tensor_copy", qpair[:], psq, waits=[("tt", cnt)])
            br(c, "c", "vv", qev)
            for hh in range(2):
                hp = slice(hh * HD, (hh + 1) * HD)
                avp = [c.ps_c, c.ps_e][hh][0:65, 0:QG]
                for ri in range(8):
                    k0, k1 = 2 * ri, 2 * ri + 2
                    sreg = [c.ps_a, c.ps_b][ri % 2]
                    qkc = None
                    for kt in range(k0, k1):
                        j = kt - k0
                        qkc = mmv(c, sreg[:, j * QG:(j + 1) * QG],
                                  kpair[hp, kt * P:(kt + 1) * P],
                                  qpair[hp, :], True, True,
                                  waits=([("vv", qev),
                                          ("ss", exp_done[ri % 2])] +
                                         (bw(c, "c", "e") if ri == 0 else []) +
                                         ([vready] if ri == 0 else []))
                                  if kt == k0 else [],
                                  inc="tt" if kt == k1 - 1 else None)
                    ecnt = sin(c, "activation", exps[ri % 2][:],
                               sreg[:, 0:1024], AF.Exp,
                               bias=negb, scale=0.125,
                               waits=[("tt", qkc), ("vv", nbc)])
                    exp_done[ri % 2] = ecnt
                    for kt in range(k0, k1):
                        j = kt - k0
                        mmv(c, c.ps_c[0:65, 0:QG], v65pt[kt][:, 65 * hh:65 * hh + 65],
                            exps[ri % 2][:, j * QG:(j + 1) * QG],
                            start=(kt == 0), stop=(kt == NT - 1),
                            waits=[("ss", ecnt)] +
                                  ([("vv", av_done[0])] if kt == 0 else []),
                            inc=None)
                        avc = mmv(c, c.ps_e[0:65, 0:QG],
                                  v65nt[kt][:, 65 * hh:65 * hh + 65],
                                  exps[ri % 2][:, j * QG:(j + 1) * QG],
                                  start=(kt == 0), stop=(kt == NT - 1),
                                  waits=([("vv", av_done[1])] if kt == 0 else []),
                                  inc="tt" if kt == NT - 1 else None)
                v1 = vin(c, "tensor_copy", rr[0:1, :], c.ps_c[64:65, 0:QG],
                         waits=[("tt", avc)])
                v2 = vin(c, "tensor_copy", rrb, c.ps_e[64:65, 0:QG])
                vin(c, "reciprocal", rr[0:1, :], rr[0:1, :])
                vin(c, "reciprocal", rrb, rrb)
                v3 = vin(c, "tensor_tensor", rr[0:1, :], rr[0:1, :],
                         rowA[0:1, gsl], ALU.mult)
                v4 = vin(c, "tensor_tensor", rrb, rrb,
                         rowB[0:1, gsl], ALU.mult)
                v5 = vin(c, "tensor_copy", rr16[0:1, :], rr[0:1, :])
                v5b = vin(c, "tensor_copy", rr16b, rrb)
                bcr = mmv(c, c.ps_f[0:64, 0:QG], c.cst["ones_row"][0:1, 0:64],
                          rr16[0:1, :], True, True,
                          waits=[("vv", v5b)] + bw(c, "f"), inc="tt")
                bcr2 = mmv(c, c.ps_f[64:128, 0:QG],
                           c.cst["ones_row"][0:1, 0:64],
                           rr16b, True, True, inc="tt")
                v6 = vin(c, "tensor_copy", bcq[0:64, :], c.ps_f[0:64, 0:QG],
                         waits=[("tt", bcr2)])
                v7 = vin(c, "tensor_copy", bcn[0:64, :], c.ps_f[64:128, 0:QG])
                br(c, "f", "vv", v7)
                tgt = stage64 if hh == 1 else None
                u1 = vin(c, "tensor_tensor", stageB[0:64, :],
                         c.ps_e[0:64, 0:QG], bcn[0:64, :], ALU.mult,
                         waits=[("vv", v7), ("bd", odd_dma)])
                av_done[1] = u1
                br(c, "e", "vv", u1)
                u2 = vin(c, "tensor_tensor", stage64[0:64, :],
                         c.ps_c[0:64, 0:QG], bcq[0:64, :], ALU.mult)
                av_done[0] = u2
                br(c, "c", "vv", u2)
                if hh == 0:
                    fin = vin(c, "tensor_tensor", ofm[pair][0:64, gsl],
                              stage64[0:64, :], stageB[0:64, :], ALU.add)
                else:
                    fin = vin(c, "tensor_tensor", stage64[0:64, :],
                              stage64[0:64, :], stageB[0:64, :], ALU.add)
                    odd_dma = dma(c, "sync", ofm[pair][64:128, gsl],
                                  stage64[0:64, :], waits=[("vv", fin)],
                                  inc="bd")
                    pr.fence("sync", "bd")
        v65_read = pr.now("tt")
        wpair_read = pr.now("tt")

    ab.pop()
    wout = [ab.take(D, f"n_wo{i}") for i in range(NCH)]
    yb = [ab.take(QG, f"n_yb{i}") for i in range(NCH)]
    wout_cnt = max(load_w(c, wout[di],
                          c.dram["l1f_wproj"][di * P:(di + 1) * P, :],
                          waits=[("tt", wpair_read)])
                   for di in range(NCH))
    pr.fence("gpsimd", "wd")

    first = [("wd", wout_cnt), ("bd", odd_dma),
             ("ss", exp_done[0]), ("ss", exp_done[1]),
             ("vv", max(av_done))]
    for g in range(NG):
        gsl = slice(g * QG, (g + 1) * QG)
        ev = []
        for oc in range(NCH):
            reg = [c.ps_a, c.ps_b][oc % 2]
            ps = reg[:, (oc // 2 % 2) * QG:(oc // 2 % 2) * QG + QG]
            cnt = None
            for di in range(NCH):
                cnt = mmv(c, ps, wout[di][:, oc * P:(oc + 1) * P],
                          ofm[di][:, gsl],
                          start=(di == 0), stop=(di == NCH - 1),
                          waits=(first + bw(c, "a", "b") +
                                 ([("vv", ev[oc - 4])] if oc >= 4 else []))
                          if di == 0 else [],
                          inc="tt" if di == NCH - 1 else None)
            ev.append(vin(c, "tensor_scalar", yb[oc][:], ps,
                          c.cst["l1f_bproj_c"][:, oc:oc + 1], None, ALU.add,
                          waits=[("tt", cnt)]))
        first = [("vv", max(ev))]
        br(c, "a", "vv", max(ev))
        br(c, "b", "vv", max(ev))
        post_ln(c, yb, g, [("vv", max(ev))])
    ab.pop(); af.pop()


def emit_l1a(c, xln, xln_ready):
    pr, ab, af = c.pr, c.ab, c.af
    ab.push(); af.push()
    ofm = [ab.take(S, f"a_ofm{i}") for i in range(NCH)]
    ab.push()
    kpair = ab.take(S, "a_kpair")
    vfm = ab.take(S, "a_vfm")
    v65 = ab.take(NT * 130, "a_v65")
    v65t = [v65[:, kt * 130:kt * 130 + 130] for kt in range(NT)]
    v65h = v65.rearrange("p (t h e) -> p t h e", h=2, e=65)
    qpair = ab.take(QG, "a_qpair")
    exps = [ab.take(1024, "a_exps0"), ab.take(1024, "a_exps1")]
    wpair = [ab.take(384, f"a_wp{i}") for i in range(NCH)]
    sqs = [ab.take(QG, "a_sqs0"), ab.take(QG, "a_sqs1")]
    stage64 = ab.take(QG, "a_st64")
    rr16 = ab.take(512, "a_rr16")[0:1, :]
    bcq = ab.take(QG, "a_bcq")
    attn_g = ab.take(4 * 384, "a_attn")
    dbuf = ab.take(256, "a_dbuf")
    dtb = ab.take(256, "a_dtb")
    lrsel = ab.take(QG, "a_lrsel")
    edgeLR = ab.take(64, "a_edge")
    rve16 = ab.take(64, "a_rve16")[0:2, :]
    zr65 = ab.take(68, "a_zr65")
    qforb = ab.take(2052, "a_qforb")
    ncol = ab.take(4, "a_ncol")[:, 0:1]
    ncol2 = [ab.take(4, "a_nc0")[:, 0:1], ab.take(4, "a_nc1")[:, 0:1]]
    gqb = ab.take(4, "a_gqb")[0:2, 0:1]
    mixt = [ab.take(P, f"a_mx{j}") for j in range(NCH)]
    mprod = [ab.take(P, "a_mpL"), ab.take(P, "a_mpR")]
    mcb = [ab.take(4, "a_mcbL"), ab.take(4, "a_mcbR")]
    mrow = [ab.take(P, "a_mrL"), ab.take(P, "a_mrR")]
    yb = None
    af.push()
    band_g = af.take(4 * 384, "a_band")
    rr = af.take(QG, "a_rr")[0:1, :]
    psb = af.take(260, "a_psb")
    pdiff2 = af.take(512, "a_pd2")
    bias2 = af.take(4, "a_bias2")
    q2col = af.take(4, "a_q2")[0:2, 0:1]
    k2col = af.take(4, "a_k2")[0:2, 0:1]
    rk2 = af.take(4, "a_rk2")[0:2, 0:1]
    tcol = af.take(4, "a_tc")[0:2, 0:1]
    macc = [af.take(4, "a_maL"), af.take(4, "a_maR")]
    drow = af.take(QG, "a_drow")[0:1, :]
    dcol = af.take(4, "a_dcol")[:, 0:1]
    gq2 = af.take(4, "a_gq2")[0:2, 0:1]
    gqt = af.take(4, "a_gqt")[0:2, 0:1]
    qf2col = af.take(4, "a_qf2")[0:2, 0:1]
    t2col = af.take(4, "a_t2c")[0:2, 0:1]

    WQ, WK, WV = c.dram["l1a_wq"], c.dram["l1a_wk"], c.dram["l1a_wv"]

    relk = c.cst["l1a_relkT"]
    sq0 = sin(c, "activation", sqs[0][0:64, 0:257], relk[0:64, :], AF.Square,
              waits=c.ready)
    rmm = mmv(c, c.ps_f[0:2, 0:257], c.cst["ones_2col"][0:64, 0:2],
              sqs[0][0:64, 0:257], True, True,
              waits=[("ss", sq0)] + bw(c, "f"), inc="tt")
    rmx = vin(c, "reduce_max", rk2, c.ps_f[0:2, 0:257], axis=AX.X,
              waits=[("tt", rmm)])
    br(c, "f", "vv", rmx)
    e16 = vin(c, "tensor_copy", rve16[0:2, 0:HD], c.cst["l1a_relv_edge"][:],
              waits=c.ready)
    zrm = vin(c, "memset", zr65[0:1, 0:65], 0.0, waits=c.ready)
    emm = mmv(c, c.ps_f[0:32, 0:HD], c.cst["edge2"][:], rve16[0:2, 0:HD],
              True, True, waits=[("vv", e16)] + bw(c, "f"), inc="tt")
    ecv = vin(c, "tensor_copy", edgeLR[0:32, 0:HD], c.ps_f[0:32, 0:HD],
              waits=[("tt", emm)])
    br(c, "f", "vv", ecv)

    odd_dma = pr.now("bd")
    av_done = 0
    exp_done = [0, 0]
    v65_read = 0
    kv_ss = 0
    wpair_read = 0
    c.scr_tt = 0
    c.l1a_ce = {"a0": 0, "a1": 0, "t0": 0, "t1": 0, "m0": 0, "m1": 0}
    c.prev_bg = 0
    c.attn_rd = 0
    c.dbuf_rd = 0
    c.dmm_prev = 0

    mxc = max(load_w(c, mixt[j], c.dram["l1a_mix"][j * P:(j + 1) * P, :],
                     waits=c.ready) for j in range(NCH))
    pr.fence("gpsimd", "wd")
    qsend = {}
    AQI = c.dram["aq_in"].tensor
    AQO = c.dram["aq_out"].tensor
    AVI = c.dram["av_in"].tensor
    AVO = c.dram["av_out"].tensor
    nmm_prev = 0
    ncol_ev = 0
    for pair in range(NH // 2):
        h0 = 2 * pair
        wc = [load_w(c, wpair[di][:, 0:P],
                     WQ[di * P:(di + 1) * P, h0 * HD:h0 * HD + P],
                     waits=[("tt", wpair_read)] if wpair_read else [])
              for di in range(NCH)]
        pr.fence("gpsimd", "wd")
        wready = ("wd", max(wc))
        qfm = ofm[pair]
        qevs = []
        mx_ev = 0
        for g in range(NG):
            gsl = slice(g * QG, (g + 1) * QG)
            key = "a" if g % 2 == 0 else "b"
            ps = [c.ps_a, c.ps_b][g % 2][:, 0:QG]
            w0 = [wready] + xln_ready + bw(c, key)
            cnt = None
            for di in range(NCH):
                cnt = mmv(c, ps, wpair[di][:, 0:P], xln[di][:, gsl],
                          start=(di == 0), stop=(di == NCH - 1),
                          waits=w0 if di == 0 else [],
                          inc="tt" if di == NCH - 1 else None)
            ev = vin(c, "tensor_scalar", qfm[:, gsl], ps,
                     c.cst["l1a_bq_c"][:, pair:pair + 1], None, ALU.add,
                     waits=[("tt", cnt)])
            br(c, key, "vv", ev)
            qevs.append(ev)
            scr = exps[g % 2][:, 0:QG]
            sq = sin(c, "activation", scr, qfm[:, gsl], AF.Square,
                     waits=[("vv", ev)] +
                           ([("tt", nmm_prev)] if nmm_prev else []))
            nmm_prev = mmv(c, c.ps_e[0:2, 0:QG], c.cst["headmask16"][:, 0:2],
                           scr, True, True,
                           waits=[("ss", sq)] + bw(c, "e"), inc="tt")
            if g == 0:
                mx_ev = vin(c, "reduce_max", gq2, c.ps_e[0:2, 0:QG],
                            axis=AX.X, waits=[("tt", nmm_prev)])
            else:
                vin(c, "reduce_max", gqt, c.ps_e[0:2, 0:QG], axis=AX.X,
                    waits=[("tt", nmm_prev)])
                mx_ev = vin(c, "tensor_tensor", gq2, gq2, gqt, ALU.max)
            br(c, "e", "vv", mx_ev)
        gbe = vin(c, "tensor_copy", gqb, gq2, waits=[("vv", mx_ev)])
        pmm2 = mmv(c, c.ps_f[:, 0:1], c.cst["pairsel2"][:], gqb,
                   True, True, waits=[("vv", gbe)] + bw(c, "f"), inc="tt")
        ncol_ev = vin(c, "tensor_copy", ncol, c.ps_f[:, 0:1],
                      waits=[("tt", pmm2)] +
                            ([("bd", qsend[pair - 1])] if pair else []))
        br(c, "f", "vv", ncol_ev)
        wpair_read = pr.now("tt")
        sd1 = dma(c, "sync",
                  bass.AP(AQI, pair * P * S, [[S, P], [1, S]]),
                  qfm[:], waits=[("vv", max(qevs))], inc="bd")
        sd2 = dma(c, "sync",
                  bass.AP(AQI, NCH * P * S + pair * P, [[P, 1], [1, P]]),
                  ncol, waits=[("vv", ncol_ev)], inc="bd")
        pr.fence("sync", "bd")
        qsend[pair] = sd2
    wpair_read = pr.now("tt")
    agq = pr.on("gpsimd", lambda e: e.collective_compute(
        "AllGather", ALU.bypass, ins=[c.dram["aq_in"]],
        outs=[c.dram["aq_out"]], replica_groups=[list(range(8))]),
        waits=[("bd", qsend[NCH - 1])], inc="gg")
    agq_cnt = agq

    qforb_rd = 0
    for pair in range(NH // 2):
        h0 = 2 * pair
        qstk2 = [kpair, vfm]
        tiles = [c.ps_a[:, 0:QG], c.ps_a[:, QG:2 * QG],
                 c.ps_b[:, 0:QG], c.ps_b[:, QG:2 * QG]]
        mixlast = 0
        for jj in range(NCH):
            buf = qstk2[jj % 2]
            QN = NCH * P * S + NCH * P
            lw = dma(c, "sync", buf[:, 0:2048],
                     bass.AP(AQO, pair * QN + jj * P * S,
                             [[S, P], [1, S]]),
                     waits=[("gg", agq_cnt)] +
                           ([("tt", mixlast)] if jj >= 2
                            else ([("tt", wpair_read)] if wpair_read else [])),
                     inc="bd")
            ln_ = dma(c, "sync", ncol2[jj % 2],
                      bass.AP(AQO, pair * QN + NCH * P * S + jj * P,
                              [[P, 1], [1, P]]),
                      waits=[("gg", agq_cnt)] +
                            ([("tt", mixlast)] if jj >= 2 else []),
                      inc="bd")
            pr.fence("sync", "bd")
            w0 = ([("bd", ln_), ("wd", mxc)] +
                  ([("tt", qforb_rd)] if (jj == 0 and qforb_rd) else []) +
                  ([("ss", exp_done[0]), ("ss", exp_done[1])] +
                   bw(c, "a") + bw(c, "b") + bw(c, "e") if jj == 0 else []))
            for ct in range(4):
                mmv(c, tiles[ct], mixt[jj], buf[:, ct * QG:(ct + 1) * QG],
                    start=(jj == 0), stop=(jj == NCH - 1),
                    waits=w0 if ct == 0 else [],
                    inc="tt" if (jj == NCH - 1 and ct == 3) else None)
            mixlast = mmv(c, c.ps_e[:, 0:1], mixt[jj], ncol2[jj % 2],
                          start=(jj == 0), stop=(jj == NCH - 1), inc="tt")
        qf_ev = []
        for ct in range(4):
            qf_ev.append(vin(c, "tensor_copy", qforb[:, ct * QG:(ct + 1) * QG],
                             tiles[ct], waits=[("tt", mixlast)]))
        qf_ev.append(vin(c, "tensor_copy", qforb[:, 2048:2049],
                         c.ps_e[:, 0:1]))
        qfor_ready = max(qf_ev)
        br(c, "a", "vv", qfor_ready); br(c, "b", "vv", qfor_ready)
        br(c, "e", "vv", qfor_ready)
        nmm2 = mmv(c, c.ps_f[0:2, 0:1], c.cst["headmask16"][:, 0:2],
                   qforb[:, 2048:2049], True, True,
                   waits=[("vv", qfor_ready)] + bw(c, "f"), inc="tt")
        qfv = vin(c, "tensor_scalar", qf2col, c.ps_f[0:2, 0:1], 1.0 / 64.0,
                  None, ALU.mult, waits=[("tt", nmm2)])
        br(c, "f", "vv", qfv)
        wcnts = []
        for di in range(NCH):
            for slot, wsrc in ((0, WQ), (1, WK), (2, WV)):
                wcnts.append(load_w(
                    c, wpair[di][:, slot * P:(slot + 1) * P],
                    wsrc[di * P:(di + 1) * P, h0 * HD:h0 * HD + P],
                    waits=[("tt", wpair_read)] if wpair_read else []))
        pr.fence("gpsimd", "wd")
        wready = ("wd", max(wcnts))
        kev = []
        for g in range(NG):
            gsl = slice(g * QG, (g + 1) * QG)
            key = "a" if g % 2 == 0 else "b"
            ps = [c.ps_a, c.ps_b][g % 2][:, 0:QG]
            w0 = [wready, ("ss", exp_done[g % 2]), ("ss", kv_ss)] +                 xln_ready + bw(c, key) +                 ([("vv", kev[g - 2])] if g >= 2 else [])
            cnt = None
            for di in range(NCH):
                cnt = mmv(c, ps, wpair[di][:, 128:256], xln[di][:, gsl],
                          start=(di == 0), stop=(di == NCH - 1),
                          waits=w0 if di == 0 else [],
                          inc="tt" if di == NCH - 1 else None)
            kev.append(vin(c, "tensor_scalar", kpair[:, gsl], ps,
                           c.cst["l1a_bk_c"][:, pair:pair + 1], None, ALU.add,
                           waits=[("tt", cnt), ("tt", mixlast)]))
        nrm_mm = 0
        for g in range(NG):
            gsl = slice(g * QG, (g + 1) * QG)
            scnt = sin(c, "activation", sqs[g % 2][:], kpair[:, gsl],
                       AF.Square, waits=[("vv", kev[g])] +
                       ([("tt", nrm_mm)] if g >= 2 else []))
            kv_ss = scnt
            nrm_mm = mmv(c, c.ps_e[0:2, 0:QG], c.cst["headmask16"][:, 0:2],
                         sqs[g % 2][:], start=(g == 0), stop=(g == NG - 1),
                         waits=[("ss", scnt)] + (bw(c, "e") if g == 0 else []),
                         inc="tt")
            if g == 0:
                br(c, "e", "tt", nrm_mm)
        kn = vin(c, "reduce_max", k2col, c.ps_e[0:2, 0:QG], axis=AX.X,
                 waits=[("tt", nrm_mm)])
        br(c, "e", "vv", kn)
        vev = []
        for g in range(NG):
            gsl = slice(g * QG, (g + 1) * QG)
            key = "a" if g % 2 == 0 else "b"
            ps = [c.ps_a, c.ps_b][g % 2][:, 0:QG]
            w0 = [wready, ("ss", exp_done[g % 2]), ("ss", kv_ss)] + bw(c, key) +                 ([("vv", vev[g - 2])] if g >= 2 else [])
            cnt = None
            for di in range(NCH):
                cnt = mmv(c, ps, wpair[di][:, 256:384], xln[di][:, gsl],
                          start=(di == 0), stop=(di == NCH - 1),
                          waits=w0 if di == 0 else [],
                          inc="tt" if di == NCH - 1 else None)
            vev.append(vin(c, "tensor_scalar", vfm[:, gsl], ps,
                           c.cst["l1a_bv_c"][:, pair:pair + 1], None, ALU.add,
                           waits=[("tt", cnt), ("tt", mixlast)]))
        vone = vin(c, "memset", v65h[:, :, :, 64:65], 1.0,
                   waits=[("tt", v65_read)])
        vtr = []
        for kt in range(NT):
            tcnt = ins(c, "tensor", "transpose", c.ps_d[:, 0:P],
                       vfm[:, kt * P:(kt + 1) * P], c.cst["identity"][:],
                       waits=[("vv", vev[kt // 4]), ("tt", v65_read)] +
                             bw(c, "d") +
                             ([("vv", vtr[kt - 1])] if kt >= 1 else []),
                       inc="tt")
            vtr.append(vin(c, "tensor_copy", v65h[:, kt, :, 0:64],
                           c.ps_d[:, 0:P].rearrange("p (h e) -> p h e", e=64),
                           waits=[("tt", tcnt)]))
        br(c, "d", "vv", max(vtr))
        vready = ("vv", max(vtr + [vone]))

        for g in range(NG):
            gsl = slice(g * QG, (g + 1) * QG)
            psq = c.ps_c[:, 0:QG]
            w0 = [wready] + xln_ready + bw(c, "c")
            cnt = None
            for di in range(NCH):
                cnt = mmv(c, psq, wpair[di][:, 0:128], xln[di][:, gsl],
                          start=(di == 0), stop=(di == NCH - 1),
                          waits=w0 if di == 0 else [],
                          inc="tt" if di == NCH - 1 else None)
            qev = vin(c, "tensor_scalar", qpair[:], psq,
                      c.cst["l1a_bq_c"][:, pair:pair + 1], None, ALU.add,
                      waits=[("tt", cnt), ("ss", c.qpair_ss)])
            br(c, "c", "vv", qev)
            scnt = sin(c, "activation", sqs[0][:], qpair[:], AF.Square,
                       waits=[("vv", qev), ("tt", nrm_mm)])
            c.qpair_ss = scnt
            qmm = mmv(c, c.ps_e[0:2, 0:QG], c.cst["headmask16"][:, 0:2],
                      sqs[0][:], True, True,
                      waits=[("ss", scnt)] + bw(c, "e"), inc="tt")
            qn = vin(c, "reduce_max", q2col, c.ps_e[0:2, 0:QG], axis=AX.X,
                     waits=[("tt", qmm)])
            br(c, "e", "vv", qn)
            lq = sin(c, "activation", q2col, q2col, AF.Ln,
                     waits=[("vv", qn), ("vv", kn), ("vv", qfv),
                            ("ss", c.bias_ss)])
            sin(c, "activation", q2col, q2col, AF.Exp, scale=0.5)
            sin(c, "activation", k2col, k2col, AF.Ln)
            sin(c, "activation", k2col, k2col, AF.Exp, scale=0.5)
            sin(c, "activation", rk2, rk2, AF.Ln)
            sin(c, "activation", rk2, rk2, AF.Exp, scale=0.5)
            sin(c, "activation", qf2col, qf2col, AF.Ln)
            sq_ = sin(c, "activation", qf2col, qf2col, AF.Exp, scale=0.5)
            v0 = vin(c, "tensor_scalar", tcol, rk2, 2.0, None, ALU.mult,
                     waits=[("ss", sq_)])
            vin(c, "tensor_tensor", tcol, tcol, qf2col, ALU.mult)
            vin(c, "tensor_tensor", t2col, q2col, k2col, ALU.mult)
            v2 = vin(c, "tensor_tensor", tcol, tcol, t2col, ALU.add)
            vin(c, "tensor_tensor", k2col, k2col, k2col, ALU.mult)
            vin(c, "tensor_tensor", qf2col, qf2col, qf2col, ALU.mult)
            v3 = vin(c, "tensor_tensor", rk2, rk2, rk2, ALU.mult)
            tmm = mmv(c, c.ps_f[0:1, 0:2], tcol, c.cst["idrow16"][0:2, 0:2],
                      True, True, waits=[("vv", v2)] + bw(c, "f"), inc="tt")
            trw = vin(c, "tensor_copy", rr[0:1, 0:2], c.ps_f[0:1, 0:2],
                      waits=[("tt", tmm)])
            bmm = mmv(c, c.ps_f[:, 0:2], c.cst["ones_row_f"][0:1, :],
                      rr[0:1, 0:2], True, True, waits=[("vv", trw)], inc="tt")
            nbc = vin(c, "tensor_scalar", bias2[:, 0:2], c.ps_f[:, 0:2],
                      -0.125 * 1.01, None, ALU.mult, waits=[("tt", bmm)])
            br(c, "f", "vv", nbc)
            c.bias_ss = pr.now("ss")
            for hh in range(2):
                h = h0 + hh
                hp = slice(hh * HD, (hh + 1) * HD)
                avp = c.ps_c[0:65, 0:QG]
                pb_ev = {}
                d_ev = {}
                for qi4 in range(4):
                    qt = g * 4 + qi4
                    pmm = mmv(c, c.ps_e[:, 0:257],
                              qforb[hp, qt * P:(qt + 1) * P], relk[hp, :],
                              True, True,
                              waits=bw(c, "e") + [("vv", qfor_ready)],
                              inc="tt")
                    pev = vin(c, "tensor_copy", psb[:, 0:257], c.ps_e[:, 0:257],
                              waits=[("tt", pmm)])
                    br(c, "e", "vv", pev)
                    z0 = vin(c, "memset", pdiff2[:, 0:128], 0.0,
                             waits=[("bd", c.prev_bg)] if c.prev_bg else [])
                    b1 = vin(c, "tensor_scalar", pdiff2[:, 128:383],
                             bass.AP(psb.tensor, psb.offset + 255,
                                     [[psb.ap[0][0], P], [-1, 255]]),
                             psb[:, 256:257], None, ALU.subtract,
                             waits=[("vv", pev)])
                    dc_ = vin(c, "tensor_tensor", dcol[:, 0:1], psb[:, 0:1],
                              psb[:, 256:257], ALU.subtract,
                              waits=[("tt", c.dmm_prev)] if c.dmm_prev else [])
                    b2 = vin(c, "tensor_copy", pdiff2[:, 383:512],
                             dcol[:, 0:1].broadcast_to([P, 129]))
                    bg = dma(c, "sync", band_g[:, qi4 * 384:(qi4 + 1) * 384],
                             bass.AP(pdiff2.tensor, pdiff2.offset + 127,
                                     [[c.af.words - 1, P], [1, 384]]),
                             waits=[("vv", b2), ("vv", b1), ("vv", z0),
                                    ("tt", c.scr_tt)],
                             inc="bd")
                    pr.fence("sync", "bd")
                    pb_ev[qi4] = bg
                    c.prev_bg = bg
                    dmm2 = mmv(c, c.ps_f[0:1, 0:P], dcol[:, 0:1],
                               c.cst["identity_f"][:], True, True,
                               waits=[("vv", dc_)] + bw(c, "f"), inc="tt")
                    c.dmm_prev = dmm2
                    d_ev[qi4] = vin(c, "tensor_copy",
                                    drow[0:1, qi4 * P:(qi4 + 1) * P],
                                    c.ps_f[0:1, 0:P], waits=[("tt", dmm2)])
                    br(c, "f", "vv", d_ev[qi4])
                dr_ready = ("vv", max(d_ev.values()))
                zats = []
                ag_guard = [("bd", c.attn_rd)] if c.attn_rd else []
                if g == 0:
                    zats.append(vin(c, "memset", attn_g[:, 0:128], 0.0,
                                    waits=ag_guard))
                if g == NG - 1:
                    zats.append(vin(c, "memset",
                                    attn_g[:, 3 * 384 + 256:4 * 384], 0.0,
                                    waits=ag_guard))
                first_av = True
                lr_first = True
                atr_ev = {}
                for ri in range(8):
                    k0 = 2 * ri
                    for kt in (k0, k0 + 1):
                        j = kt - k0
                        sreg = [c.ps_a, c.ps_b][ri % 2]
                        seg = sreg[:, j * QG:(j + 1) * QG]
                        writers = [("qk", None, None)]
                        for qi4 in range(4):
                            qt = g * 4 + qi4
                            s_ = kt - qt + 1
                            if 0 <= s_ <= 2:
                                writers.append(("inj", qi4, s_))
                        for qi4 in range(4):
                            qt = g * 4 + qi4
                            if kt >= qt + 2:
                                writers.append(("delta", qi4, None))
                        for wi, (kind, qi4, s_) in enumerate(writers):
                            st_ = (wi == 0)
                            sp_ = (wi == len(writers) - 1)
                            if kind == "inj":
                                ins(c, "tensor", "matmul",
                                    seg[:, qi4 * P:(qi4 + 1) * P],
                                    band_g[:, qi4 * 384 + s_ * P:
                                           qi4 * 384 + (s_ + 1) * P],
                                    c.cst["identity_f"][:],
                                    start=st_, stop=sp_, is_transpose=True,
                                    waits=[("bd", pb_ev[qi4])] +
                                          ([("ss", exp_done[ri % 2])]
                                           if wi == 0 else []),
                                    inc="tt")
                            elif kind == "qk":
                                mmv(c, seg, kpair[hp, kt * P:(kt + 1) * P],
                                    qpair[hp, :], start=st_, stop=sp_,
                                    waits=([("vv", qev),
                                            ("ss", exp_done[ri % 2])]
                                           if wi == 0 else []),
                                    inc="tt")
                            else:
                                mmv(c, seg[:, qi4 * P:(qi4 + 1) * P],
                                    c.cst["ones_row_f"][0:1, :],
                                    drow[0:1, qi4 * P:(qi4 + 1) * P],
                                    start=st_, stop=sp_,
                                    waits=[dr_ready],
                                    inc="tt")
                    qkc = pr.now("tt")
                    ecnt = sin(c, "activation", exps[ri % 2][:],
                               sreg[:, 0:1024], AF.Exp,
                               bias=bias2[:, hh:hh + 1], scale=0.125,
                               waits=[("tt", qkc), ("vv", nbc)])
                    exp_done[ri % 2] = ecnt
                    for kt in (k0, k0 + 1):
                        j = kt - k0
                        mmv(c, avp, v65t[kt][:, 65 * hh:65 * hh + 65],
                            exps[ri % 2][:, j * QG:(j + 1) * QG],
                            start=first_av, stop=(kt == NT - 1),
                            waits=[("ss", ecnt), vready] +
                                  (bw(c, "c") if first_av else []),
                            inc=None)
                        mmv(c, c.ps_e[0:32, 0:QG],
                            c.cst["lr_step"][:, kt * 32:(kt + 1) * 32],
                            exps[ri % 2][:, j * QG:(j + 1) * QG],
                            start=lr_first, stop=(kt == NT - 1),
                            waits=(bw(c, "e") if lr_first else []),
                            inc="tt" if kt == NT - 1 else None)
                        first_av = False
                        lr_first = False
                        for qi4 in range(4):
                            qt = g * 4 + qi4
                            s_ = kt - qt + 1
                            if 0 <= s_ <= 2:
                                tcn = ins(c, "tensor", "transpose",
                                          c.ps_d[:, 0:P],
                                          exps[ri % 2][:, j * QG + qi4 * P:
                                                        j * QG + (qi4 + 1) * P],
                                          c.cst["identity"][:],
                                          waits=[("ss", ecnt)] + bw(c, "d"),
                                          inc="tt")
                                atr_ev[(qi4, s_)] = vin(
                                    c, "tensor_copy",
                                    attn_g[:, qi4 * 384 + s_ * P:
                                           qi4 * 384 + (s_ + 1) * P],
                                    c.ps_d[:, 0:P],
                                    waits=[("tt", tcn)] +
                                          ([("bd", c.attn_rd)]
                                           if c.attn_rd else []))
                                br(c, "d", "vv", atr_ev[(qi4, s_)])
                lr_mm = pr.now("tt")
                v1 = vin(c, "tensor_copy", rr, avp[64:65, :],
                         waits=[("tt", lr_mm)])
                vin(c, "reciprocal", rr, rr)
                v3 = vin(c, "tensor_copy", rr16, rr)
                bcr = mmv(c, c.ps_f[0:64, 0:QG], c.cst["ones_row"][0:1, 0:64],
                          rr16, True, True, waits=[("vv", v3)] + bw(c, "f"),
                          inc="tt")
                v4 = vin(c, "tensor_copy", bcq[0:64, :], c.ps_f[0:64, 0:QG],
                         waits=[("tt", bcr)])
                br(c, "f", "vv", v4)
                if hh == 0:
                    adn = vin(c, "tensor_tensor", ofm[pair][0:64, gsl],
                              avp[0:64, :], bcq[0:64, :], ALU.mult,
                              waits=[("vv", v4), ("bd", odd_dma),
                                     ("bd", qsend[pair])])
                else:
                    adn = vin(c, "tensor_tensor", stage64[0:64, :],
                              avp[0:64, :], bcq[0:64, :], ALU.mult,
                              waits=[("vv", v4), ("bd", odd_dma)])
                    odd_dma = dma(c, "sync", ofm[pair][64:128, gsl],
                                  stage64[0:64, :], waits=[("vv", adn)],
                                  inc="bd")
                    pr.fence("sync", "bd")
                br(c, "c", "vv", adn)
                mmv(c, c.ps_c[0:64, 0:QG], zr65[0:1, 0:64],
                    c.cst["lr_step"][0:1, 0:QG], start=True, stop=False,
                    waits=bw(c, "c") + [("vv", zrm)], inc=None)
                for qi4 in range(4):
                    qt = g * 4 + qi4
                    aw = [("vv", v) for (q_, s_), v in atr_ev.items()
                          if q_ == qi4] + [("vv", z) for z in zats]
                    dg = dma(c, "sync", dbuf[:, 0:255],
                             bass.AP(attn_g.tensor,
                                     attn_g.offset + qi4 * 384 + 1,
                                     [[c.ab.words + 1, P], [1, 255]]),
                             waits=aw + ([("tt", c.dbuf_rd)]
                                         if c.dbuf_rd else []),
                             inc="bd")
                    pr.fence("sync", "bd")
                    tr1 = ins(c, "tensor", "transpose", c.ps_d[:, 0:P],
                              dbuf[:, 0:128], c.cst["identity"][:],
                              waits=[("bd", dg)] + bw(c, "d"), inc="tt")
                    ev1 = vin(c, "tensor_copy", dtb[:, 0:128],
                              c.ps_d[:, 0:P], waits=[("tt", tr1)])
                    tr2 = ins(c, "tensor", "transpose", c.ps_d[0:127, 0:P],
                              dbuf[:, 128:255], c.cst["identity"][:],
                              waits=[("vv", ev1)], inc="tt")
                    ev2 = vin(c, "tensor_copy", dtb[0:127, 128:256],
                              c.ps_d[0:127, 0:P], waits=[("tt", tr2)])
                    br(c, "d", "vv", ev2)
                    c.dbuf_rd = tr2
                    c.attn_rd = dg
                    qsl = slice(qi4 * P, (qi4 + 1) * P)
                    mmv(c, avp[0:64, qsl], c.cst["l1a_relv_mid"][:, 0:64],
                        dtb[:, 0:128], start=False, stop=False,
                        waits=[("vv", ev2)], inc=None)
                    mmv(c, avp[0:64, qsl], c.cst["l1a_relv_mid"][0:127, 64:128],
                        dtb[0:127, 128:256], start=False, stop=False,
                        inc=None)
                    ce = c.l1a_ce
                    base = qi4 * 384
                    for si, (cb, msk, lt) in enumerate(
                            ((1, "tri", rve16[0:1, 0:HD]),
                             (256, "trin", c.cst["l1a_rv0"][0:1, 0:HD]))):
                        mp_ = vin(c, "tensor_tensor", mprod[si][:, 0:127],
                                  attn_g[:, base + cb:base + cb + 127],
                                  c.cst[msk][:, 0:127], ALU.mult,
                                  waits=aw + [("ss", ce[f"a{si}"])])
                        ac_ = sin(c, "activation", mprod[si][:, 0:127],
                                  mprod[si][:, 0:127], AF.Identity,
                                  accum_out=macc[si][:, 0:1],
                                  waits=[("vv", mp_)])
                        ce[f"a{si}"] = ac_
                        cb_ = vin(c, "tensor_copy", mcb[si][:, 0:1],
                                  macc[si][:, 0:1],
                                  waits=[("ss", ac_), ("tt", ce[f"t{si}"])])
                        t_ = ins(c, "tensor", "transpose", c.ps_d[0:1, 0:P],
                                 mcb[si][:, 0:1], c.cst["identity"][:],
                                 waits=[("vv", cb_)] + bw(c, "d"), inc="tt")
                        ce[f"t{si}"] = t_
                        rc_ = vin(c, "tensor_copy", mrow[si][0:1, :],
                                  c.ps_d[0:1, 0:P],
                                  waits=[("tt", t_), ("tt", ce[f"m{si}"])])
                        br(c, "d", "vv", rc_)
                        mm_ = mmv(c, avp[0:64, qsl],
                                  lt, mrow[si][0:1, :],
                                  start=False, stop=False,
                                  waits=[("vv", rc_), ("vv", e16)], inc="tt")
                        ce[f"m{si}"] = mm_
                lsl = vin(c, "tensor_tensor", lrsel[0:32, :],
                          c.ps_e[0:32, 0:QG],
                          c.cst["selmask"][:, gsl], ALU.mult,
                          waits=[("tt", lr_mm)])
                br(c, "e", "vv", lsl)
                edm = mmv(c, avp[0:64, :], edgeLR[0:32, 0:HD],
                          lrsel[0:32, :], start=False, stop=True,
                          waits=[("vv", lsl), ("vv", ecv)], inc="tt")
                wv = vin(c, "tensor_tensor", stage64[0:64, :],
                         c.ps_c[0:64, 0:QG], bcq[0:64, :], ALU.mult,
                         waits=[("tt", edm), ("bd", odd_dma)])
                odd_dma = dma(c, "sync",
                              bass.AP(AVI, (pair * P + hh * HD) * S + g * QG,
                                      [[S, HD], [1, QG]]),
                              stage64[0:64, :], waits=[("vv", wv)], inc="bd")
                pr.fence("sync", "bd")
                av_done = wv
                br(c, "c", "vv", wv)
                c.scr_tt = pr.now("tt")
        v65_read = pr.now("tt")
        wpair_read = pr.now("tt")

    agv = pr.on("gpsimd", lambda e: e.collective_compute(
        "AllGather", ALU.bypass, ins=[c.dram["av_in"]],
        outs=[c.dram["av_out"]], replica_groups=[list(range(8))]),
        waits=[("bd", odd_dma)], inc="gg")
    w2add = 0
    mixlast = 0
    for pair in (range(NH // 2) if not getattr(Ctx, "SKIPC", False) else []):
        qstk2 = [kpair, vfm]
        tiles = [c.ps_a[:, 0:QG], c.ps_a[:, QG:2 * QG],
                 c.ps_b[:, 0:QG], c.ps_b[:, QG:2 * QG]]
        for jj in range(NCH):
            buf = qstk2[jj % 2]
            lw = dma(c, "sync", buf[:, 0:2048],
                     bass.AP(AVO, (pair * 8 * P + jj * P) * S,
                             [[S, P], [1, S]]),
                     waits=[("gg", agv)] +
                           ([("tt", mixlast)] if mixlast
                            else [("tt", wpair_read)]),
                     inc="bd")
            pr.fence("sync", "bd")
            w0 = [("bd", lw), ("wd", mxc), ("ss", exp_done[0]),
                  ("ss", exp_done[1])] +                  (bw(c, "a") + bw(c, "b") if jj == 0 else [])
            for ct in range(4):
                r_ = mmv(c, tiles[ct], mixt[jj],
                         buf[:, ct * QG:(ct + 1) * QG],
                         start=(jj == 0), stop=(jj == NCH - 1),
                         waits=w0 if ct == 0 else [],
                         inc="tt" if ct == 3 else None)
                if ct == 3:
                    mixlast = r_
        evs = [vin(c, "tensor_tensor", ofm[pair][:, ct * QG:(ct + 1) * QG],
                   ofm[pair][:, ct * QG:(ct + 1) * QG], tiles[ct], ALU.add,
                   waits=[("tt", mixlast), ("bd", odd_dma)])
               for ct in range(4)]
        w2add = max(evs)
        br(c, "a", "vv", w2add); br(c, "b", "vv", w2add)
    wpair_read = pr.now("tt")

    ab.pop(); af.pop()
    wout = [ab.take(D, f"a_wo{i}") for i in range(NCH)]
    yb = [ab.take(QG, f"a_yb{i}") for i in range(NCH)]
    wout_cnt = max(load_w(c, wout[di],
                          c.dram["l1a_wo"][di * P:(di + 1) * P, :],
                          waits=[("tt", wpair_read)])
                   for di in range(NCH))
    pr.fence("gpsimd", "wd")
    first = [("wd", wout_cnt), ("bd", odd_dma), ("ss", exp_done[0]),
             ("ss", exp_done[1]), ("vv", av_done), ("vv", w2add)]
    for g in range(NG):
        gsl = slice(g * QG, (g + 1) * QG)
        ev = []
        for oc in range(NCH):
            reg = [c.ps_a, c.ps_b][oc % 2]
            ps = reg[:, (oc // 2 % 2) * QG:(oc // 2 % 2) * QG + QG]
            cnt = None
            for di in range(NCH):
                cnt = mmv(c, ps, wout[di][:, oc * P:(oc + 1) * P],
                          ofm[di][:, gsl],
                          start=(di == 0), stop=(di == NCH - 1),
                          waits=(first + bw(c, "a", "b") +
                                 ([("vv", ev[oc - 4])] if oc >= 4 else []))
                          if di == 0 else [],
                          inc="tt" if di == NCH - 1 else None)
            ev.append(vin(c, "tensor_scalar", yb[oc][:], ps,
                          c.cst["l1a_bo_c"][:, oc:oc + 1], None, ALU.add,
                          waits=[("tt", cnt)]))
        first = [("vv", max(ev))]
        br(c, "a", "vv", max(ev))
        br(c, "b", "vv", max(ev))
        post_ln(c, yb, g, [("vv", max(ev))])
    ab.pop(); af.pop()


def build_chain(c, blocks):
    c.w2slot = {}
    if "L0A" in blocks:
        c.ab.push()
        xln = [c.ab.take(S, f"xln{i}") for i in range(NCH)]
        xcnt = pre_ln(c, xln, c.ready)
        emit_l0a(c, xln, [("vv", xcnt)])
        c.ab.pop()
        c.pr.barrier()
    if "L0F" in blocks:
        c.ab.push()
        xi = [c.ab.take(S + 8, f"xi{i}") for i in range(NCH)]
        gz = [vin(c, "memset", xi[i][:, 0:1], 0.0) for i in range(NCH)]
        xcnt = pre_ln(c, [t[:, 1:1 + S] for t in xi],
                      c.ready + [("vv", max(gz))])
        xw = emit_l0f(c, xi, [("vv", xcnt)])
        for g in range(NG):
            post_ln(c, [xi[ch][:, 1 + g * QG:1 + (g + 1) * QG]
                        for ch in range(NCH)], g, xw)
        c.ab.pop()
        c.pr.barrier()
    if "L1A" in blocks:
        c.qpair_ss = 0
        c.bias_ss = 0
        c.ab.push()
        xln = [c.ab.take(S, f"axln{i}") for i in range(NCH)]
        xcnt = pre_ln(c, xln, c.ready)
        emit_l1a(c, xln, [("vv", xcnt)])
        c.ab.pop()
        c.pr.barrier()
    if "L1F" in blocks:
        c.ab.push()
        xln = [c.ab.take(S, f"nxln{i}") for i in range(NCH)]
        xcnt = pre_ln(c, xln, c.ready)
        emit_l1f(c, xln, [("vv", xcnt)])
        c.ab.pop()
        c.pr.barrier()


def build_program(blocks=BLOCKS):
    nc = bass.Bass("TRN2", target_bir_lowering=False)
    es = contextlib.ExitStack()
    nc._keepalive = es
    c = Ctx()
    c.nc, c.es = nc, es
    dram = {}

    def din(name, shape, dtype=F32):
        dram[name] = nc.dram_tensor(name, list(shape), dtype,
                                    kind="ExternalInput").ap()

    din("x_in", [D, S])
    din("l0a_wqkv", [D, 3 * D]); din("l0a_bqkv_c", [P, 24])
    din("l0a_wout", [D, D]); din("l0a_bout_c", [P, 8])
    for l in range(2):
        din(f"l0f{l}_wqkv", [D, 1536]); din(f"l0f{l}_wo", [512, D])
        din(f"l0f{l}_w1a", [D, HIDP]); din(f"l0f{l}_w1g", [D, HIDP])
        din(f"l0f{l}_w2", [HIDP, D])
        din(f"l0f{l}_b1a_c", [P, 22]); din(f"l0f{l}_b1g_c", [P, 22])
        din(f"l0f{l}_b2_c", [P, 8]); din(f"l0f{l}_lng_c", [P, 8])
        din(f"l0f{l}_lnb_c", [P, 8])
        din(f"l0f{l}_qscale", [1, HD]); din(f"l0f{l}_kscale", [1, HD])
    for w in ("wq", "wk", "wv", "wo"):
        din(f"l1a_{w}", [D, D])
    din("l1a_bq_c", [P, 8]); din("l1a_bk_c", [P, 8]); din("l1a_bv_c", [P, 8])
    din("l1a_bo_c", [P, 8])
    din("l1a_relkT", [P, 257]); din("l1a_relv_mid", [P, P])
    din("l1a_relv_edge", [2, HD]); din("l1a_rv0", [1, HD])
    din("tri", [P, P], BF16); din("trin", [P, P], BF16)
    din("l1a_mix", [8 * P, P])
    QN = NCH * P * S + NCH * P
    for nm, shp in (("aq_in", [1, QN]), ("aq_out", [8, QN]),
                    ("av_in", [8 * P, S]), ("av_out", [64 * P, S])):
        kw = {"addr_space": "Shared"} if nm.endswith("_out") else {}
        dram[nm] = nc.dram_tensor(nm, shp, BF16, kind="Internal", **kw).ap()
    din("l1f_wqkv", [D, 3 * D]); din("l1f_wproj", [D, D])
    din("l1f_bproj_c", [P, 8]); din("l1f_qg", [NH, HD])
    din("identity", [P, P], BF16)
    din("ones_row", [1, P], BF16); din("ones_row_f", [1, P])
    din("ones_col", [P, 1], BF16)
    din("idrow16", [16, 16])
    din("headmask16", [P, NCH * NH], BF16)
    din("mask_m4", [P, P], BF16); din("mask_m0", [P, P], BF16)
    din("lr_step", [P, NT * 32], BF16); din("selmask", [32, S], BF16)
    din("rot_qc", [S, HD]); din("rot_qs", [S, HD])
    din("rot_kc", [S, HD]); din("rot_ks", [S, HD])
    din("edge2", [2, 32], BF16)
    din("c2p65", [16, 1])
    din("pairsel2", [2, P], BF16)
    din("ones_2col", [P, 2], BF16)
    din("identity_f", [P, P])
    din("rotperm", [P, P], BF16)
    for nme in ("fm_qc", "fm_qs", "fm_kc", "fm_ks"):
        din(nme, [P, S], BF16)
    for l in range(2):
        din(f"l0f{l}_qscale_c", [P, 1])
        din(f"l0f{l}_kscale_c", [P, 1])
    c.dram = dram
    c.y_out = nc.dram_tensor("y_out", [D, S], F32, kind="ExternalOutput").ap()
    c.dbg_out = nc.dram_tensor("dbg_out", [P, 8192], F32, kind="ExternalOutput").ap()
    c.DBG = Ctx.DBG
    c.cc_in = nc.dram_tensor("cc_in", [1, 2], F32, kind="Internal")
    c.cc_out = nc.dram_tensor("cc_out", [1, 2], F32, kind="Internal",
                              addr_space="Shared")

    pr = Prog(nc, es)
    c.pr = pr
    for s in ("wd", "cd", "tt", "vv", "ss", "gg", "so", "ph", "bd"):
        pr.sem(s)

    c.xs = [es.enter_context(nc.sbuf_tensor(f"x{i}", [P, S + 8], F32))
            for i in range(NCH)]
    c.XD = [t[:, 1:1 + S] for t in c.xs]

    cst = {}
    c.cst = cst

    def sbc(name, shape, dtype=F32, cast=False):
        t = es.enter_context(nc.sbuf_tensor("c_" + name, list(shape), dtype))
        cst[name] = t
        if cast:
            pr.on("gpsimd", lambda e, t=t, n=name: e.dma_start(
                out=t[:], in_=dram[n]), inc="wd", inc_n=16)
        else:
            pr.on("sync", lambda e, t=t, n=name: e.dma_start(
                out=t[:], in_=dram[n]), inc="cd", inc_n=16)

    for n, sh, dt in [
        ("identity", [P, P], BF16), ("ones_row", [1, P], BF16),
        ("ones_row_f", [1, P], F32), ("ones_col", [P, 1], BF16),
        ("idrow16", [16, 16], F32), ("headmask16", [P, NCH * NH], BF16),
        ("mask_m4", [P, P], BF16), ("mask_m0", [P, P], BF16),
        ("lr_step", [P, NT * 32], BF16), ("selmask", [32, S], BF16),
        ("edge2", [2, 32], BF16),
        ("l0a_bqkv_c", [P, 24], F32), ("l0a_bout_c", [P, 8], F32),
        ("l1a_bq_c", [P, 8], F32), ("l1a_bk_c", [P, 8], F32),
        ("l1a_bv_c", [P, 8], F32), ("l1a_bo_c", [P, 8], F32),
        ("l1a_relv_edge", [2, HD], F32),
        ("tri", [P, P], BF16), ("trin", [P, P], BF16),
        ("l1f_bproj_c", [P, 8], F32), ("l1f_qg", [NH, HD], F32),
        ("c2p65", [16, 1], F32),
        ("pairsel2", [2, P], BF16), ("rotperm", [P, P], BF16),
        ("ones_2col", [P, 2], BF16), ("identity_f", [P, P], F32),
    ]:
        sbc(n, sh, dt)
    for l in range(2):
        sbc(f"l0f{l}_b1a_c", [P, 22]); sbc(f"l0f{l}_b1g_c", [P, 22])
        sbc(f"l0f{l}_b2_c", [P, 8]); sbc(f"l0f{l}_lng_c", [P, 8])
        sbc(f"l0f{l}_lnb_c", [P, 8])
        sbc(f"l0f{l}_qscale", [1, HD]); sbc(f"l0f{l}_kscale", [1, HD])
        sbc(f"l0f{l}_qscale_c", [P, 1]); sbc(f"l0f{l}_kscale_c", [P, 1])
    sbc("l1a_relkT", [P, 257], BF16, cast=True)
    sbc("l1a_relv_mid", [P, P], BF16, cast=True)
    sbc("l1a_rv0", [1, HD], BF16, cast=True)
    for ch in range(NCH):
        pr.on("sync", lambda e, ch=ch: e.dma_start(
            out=c.XD[ch], in_=dram["x_in"][ch * P:(ch + 1) * P, :]),
            inc="cd", inc_n=16)
    pr.fence("sync", "cd")
    pr.fence("gpsimd", "wd")
    c.ready = [("cd", pr.now("cd")), ("wd", pr.now("wd"))]

    c.ab = Arena(nc, es, "arena_b16", BF16, 58880)
    c.af = Arena(nc, es, "arena_f32", F32, 4288)
    c.lnsq = [c.ab.take(QG, "lnsq0"), c.ab.take(QG, "lnsq1")]
    c.nbcol = c.af.take(4, "nbcol")
    c.nbrow = c.af.take(16, "nbrow")

    c.ps_a = es.enter_context(nc.psum_tensor("ps_a", [P, 1024], F32))
    c.ps_b = es.enter_context(nc.psum_tensor("ps_b", [P, 1024], F32))
    c.ps_c = es.enter_context(nc.psum_tensor("ps_c", [P, 512], F32))
    c.ps_e = es.enter_context(nc.psum_tensor("ps_e", [P, 512], F32))
    c.ps_f = es.enter_context(nc.psum_tensor("ps_f", [P, 512], F32))
    c.ps_d = es.enter_context(nc.psum_tensor("ps_d", [P, 1024], BF16))

    c.bank = {k: ("vv", 0) for k in ("a", "b", "c", "e", "f", "d", "nb")}
    c.bias_read = 0
    c.kpair_ss = 0
    c.qpair_ss = 0

    build_chain(c, blocks)

    for ch in range(NCH):
        pr.on("sync", lambda e, ch=ch: e.dma_start(
            out=c.y_out[ch * P:(ch + 1) * P, :], in_=c.XD[ch]),
            waits=[("ph", pr.now("ph")), ("vv", pr.now("vv")),
                   ("ss", pr.now("ss")), ("tt", pr.now("tt"))],
            inc="so", inc_n=16)
    pr.on("sync", lambda e: e.nop(), waits=[("so", pr.now("so"))])

    pr.check()
    print(f"arena peaks: ab={c.ab.peak}/{c.ab.words} af={c.af.peak}/{c.af.words}",
          flush=True)
    nc._ctx = c

    with nc.Block() as block:
        for ename in ENGINES:
            def mk(ename):
                def body(eng):
                    for fn in pr.q[ename]:
                        fn(eng)
                return body
            getattr(block, ename)(mk(ename))
    return nc


_NC_CACHE = {}


def kernel(**inputs):
    maps = stage_inputs(inputs)
    key = tuple(BLOCKS)
    if key not in _NC_CACHE:
        _NC_CACHE[key] = build_program(BLOCKS)
    nc = _NC_CACHE[key]
    res = run_bass_kernel_spmd(nc, maps, core_ids=list(range(8)))
    global _DBG_RES
    _DBG_RES = [np.asarray(r["dbg_out"], np.float32) for r in res.results] \
        if Ctx.DBG else []
    return np.stack([np.asarray(r["y_out"], np.float32).T for r in res.results])


_DBG_RES = []


# version 06 (final)
# speedup vs baseline: 1.1992x; 1.1992x over previous
import contextlib
import numpy as np
import ml_dtypes

import concourse.bass as bass
import concourse.mybir as mybir
from concourse.bass_utils import run_bass_kernel_spmd

F32 = mybir.dt.float32
BF16 = mybir.dt.bfloat16
I32 = mybir.dt.int32
AF = mybir.ActivationFunctionType
ALU = mybir.AluOpType
AX = mybir.AxisListType

P = 128
S = 2048
D = 1024
NCH = D // P
NT = S // P
QG = 512
NG = S // QG
NH = 16
HD = 64
HIDP = 2816
NEG = -1048576.0
EPS = 1e-5

ENGINES = ("sync", "tensor", "vector", "scalar", "gpsimd")
BLOCKS = ("L0A", "L0F", "L1A", "L1F")


class Prog:
    def __init__(self, nc, es):
        self.nc = nc
        self.es = es
        self.q = {e: [] for e in ENGINES}
        self.meta = {e: [] for e in ENGINES}
        self.sems = {}
        self.cnt = {}
        self.last_wait = {e: {} for e in ENGINES}

    def sem(self, name):
        self.sems[name] = self.es.enter_context(self.nc.semaphore(name))
        self.cnt[name] = 0

    def now(self, name):
        return self.cnt[name]

    def on(self, eng, builder, waits=(), inc=None, inc_n=1):
        own = {"vector": "vv", "scalar": "ss"}.get(eng)
        if own is not None and self.cnt[own]:
            waits = list(waits) + [(own, self.cnt[own])]
        ww = []
        lw = self.last_wait[eng]
        for s, v in waits:
            if v > 0 and lw.get(s, 0) < v:
                ww.append((s, v))
                lw[s] = v
        post = None
        if inc is not None:
            self.cnt[inc] += inc_n
            post = self.cnt[inc]
        sems = self.sems

        def run(engine, ww=tuple(ww), builder=builder, inc=inc, inc_n=inc_n):
            for s, v in ww:
                engine.wait_ge(sems[s], v)
            try:
                ins = builder(engine)
            except Exception as ex:
                co = getattr(builder, "__code__", None)
                raise RuntimeError(
                    f"emit at {co.co_filename}:{co.co_firstlineno}: {ex}") from ex
            if inc is not None:
                ins.then_inc(sems[inc], inc_n)
        self.q[eng].append(run)
        self.meta[eng].append((tuple(ww), ((inc, inc_n),) if inc else ()))
        return post

    def fence(self, eng, sem):
        self.on(eng, lambda e: e.nop(), waits=[(sem, self.now(sem))])

    def barrier(self):
        for e in ENGINES:
            self.on(e, lambda eng: eng.nop(), inc="ph")
        v1 = self.now("ph")
        for e in ENGINES:
            self.on(e, lambda eng: eng.nop(), waits=[("ph", v1)])
        return v1

    def check(self):
        final = dict(self.cnt)
        for e in ENGINES:
            for ww, _ in self.meta[e]:
                for s, v in ww:
                    assert v <= final[s], \
                        f"wait {s}>={v} unreachable (final {final[s]}) on {e}"
        pos = {e: 0 for e in ENGINES}
        val = {s: 0 for s in self.cnt}
        progress = True
        while progress:
            progress = False
            for e in ENGINES:
                while pos[e] < len(self.meta[e]):
                    ww, incs = self.meta[e][pos[e]]
                    if all(val[s] >= v for s, v in ww):
                        for s, n in incs:
                            val[s] += n
                        pos[e] += 1
                        progress = True
                    else:
                        break
        stuck = {e: (pos[e], len(self.meta[e])) for e in ENGINES
                 if pos[e] < len(self.meta[e])}
        if stuck:
            detail = {e: self.meta[e][pos[e]][0] for e in stuck}
            raise RuntimeError(f"deadlock: {stuck}; waits {detail}")


class Arena:
    def __init__(self, nc, es, name, dtype, words):
        self.t = es.enter_context(nc.sbuf_tensor(name, [P, words], dtype))
        self.words = words
        self.off = 0
        self.peak = 0
        self.marks = []

    def take(self, n, tag=""):
        n = (n + 3) & ~3
        assert self.off + n <= self.words, \
            f"arena overflow {tag}: {self.off}+{n}>{self.words}"
        v = self.t[:, self.off:self.off + n]
        self.map = getattr(self, "map", {})
        self.map[tag] = (self.off, n)
        self.off += n
        self.peak = max(self.peak, self.off)
        return v

    def push(self):
        self.marks.append(self.off)

    def pop(self):
        self.off = self.marks.pop()


def _rotary_tables():
    n, dh, window = S, HD, 512
    half = np.arange(0, dh, 2, dtype=np.float64)
    inv_freq = 1.0 / 10000.0 ** (half / dh)
    t = np.arange(n, dtype=np.float64)
    freqs = t[:, None] * inv_freq[None, :]
    freqs = np.concatenate([freqs, freqs], axis=-1)
    base = (half + 0.4 * dh) / (1.4 * dh)
    power = (t - n // 2) / float(window // 2)
    scale = base[None, :] ** power[:, None]
    scale = np.concatenate([scale, scale], axis=-1)
    cos, sin = np.cos(freqs), np.sin(freqs)
    sgn = np.concatenate([-np.ones(32), np.ones(32)])[None, :]
    qc = (cos * scale * 8.0).astype(np.float32)
    qs = (sin * scale * 8.0 * sgn).astype(np.float32)
    kc = (cos / scale).astype(np.float32)
    ks = (sin / scale * sgn).astype(np.float32)
    return qc, qs, kc, ks


def _consts():
    bf = ml_dtypes.bfloat16
    c = {}
    c["identity"] = np.eye(P, dtype=np.float32).astype(bf)
    c["ones_row"] = np.ones((1, P), dtype=np.float32).astype(bf)
    c["ones_row_f"] = np.ones((1, P), dtype=np.float32)
    c["ones_col"] = np.ones((P, 1), dtype=np.float32).astype(bf)
    c["idrow16"] = np.eye(16, dtype=np.float32)
    hm = np.zeros((P, NCH * NH), dtype=np.float32)
    for ch in range(NCH):
        hm[0:64, ch * NH + 2 * ch] = 1.0
        hm[64:128, ch * NH + 2 * ch + 1] = 1.0
    c["headmask16"] = hm.astype(bf)
    ki = np.arange(P)[:, None]
    qi = np.arange(P)[None, :]
    c["mask_m4"] = (np.where(ki < qi, NEG, 0.0)).astype(bf)
    c["mask_m0"] = (np.where(ki > qi, NEG, 0.0)).astype(bf)
    st = np.zeros((P, NT * 32), dtype=np.float32)
    for kt in range(NT):
        kabs = kt * P + np.arange(P)[:, None]
        j = np.arange(16)[None, :]
        st[:, kt * 32:kt * 32 + 16] = (kabs <= 128 * j - 128)
        st[:, kt * 32 + 16:kt * 32 + 32] = (kabs >= 128 * j + 128)
    c["lr_step"] = st.astype(bf)
    sel = np.zeros((32, S), dtype=np.float32)
    for j in range(NT):
        sel[j, j * P:(j + 1) * P] = 1.0
        sel[16 + j, j * P:(j + 1) * P] = 1.0
    c["selmask"] = sel.astype(bf)
    e2 = np.zeros((2, 32), dtype=np.float32)
    e2[0, 0:16] = 1.0
    e2[1, 16:32] = 1.0
    c["edge2"] = e2.astype(bf)
    c["c2p65"] = np.full((16, 1), 2.0 ** 65, np.float32)
    c["ones_2col"] = np.ones((P, 2), np.float32).astype(bf)
    c["identity_f"] = np.eye(P, dtype=np.float32)
    tr_ = (np.arange(P)[None, :] < np.arange(P)[:, None]).astype(np.float32)
    c["tri"] = tr_.astype(bf)
    c["trin"] = (-tr_).astype(bf)
    ps2 = np.zeros((2, P), np.float32)
    ps2[0, 0:64] = 1.0
    ps2[1, 64:128] = 1.0
    c["pairsel2"] = ps2.astype(bf)
    perm = np.zeros((P, P), np.float32)
    for blk in range(2):
        for d in range(32):
            perm[blk * 64 + d + 32, blk * 64 + d] = 1.0
            perm[blk * 64 + d, blk * 64 + d + 32] = 1.0
    c["rotperm"] = perm.astype(bf)
    qc, qs, kc, ks = _rotary_tables()
    for nme, tab in (("fm_qc", qc), ("fm_qs", qs), ("fm_kc", kc), ("fm_ks", ks)):
        fm = np.tile(tab.T, (2, 1))
        c[nme] = np.ascontiguousarray(fm).astype(bf)
    qc, qs, kc, ks = _rotary_tables()
    c["rot_qc"], c["rot_qs"], c["rot_kc"], c["rot_ks"] = qc, qs, kc, ks
    return c


def _cols(v, pad_to=None):
    v = np.asarray(v, dtype=np.float32).ravel()
    if pad_to is not None and v.size < pad_to:
        v = np.concatenate([v, np.zeros(pad_to - v.size, np.float32)])
    return np.ascontiguousarray(v.reshape(v.size // P, P).T)


def stage_inputs(inputs):
    f = lambda a: np.ascontiguousarray(np.asarray(a, dtype=np.float32))
    cm = {}
    cm["l0a_wqkv"] = np.ascontiguousarray(f(inputs["L0_attn_qkv_w"]).T)
    cm["l0a_bqkv_c"] = _cols(inputs["L0_attn_qkv_b"])
    cm["l0a_wout"] = np.ascontiguousarray(f(inputs["L0_attn_out_w"]).T)
    cm["l0a_bout_c"] = _cols(inputs["L0_attn_out_b"])
    w1 = f(inputs["L0_ffn_w1"])
    w2 = f(inputs["L0_ffn_w2"])
    b1 = f(inputs["L0_ffn_b1"])
    HID = 2730
    for l in range(2):
        cm[f"l0f{l}_wqkv"] = f(inputs["L0_ffn_wqkv"][l])
        cm[f"l0f{l}_wo"] = f(inputs["L0_ffn_wo"][l])
        w1a = np.zeros((D, HIDP), np.float32); w1a[:, :HID] = w1[l][:, :HID]
        w1g = np.zeros((D, HIDP), np.float32); w1g[:, :HID] = w1[l][:, HID:]
        cm[f"l0f{l}_w1a"] = w1a
        cm[f"l0f{l}_w1g"] = w1g
        w2p = np.zeros((HIDP, D), np.float32); w2p[:HID] = w2[l]
        cm[f"l0f{l}_w2"] = w2p
        cm[f"l0f{l}_b1a_c"] = _cols(b1[l][:HID], HIDP)
        cm[f"l0f{l}_b1g_c"] = _cols(b1[l][HID:], HIDP)
        cm[f"l0f{l}_b2_c"] = _cols(inputs["L0_ffn_b2"][l])
        cm[f"l0f{l}_lng_c"] = _cols(inputs["L0_ffn_ln_g"][l])
        cm[f"l0f{l}_lnb_c"] = _cols(inputs["L0_ffn_ln_b"][l])
        cm[f"l0f{l}_qscale"] = f(inputs["L0_ffn_q_scale"][l]).reshape(1, HD)
        cm[f"l0f{l}_kscale"] = f(inputs["L0_ffn_k_scale"][l]).reshape(1, HD)
        cm[f"l0f{l}_qscale_c"] = np.ascontiguousarray(
            np.tile(f(inputs["L0_ffn_q_scale"][l]).reshape(1, HD), (1, 2)).reshape(P, 1))
        cm[f"l0f{l}_kscale_c"] = np.ascontiguousarray(
            np.tile(f(inputs["L0_ffn_k_scale"][l]).reshape(1, HD), (1, 2)).reshape(P, 1))
    cm["l1a_wq"] = np.ascontiguousarray(f(inputs["L1_attn_Wq"]).T)
    cm["l1a_wk"] = np.ascontiguousarray(f(inputs["L1_attn_Wk"]).T)
    cm["l1a_wv"] = np.ascontiguousarray(f(inputs["L1_attn_Wv"]).T)
    cm["l1a_wo"] = np.ascontiguousarray(f(inputs["L1_attn_Wo"]).T)
    cm["l1a_bq_c"] = _cols(inputs["L1_attn_bq"])
    cm["l1a_bk_c"] = _cols(inputs["L1_attn_bk"])
    cm["l1a_bv_c"] = _cols(inputs["L1_attn_bv"])
    cm["l1a_bo_c"] = _cols(inputs["L1_attn_bo"])
    _rkT = np.ascontiguousarray(f(inputs["L1_attn_rel_k_emb"]).T)
    cm["l1a_relkT"] = np.concatenate([_rkT, _rkT], axis=0)
    rv = f(inputs["L1_attn_rel_v_emb"])
    rvm = np.zeros((256, HD), np.float32)
    rvm[:255] = rv[1:256][::-1]
    cm["l1a_relv_mid"] = np.concatenate([rvm[0:128], rvm[128:256]], axis=1)
    cm["l1a_relv_edge"] = np.ascontiguousarray(np.stack([rv[256], rv[0]]))
    cm["l1a_rv0"] = np.ascontiguousarray(rv[0:1, :])
    cm["l1f_wqkv"] = f(inputs["L1_ffn_W_qkv"])
    cm["l1f_wproj"] = f(inputs["L1_ffn_W_proj"])
    cm["l1f_bproj_c"] = _cols(inputs["L1_ffn_b_proj"])
    cm["l1f_qg"] = np.ascontiguousarray(f(inputs["L1_ffn_g_info"][0, 0]).reshape(NH, HD))
    cm.update(_consts())
    x = np.asarray(inputs["L0_attn_x"], dtype=np.float32)
    mixes = []
    for b in range(8):
        mx = np.zeros((8, P, P), np.float32)
        kA, hA = b // 2, b % 2
        for d in range(HD):
            mx[kA, hA * HD + d, d] = 1.0
            mx[4 + kA, hA * HD + d, HD + d] = 1.0
        mixes.append(np.ascontiguousarray(mx.reshape(8 * P, P)))
    return [dict(cm, x_in=np.ascontiguousarray(x[b].T), l1a_mix=mixes[b])
            for b in range(8)]


class Ctx:
    DBG = False


def dbg(c, ap, waits, slot, n, cast=False):
    if not getattr(c, "DBG", False):
        return
    q = "gpsimd" if cast else "sync"
    c.pr.on(q, lambda e: e.dma_start(
        out=c.dbg_out[0:ap.shape[0], slot:slot + n], in_=ap,
        allow_slow_non_contiguous=True),
        waits=waits, inc="so", inc_n=16)


def bw(c, *keys):
    out = []
    for k in keys:
        sem, cnt = c.bank[k]
        if cnt:
            out.append((sem, cnt))
    return out


def br(c, key, sem, cnt):
    c.bank[key] = (sem, cnt)


def mmv(c, ps_ap, lhsT, rhs, start, stop, waits=(), inc=None, tp=None):
    def b(eng, ps_ap=ps_ap, lhsT=lhsT, rhs=rhs, start=start, stop=stop, tp=tp):
        return eng.matmul(ps_ap, lhsT, rhs, start=start, stop=stop,
                          tile_position=tp)
    return c.pr.on("tensor", b, waits=waits, inc=inc)


def ins(c, eng, opname, *args, waits=(), inc=None, inc_n=1, **kw):
    def b(e, opname=opname, args=args, kw=kw):
        return getattr(e, opname)(*args, **kw)
    return c.pr.on(eng, b, waits=waits, inc=inc, inc_n=inc_n)


def vin(c, opname, *args, waits=(), inc="vv", **kw):
    return ins(c, "vector", opname, *args, waits=waits, inc=inc, **kw)


def sin(c, opname, *args, waits=(), inc="ss", **kw):
    return ins(c, "scalar", opname, *args, waits=waits, inc=inc, **kw)


def dma(c, qname, out, in_, waits=(), inc=None, **kw):
    return ins(c, qname, "dma_start", waits=waits, inc=inc, inc_n=16,
               out=out, in_=in_, **kw)


def load_w(c, dst, src_ap, waits=()):
    return c.pr.on("gpsimd", lambda e: e.dma_start(out=dst, in_=src_ap),
                   waits=waits, inc="wd", inc_n=16)


def vop(c, builder, waits=(), inc="vv"):
    return c.pr.on("vector", builder, waits=waits, inc=inc)


def sop(c, builder, waits=(), inc="ss"):
    return c.pr.on("scalar", builder, waits=waits, inc=inc)


def bc_row16(c, row_b16, n, out_b16, waits=()):
    ps = c.ps_f[:, 0:n]
    t = mmv(c, ps, c.cst["ones_row"][0:1, :], row_b16, True, True,
            waits=list(waits) + bw(c, "f"), inc="tt")
    cnt = vin(c, "tensor_copy", out_b16, ps, waits=[("tt", t)])
    br(c, "f", "vv", cnt)
    return cnt


def bc_rowf(c, row_f32, n, out_f32, waits=()):
    ps = c.ps_f[:, 0:n]
    t = mmv(c, ps, c.cst["ones_row_f"][0:1, :], row_f32, True, True,
            waits=list(waits) + bw(c, "f"), inc="tt")
    cnt = vin(c, "tensor_copy", out_f32, ps, waits=[("tt", t)])
    br(c, "f", "vv", cnt)
    return cnt


def ln_rows(c, xb_of, rows, waits, n=QG):
    onescol = c.cst["ones_col"]
    sq2 = c.lnsq
    mcnt = qcnt = None
    prev = [0, 0]
    for ch in range(NCH):
        w = list(waits) + (bw(c, "a") if ch == 0 else [])
        mcnt = mmv(c, c.ps_a[0:1, 0:n], onescol[:], xb_of(ch),
                   start=(ch == 0), stop=(ch == NCH - 1), waits=w, inc="tt")
        if ch == 0:
            br(c, "a", "tt", mcnt)
        scnt = sin(c, "activation", sq2[ch % 2][:, 0:n], xb_of(ch), AF.Square,
                   waits=w + ([("tt", prev[ch % 2])] if prev[ch % 2] else []))
        qcnt = mmv(c, c.ps_a[0:1, QG:QG + n], onescol[:], sq2[ch % 2][:, 0:n],
                   start=(ch == 0), stop=(ch == NCH - 1),
                   waits=[("ss", scnt)], inc="tt")
        prev[ch % 2] = qcnt
    mu, rstd, tmp = rows["mu"], rows["rstd"], rows["tmp"]
    vin(c, "tensor_scalar", mu, c.ps_a[0:1, 0:n], 1.0 / D, None, ALU.mult,
        waits=[("tt", mcnt)])
    vin(c, "tensor_tensor", tmp, mu, mu, ALU.mult)
    vin(c, "scalar_tensor_tensor", tmp, c.ps_a[0:1, QG:QG + n], 1.0 / D, tmp,
        ALU.mult, ALU.subtract, waits=[("tt", qcnt)])
    v1 = vin(c, "tensor_scalar", tmp, tmp, EPS, None, ALU.add)
    s1 = sin(c, "activation", tmp, tmp, AF.Sqrt, waits=[("vv", v1)])
    cnt = vin(c, "reciprocal", rstd, tmp, waits=[("ss", s1)])
    br(c, "a", "tt", qcnt)
    br(c, "a", "vv", cnt)
    return cnt


def pow2_upper_sqrt(c, col, nrows, waits):
    ci = col.bitcast(I32)
    vin(c, "tensor_scalar", ci, ci, 1, None, ALU.logical_shift_right,
        waits=waits)
    return vin(c, "tensor_tensor", col, col, c.cst["c2p65"][0:nrows, 0:1],
               ALU.mult)


def neg_bias(c, q2, k2, scale, bias_out, nh, waits):
    t = c.nbcol[0:nh, 0:1]
    v0 = vin(c, "tensor_tensor", t, q2, k2, ALU.mult,
             waits=list(waits) + [("tt", c.bank["nb"][1])])
    s0 = sin(c, "activation", t, t, AF.Ln, waits=[("vv", v0)])
    s1 = sin(c, "activation", t, t, AF.Exp, scale=0.5)
    v1 = vin(c, "tensor_scalar", t, t, -scale * 1.01, None, ALU.mult,
             waits=[("ss", s1)])
    rowp = c.ps_f[0:1, 0:nh]
    t1 = mmv(c, rowp, t, c.cst["idrow16"][0:nh, 0:nh], True, True,
             waits=[("vv", v1)] + bw(c, "f"), inc="tt")
    row = c.nbrow[0:1, 0:nh]
    v2 = vin(c, "tensor_copy", row, rowp, waits=[("tt", t1)])
    ps = c.ps_f[:, 64:64 + nh]
    t2 = mmv(c, ps, c.cst["ones_row_f"][0:1, :], row, True, True,
             waits=[("vv", v2)], inc="tt")
    c.bank["nb"] = ("tt", t2)
    cnt = vin(c, "tensor_copy", bias_out, ps, waits=[("tt", t2)])
    br(c, "f", "vv", cnt)
    return cnt


def pre_ln(c, xln, waits):
    ab, af = c.ab, c.af
    af.push(); ab.push()
    rows = {k: af.take(512, k)[0:1, :] for k in ("mu", "rstd", "tmp")}
    mubc = af.take(QG, "mubc")
    rstdbc = af.take(QG, "rstdbc")
    tbuf = af.take(QG, "t")
    xb2 = [ab.take(QG, "xb0"), ab.take(QG, "xb1")]
    last = 0
    last_q = [0, 0]
    last_c = [0, 0]
    last_s = [0, 0]
    for g in range(NG):
        gsl = slice(g * QG, (g + 1) * QG)
        onescol = c.cst["ones_col"]
        sq2 = c.lnsq
        mcnt = qcnt = None
        for ch in range(NCH):
            cnt = vin(c, "tensor_copy", xb2[ch % 2][:], c.XD[ch][:, gsl],
                      waits=list(waits) +
                      ([("tt", last_q[ch % 2])] if last_q[ch % 2] else []) +
                      ([("vv", last_c[ch % 2])] if last_c[ch % 2] else []))
            last_c[ch % 2] = cnt
            w = [("vv", cnt)] + (bw(c, "a") if ch == 0 and g == 0 else [])
            mcnt = mmv(c, c.ps_a[0:1, 0:QG], onescol[:], xb2[ch % 2][:],
                       start=(ch == 0), stop=(ch == NCH - 1), waits=w, inc="tt")
            if ch == 0 and g == 0:
                br(c, "a", "tt", mcnt)
            scnt = sin(c, "activation", sq2[ch % 2][:, 0:QG], xb2[ch % 2][:],
                       AF.Square,
                       waits=w + ([("tt", last_q[ch % 2])] if last_q[ch % 2] else [])
                       + ([("ss", last_s[ch % 2])] if last_s[ch % 2] else []))
            last_s[ch % 2] = scnt
            qcnt = mmv(c, c.ps_a[0:1, QG:QG + QG], onescol[:], sq2[ch % 2][:, 0:QG],
                       start=(ch == 0), stop=(ch == NCH - 1),
                       waits=[("ss", scnt)], inc="tt")
            last_q[ch % 2] = qcnt
        mu, rstd, tmp = rows["mu"], rows["rstd"], rows["tmp"]
        vin(c, "tensor_scalar", mu, c.ps_a[0:1, 0:QG], 1.0 / D, None, ALU.mult,
            waits=[("tt", mcnt)])
        vin(c, "tensor_tensor", tmp, mu, mu, ALU.mult)
        vin(c, "scalar_tensor_tensor", tmp, c.ps_a[0:1, QG:2 * QG], 1.0 / D,
            tmp, ALU.mult, ALU.subtract, waits=[("tt", qcnt)])
        v1 = vin(c, "tensor_scalar", tmp, tmp, EPS, None, ALU.add)
        s1 = sin(c, "activation", tmp, tmp, AF.Sqrt, waits=[("vv", v1)])
        vcn = vin(c, "reciprocal", rstd, tmp, waits=[("ss", s1)])
        br(c, "a", "vv", vcn)
        b1 = bc_rowf(c, mu, QG, mubc, waits=[("vv", vcn)])
        b2 = bc_rowf(c, rstd, QG, rstdbc, waits=[("vv", vcn)])
        if g == 0:
            dbg(c, c.XD[0][:, 0:512], list(waits), 4608, 512)
            dbg(c, mubc[:, 0:512], [("vv", b1)], 5120, 512)
            dbg(c, rstdbc[:, 0:512], [("vv", b2)], 5632, 512)
        for ch in range(NCH):
            vin(c, "tensor_tensor", tbuf, c.XD[ch][:, gsl], mubc, ALU.subtract,
                waits=[("vv", b2)])
            last = vin(c, "tensor_tensor", xln[ch][:, gsl], tbuf, rstdbc,
                       ALU.mult)
    af.pop(); ab.pop()
    return last


def post_ln(c, yb_g, g, waits):
    ab, af = c.ab, c.af
    af.push(); ab.push()
    rows = {k: af.take(512, k)[0:1, :] for k in ("mu", "rstd", "tmp")}
    mubc16 = ab.take(QG, "mubc16")
    mur16 = ab.take(512, "mur16")[0:1, :]
    rstdbc = af.take(QG, "rstdbc")
    tbuf = af.take(QG, "t")
    gsl = slice(g * QG, (g + 1) * QG)
    lcnt = ln_rows(c, lambda ch: yb_g[ch][:], rows, list(waits))
    m16 = vin(c, "tensor_copy", mur16, rows["mu"], waits=[("vv", lcnt)])
    b1 = bc_row16(c, mur16, QG, mubc16, waits=[("vv", m16)])
    b2 = bc_rowf(c, rows["rstd"], QG, rstdbc, waits=[("vv", lcnt)])
    last = 0
    for ch in range(NCH):
        vin(c, "tensor_tensor", tbuf, yb_g[ch][:], mubc16, ALU.subtract,
            waits=[("vv", b2)])
        vin(c, "tensor_tensor", tbuf, tbuf, rstdbc, ALU.mult)
        last = vin(c, "tensor_tensor", c.XD[ch][:, gsl], c.XD[ch][:, gsl],
                   tbuf, ALU.add)
    af.pop(); ab.pop()
    return last


def emit_l0a(c, xln, xln_ready):
    pr, ab, af = c.pr, c.ab, c.af
    ab.push(); af.push()
    WQKV = c.dram["l0a_wqkv"]
    ofm = [ab.take(S, f"ofm{i}") for i in range(NCH)]
    wout = [ab.take(D, f"wo{i}") for i in range(NCH)]
    wout_cnt = max(load_w(c, wout[di], c.dram["l0a_wout"][di * P:(di + 1) * P, :])
                   for di in range(NCH))
    pr.fence("gpsimd", "wd")
    ab.push()
    kpair = ab.take(S, "kpair")
    vfm = ab.take(S, "vfm")
    v65 = ab.take(NT * 130, "v65")
    v65t = [v65[:, kt * 130:kt * 130 + 130] for kt in range(NT)]
    v65h = v65.rearrange("p (t h e) -> p t h e", h=2, e=65)
    qpair = ab.take(QG, "qpair")
    exps = [ab.take(1024, "exps0"), ab.take(1024, "exps1")]
    wpair = [ab.take(384, f"wp{di}") for di in range(NCH)]
    sqs = [ab.take(QG, "sqs0"), ab.take(QG, "sqs1")]
    stage64 = ab.take(QG, "stage64")
    rr16 = ab.take(512, "rr16")[0:1, :]
    bcq = ab.take(QG, "bcq")
    rr = af.take(QG, "rr")[0:1, :]
    bias2 = af.take(4, "bias2")
    k2col = af.take(4, "k2col")[0:2, 0:1]
    q2col = af.take(4, "q2col")[0:2, 0:1]
    odd_dma = 0
    v65_read = 0
    av_done = [0, 0]
    exp_done = [0, 0]
    kv_evac = 0

    wpair_read = 0
    for pair in range(NH // 2):
        h0 = 2 * pair
        wcnts = []
        for di in range(NCH):
            src = WQKV[di * P:(di + 1) * P, :]
            for slot, base in ((0, h0 * HD), (1, D + h0 * HD), (2, 2 * D + h0 * HD)):
                wcnts.append(load_w(c, wpair[di][:, slot * 128:slot * 128 + 128],
                                    src[:, base:base + 128],
                                    waits=[("tt", wpair_read)] if wpair_read else []))
        pr.fence("gpsimd", "wd")
        wready = ("wd", max(wcnts))
        kev = []
        for g in range(NG):
            gsl = slice(g * QG, (g + 1) * QG)
            reg = [c.ps_a, c.ps_b][g % 2]
            ps = reg[:, 0:QG]
            w0 = [wready] + xln_ready + [("ss", exp_done[g % 2])] + \
                ([("vv", kev[g - 2])] if g >= 2 else [])
            cnt = None
            for di in range(NCH):
                cnt = mmv(c, ps, wpair[di][:, 128:256], xln[di][:, gsl],
                          start=(di == 0), stop=(di == NCH - 1),
                          waits=w0 if di == 0 else [],
                          inc="tt" if di == NCH - 1 else None)
            kev.append(vin(c, "tensor_scalar", kpair[:, gsl], ps,
                           c.cst["l0a_bqkv_c"][:, 8 + pair:9 + pair], None,
                           ALU.add, waits=[("tt", cnt), ("ss", c.kpair_ss)]))
        nrm_mm = 0
        for g in range(NG):
            gsl = slice(g * QG, (g + 1) * QG)
            scnt = sin(c, "activation", sqs[g % 2][:], kpair[:, gsl], AF.Square,
                       waits=[("vv", kev[g])] + ([("tt", nrm_mm)] if g >= 2 else []))
            c.kpair_ss = scnt
            nrm_mm = mmv(c, c.ps_e[0:2, 0:QG], c.cst["headmask16"][:, 0:2],
                         sqs[g % 2][:], start=(g == 0), stop=(g == NG - 1),
                         waits=[("ss", scnt)] + (bw(c, "e") if g == 0 else []),
                         inc="tt")
            if g == 0:
                br(c, "e", "tt", nrm_mm)
        kn = vin(c, "reduce_max", k2col, c.ps_e[0:2, 0:QG], axis=AX.X,
                 waits=[("tt", nrm_mm)])
        br(c, "e", "vv", kn)
        vev = []
        for g in range(NG):
            gsl = slice(g * QG, (g + 1) * QG)
            reg = [c.ps_a, c.ps_b][g % 2]
            ps = reg[:, QG:2 * QG]
            w0 = [wready] + xln_ready + [("ss", exp_done[g % 2])] + \
                ([("vv", vev[g - 2])] if g >= 2 else [])
            cnt = None
            for di in range(NCH):
                cnt = mmv(c, ps, wpair[di][:, 256:384], xln[di][:, gsl],
                          start=(di == 0), stop=(di == NCH - 1),
                          waits=w0 if di == 0 else [],
                          inc="tt" if di == NCH - 1 else None)
            vev.append(vin(c, "tensor_scalar", vfm[:, gsl], ps,
                           c.cst["l0a_bqkv_c"][:, 16 + pair:17 + pair], None,
                           ALU.add, waits=[("tt", cnt)]))
        vone = vin(c, "memset", v65h[:, :, :, 64:65], 1.0,
                   waits=[("tt", v65_read)])
        vtr = []
        for kt in range(NT):
            tcnt = ins(c, "tensor", "transpose", c.ps_d[:, 0:P],
                       vfm[:, kt * P:(kt + 1) * P], c.cst["identity"][:],
                       waits=[("vv", vev[kt // 4]), ("tt", v65_read)] +
                             bw(c, "d") +
                             ([("vv", vtr[kt - 1])] if kt >= 1 else []),
                       inc="tt")
            vtr.append(vin(c, "tensor_copy", v65h[:, kt, :, 0:64],
                           c.ps_d[:, 0:P].rearrange("p (h e) -> p h e", e=64),
                           waits=[("tt", tcnt)]))
        br(c, "d", "vv", max(vtr))
        vready = [("vv", max(vtr + [vone]))]

        if pair == 0:
            dbg(c, xln[0][:, 0:512], xln_ready, 0, 512, cast=True)
            dbg(c, kpair[:, 0:512], [("vv", kev[0])], 512, 512, cast=True)
            dbg(c, vfm[:, 0:512], [("vv", vev[0])], 1024, 512, cast=True)
        for g in range(NG):
            gsl = slice(g * QG, (g + 1) * QG)
            psq = c.ps_c[:, 0:QG]
            w0 = [wready] + xln_ready + bw(c, "c")
            cnt = None
            for di in range(NCH):
                cnt = mmv(c, psq, wpair[di][:, 0:128], xln[di][:, gsl],
                          start=(di == 0), stop=(di == NCH - 1),
                          waits=w0 if di == 0 else [],
                          inc="tt" if di == NCH - 1 else None)
            qev = vin(c, "tensor_scalar", qpair[:], psq,
                      c.cst["l0a_bqkv_c"][:, pair:pair + 1], None, ALU.add,
                      waits=[("tt", cnt), ("ss", c.qpair_ss)])
            br(c, "c", "vv", qev)
            scnt = sin(c, "activation", sqs[0][:], qpair[:], AF.Square,
                       waits=[("vv", qev), ("tt", nrm_mm)])
            c.qpair_ss = scnt
            qmm = mmv(c, c.ps_e[0:2, 0:QG], c.cst["headmask16"][:, 0:2],
                      sqs[0][:], True, True,
                      waits=[("ss", scnt)] + bw(c, "e"), inc="tt")
            qn = vin(c, "reduce_max", q2col, c.ps_e[0:2, 0:QG], axis=AX.X,
                     waits=[("tt", qmm)])
            br(c, "e", "vv", qn)
            bias_cnt = neg_bias(c, q2col, k2col, 0.125, bias2[:, 0:2], 2,
                                waits=[("vv", qn), ("vv", kn),
                                       ("ss", c.bias_read)])
            if pair == 0 and g == 0:
                dbg(c, qpair[:, 0:512], [("vv", qev)], 1536, 512, cast=True)
                dbg(c, bias2[:, 0:2], [("vv", bias_cnt)], 2048, 2)
                dbg(c, q2col, [("vv", qn)], 2052, 1)
                dbg(c, k2col, [("vv", kn)], 2056, 1)
            for hh in range(2):
                h = h0 + hh
                hp = slice(hh * HD, (hh + 1) * HD)
                avps = [c.ps_c, c.ps_e][hh][0:65, 0:QG]
                avkey = "c" if hh == 0 else "e"
                for ri in range(8):
                    k0, k1 = 2 * ri, 2 * ri + 2
                    sreg = [c.ps_a, c.ps_b][ri % 2]
                    qkc = None
                    for kt in range(k0, k1):
                        j = kt - k0
                        qkc = mmv(c, sreg[:, j * QG:(j + 1) * QG],
                                  kpair[hp, kt * P:(kt + 1) * P],
                                  qpair[hp, :], True, True,
                                  waits=([("vv", qev),
                                          ("ss", exp_done[ri % 2])] +
                                         (bw(c, avkey) if ri == 0 else []) +
                                         (vready if ri == 0 else []))
                                  if kt == k0 else [],
                                  inc="tt" if kt == k1 - 1 else None)
                    ecnt = sin(c, "activation", exps[ri % 2][:],
                               sreg[:, 0:1024], AF.Exp,
                               bias=bias2[:, hh:hh + 1], scale=0.125,
                               waits=[("tt", qkc), ("vv", bias_cnt)])
                    c.bias_read = ecnt
                    exp_done[ri % 2] = ecnt
                    if pair == 0 and g == 0 and hh == 0 and ri == 0:
                        dbg(c, exps[0][:, 0:512], [("ss", ecnt)], 2560, 512,
                            cast=True)
                    for kt in range(k0, k1):
                        j = kt - k0
                        mmv(c, avps, v65t[kt][:, 65 * hh:65 * hh + 65],
                            exps[ri % 2][:, j * QG:(j + 1) * QG],
                            start=(kt == 0), stop=(kt == NT - 1),
                            waits=[("ss", ecnt)] +
                                  (bw(c, avkey) if kt == 0 else []),
                            inc="tt" if kt == NT - 1 else None)
                avc = pr.now("tt")
                v65_read = avc
                v1 = vin(c, "tensor_copy", rr, avps[64:65, :],
                         waits=[("tt", avc)])
                if pair == 0 and g == 0 and hh == 0:
                    dbg(c, rr[0:1, :], [("vv", v1)], 3072, 512)
                vin(c, "reciprocal", rr, rr)
                v3 = vin(c, "tensor_copy", rr16, rr)
                bcr = mmv(c, c.ps_f[0:64, 0:QG], c.cst["ones_row"][0:1, 0:64],
                          rr16, True, True,
                          waits=[("vv", v3)] + bw(c, "f"), inc="tt")
                v4 = vin(c, "tensor_copy", bcq[0:64, :], c.ps_f[0:64, 0:QG],
                         waits=[("tt", bcr)])
                br(c, "f", "vv", v4)
                if hh == 0:
                    av_done[0] = vin(c, "tensor_tensor", ofm[pair][0:64, gsl],
                                     avps[0:64, :], bcq[0:64, :], ALU.mult,
                                     waits=[("vv", v4), ("bd", odd_dma)])
                    br(c, "c", "vv", av_done[0])
                else:
                    st = vin(c, "tensor_tensor", stage64[0:64, :],
                             avps[0:64, :], bcq[0:64, :], ALU.mult,
                             waits=[("vv", v4), ("bd", odd_dma)])
                    av_done[1] = st
                    br(c, "e", "vv", st)
                    odd_dma = dma(c, "sync", ofm[pair][64:128, gsl],
                                  stage64[0:64, :], waits=[("vv", st)],
                                  inc="bd")
                    pr.fence("sync", "bd")
        wpair_read = pr.now("tt")
    c.bank["c"] = ("vv", av_done[0])
    c.bank["e"] = ("vv", av_done[1])
    ab.pop()
    yb = [ab.take(QG, f"yb{i}") for i in range(NCH)]

    first = [("wd", wout_cnt), ("bd", odd_dma),
             ("ss", exp_done[0]), ("ss", exp_done[1]),
             ("vv", max(av_done))]
    for g in range(NG):
        gsl = slice(g * QG, (g + 1) * QG)
        ev = []
        for oc in range(NCH):
            reg = [c.ps_a, c.ps_b][oc % 2]
            ps = reg[:, (oc // 2 % 2) * QG:(oc // 2 % 2) * QG + QG]
            cnt = None
            for di in range(NCH):
                cnt = mmv(c, ps, wout[di][:, oc * P:(oc + 1) * P],
                          ofm[di][:, gsl],
                          start=(di == 0), stop=(di == NCH - 1),
                          waits=(first + bw(c, "a", "b") +
                                 ([("vv", ev[oc - 4])] if oc >= 4 else []))
                          if di == 0 else [],
                          inc="tt" if di == NCH - 1 else None)
            ev.append(vin(c, "tensor_scalar", yb[oc][:], ps,
                          c.cst["l0a_bout_c"][:, oc:oc + 1], None, ALU.add,
                          waits=[("tt", cnt)]))
        if g == 0:
            dbg(c, ofm[0][:, 0:512], [("vv", max(ev))], 3584, 512, cast=True)
            dbg(c, yb[0][:], [("vv", max(ev))], 4096, 512, cast=True)
        first = [("vv", max(ev))]
        br(c, "a", "vv", max(ev))
        br(c, "b", "vv", max(ev))
        post_ln(c, yb, g, [("vv", max(ev))])
    ab.pop(); af.pop()



def emit_l0f(c, xi, xi_ready):
    pr, ab, af = c.pr, c.ab, c.af

    def xsh(di, g):
        off = 1 if di < 4 else 0
        return xi[di][:, off + g * QG: off + g * QG + QG]

    def xdat(di, g):
        return xi[di][:, 1 + g * QG: 1 + g * QG + QG]

    xw = list(xi_ready)
    c.l0f_wread = getattr(c, "l0f_wread", 0)
    c.l0f_bd = getattr(c, "l0f_bd", pr.now("bd"))
    for l in range(2):
        lw_ = f"l0f{l}_"
        ab.push(); af.push()
        ofm = [ab.take(S, f"f_ofm{i}") for i in range(4)]
        wo_res = [ab.take(D, f"f_wo{i}") for i in range(4)]
        v65 = ab.take(NT * 130, "f_v65")
        v65t = [v65[:, kt * 130:kt * 130 + 130] for kt in range(NT)]
        v65h = v65.rearrange("p (t h e) -> p t h e", h=2, e=65)
        wpair = [ab.take(384, f"f_wp{i}") for i in range(NCH)]
        qfm = ab.take(S, "f_qfm")
        kfm = ab.take(S, "f_kfm")
        qpfm = ab.take(S, "f_qpfm")
        kpfm = ab.take(S, "f_kpfm")
        vfm = ab.take(S, "f_vfm")
        tabc = ab.take(S, "f_tabc")
        tabs = ab.take(S, "f_tabs")
        rnb = ab.take(S, "f_rnb")[0:2, :]
        exps = ab.take(640, "f_exps")
        stage64 = ab.take(QG, "f_stage64")
        rr16 = ab.take(512, "f_rr16")[0:1, :]
        bcq = ab.take(QG, "f_bcq")
        sqs = [ab.take(QG, "f_sqs0"), ab.take(QG, "f_sqs1")]
        rn2 = af.take(2048, "f_rn2")[0:2, :]
        tmp1 = af.take(QG, "f_tmp1")
        tmp2 = af.take(QG, "f_tmp2")
        rsbc = af.take(QG, "f_rsbc")
        negb = af.take(4, "f_negb")[:, 0:1]
        rr = af.take(QG, "f_rr")[0:1, :]
        bq = af.take(4, "f_bq")[0:1, 0:1]
        bk = af.take(4, "f_bk")[0:1, 0:1]

        wo_cnt = max(load_w(c, wo_res[i], c.dram[lw_ + "wo"][i * P:(i + 1) * P, :],
                            waits=[("tt", c.l0f_wread)] if c.l0f_wread else [])
                     for i in range(4))
        pr.fence("gpsimd", "wd")
        vin(c, "reduce_max", bq, c.cst[lw_ + "qscale"][:], axis=AX.X,
            apply_absolute_value=True, waits=c.ready)
        vin(c, "reduce_max", bk, c.cst[lw_ + "kscale"][:], axis=AX.X,
            apply_absolute_value=True)
        v0 = vin(c, "tensor_tensor", bq, bq, bk, ALU.mult)
        t2b = mmv(c, c.ps_f[:, 0:1], c.cst["ones_row_f"][0:1, :], bq, True, True,
                  waits=[("vv", v0)] + bw(c, "f"), inc="tt")
        nbc = vin(c, "tensor_scalar", negb, c.ps_f[:, 0:1], -8.0 * 1.02, None,
                  ALU.mult, waits=[("tt", t2b)])
        br(c, "f", "vv", nbc)

        av_done = [0, 0]
        exp_done = 0
        exps_read = 0
        odd_dma = c.l0f_bd
        tab_read = 0
        v65_read = 0
        for pp in range(4):
            wcnts = []
            for di in range(NCH):
                src = c.dram[lw_ + "wqkv"][di * P:(di + 1) * P, :]
                for slot, base in ((0, pp * P), (1, 512 + pp * P),
                                   (2, 1024 + pp * P)):
                    wcnts.append(load_w(
                        c, wpair[di][:, slot * P:(slot + 1) * P],
                        src[:, base:base + P],
                        waits=[("tt", c.l0f_wread)] if c.l0f_wread else []))
            pr.fence("gpsimd", "wd")
            wready = ("wd", max(wcnts))
            evs = {}
            sq_read = 0
            for ti, (tgt, slot) in enumerate(((qfm, 0), (kfm, 1), (vfm, 2))):
                for g in range(NG):
                    gsl = slice(g * QG, (g + 1) * QG)
                    reg = [c.ps_a, c.ps_b][(ti * NG + g) % 2]
                    ps = reg[:, 0:QG]
                    w0 = [wready, ("ss", exp_done), ("ss", sq_read)] + xw +                         ([("vv", evs[ti * NG + g - 2])]
                         if ti * NG + g - 2 in evs else [])
                    cnt = None
                    for di in range(NCH):
                        cnt = mmv(c, ps, wpair[di][:, slot * P:(slot + 1) * P],
                                  xsh(di, g),
                                  start=(di == 0), stop=(di == NCH - 1),
                                  waits=w0 if di == 0 else [],
                                  inc="tt" if di == NCH - 1 else None)
                    evs[ti * NG + g] = vin(c, "tensor_copy", tgt[:, gsl], ps,
                                           waits=[("tt", cnt)])
            qkv_ev = pr.now("vv")
            vone = vin(c, "memset", v65h[:, :, :, 64:65], 1.0,
                       waits=[("tt", v65_read), ("vv", qkv_ev)])
            vtr = []
            for kt in range(NT):
                tcnt = ins(c, "tensor", "transpose", c.ps_d[:, 0:P],
                           vfm[:, kt * P:(kt + 1) * P], c.cst["identity"][:],
                           waits=[("vv", qkv_ev), ("tt", v65_read)] + bw(c, "d") +
                                 ([("vv", vtr[kt - 1])] if kt >= 1 else []),
                           inc="tt")
                vtr.append(vin(c, "tensor_copy", v65h[:, kt, :, 0:64],
                               c.ps_d[:, 0:P].rearrange("p (h e) -> p h e", e=64),
                               waits=[("tt", tcnt)]))
            br(c, "d", "vv", max(vtr))
            vready = ("vv", max(vtr + [vone]))
            rnb_read = 0
            for ti, src_ in enumerate((qfm, kfm)):
                nprev = 0
                for g in range(NG):
                    gsl = slice(g * QG, (g + 1) * QG)
                    scnt = sin(c, "activation", sqs[g % 2][:], src_[:, gsl],
                               AF.Square, waits=[("vv", qkv_ev)])
                    sq_read = scnt
                    nmm = mmv(c, c.ps_e[0:2, 0:QG], c.cst["headmask16"][:, 0:2],
                              sqs[g % 2][:], True, True,
                              waits=[("ss", scnt)] + bw(c, "e"), inc="tt")
                    br(c, "e", "tt", nmm)
                    lncnt = sin(c, "activation", rn2[:, gsl], c.ps_e[0:2, 0:QG],
                                AF.Ln, waits=[("tt", nmm)])
                    ecnt = sin(c, "activation", rn2[:, gsl], rn2[:, gsl],
                               AF.Exp, scale=-0.5)
                    br(c, "e", "ss", ecnt)
                    nprev = vin(c, "tensor_copy", rnb[:, gsl], rn2[:, gsl],
                                waits=[("ss", ecnt), ("tt", rnb_read)])
                tgt = (qpfm, kpfm)[ti]
                tc_ = load_w(c, tabc, c.dram[("fm_qc", "fm_kc")[ti]],
                             waits=[("vv", tab_read)] if tab_read else [])
                ts_ = load_w(c, tabs, c.dram[("fm_qs", "fm_ks")[ti]])
                pr.fence("gpsimd", "wd")
                scol = c.cst[lw_ + ("qscale_c", "kscale_c")[ti]]
                for g in range(NG):
                    gsl = slice(g * QG, (g + 1) * QG)
                    bmm = mmv(c, c.ps_f[:, 0:QG], c.cst["pairsel2"][:],
                              rnb[:, gsl], True, True,
                              waits=[("vv", nprev)] + bw(c, "f"), inc="tt")
                    rnb_read = bmm
                    rsb = vin(c, "tensor_scalar", rsbc, c.ps_f[:, 0:QG],
                              scol[:, 0:1], None, ALU.mult, waits=[("tt", bmm)])
                    br(c, "f", "vv", rsb)
                    rmm = mmv(c, c.ps_e[:, 0:QG], c.cst["rotperm"][:],
                              src_[:, gsl], True, True, waits=bw(c, "e"),
                              inc="tt")
                    u1 = vin(c, "tensor_tensor", tmp1, src_[:, gsl],
                             tabc[:, gsl], ALU.mult, waits=[("wd", ts_)])
                    u2 = vin(c, "tensor_tensor", tmp2, c.ps_e[:, 0:QG],
                             tabs[:, gsl], ALU.mult, waits=[("tt", rmm)])
                    br(c, "e", "vv", u2)
                    vin(c, "tensor_tensor", tmp1, tmp1, tmp2, ALU.add)
                    tab_read = vin(c, "tensor_tensor", tgt[:, gsl], tmp1, rsbc,
                                   ALU.mult)
            rot_done = pr.now("vv")
            for hh in range(2):
                hp = slice(hh * HD, (hh + 1) * HD)
                avp = [c.ps_c, c.ps_e][hh][0:65, 0:QG]
                avkey = ("c", "e")[hh]
                for w in range(4):
                    first_av = True
                    for qt in range(4 * w, 4 * w + 4):
                        kts = [k for k in range(qt - 4, qt + 1) if k >= 0]
                        reg = [c.ps_a, c.ps_b][qt % 2]
                        qkc = None
                        nk = len(kts)
                        nb0 = min(nk, 4)
                        for ki, kt in enumerate(kts):
                            st_ = (ki == 0 or ki == 4)
                            sp_ = (ki == nb0 - 1 or ki == nk - 1)
                            qkc = mmv(c, reg[:, ki * P:(ki + 1) * P],
                                      kpfm[hp, kt * P:(kt + 1) * P],
                                      qpfm[hp, qt * P:(qt + 1) * P],
                                      start=st_, stop=sp_,
                                      waits=([("vv", rot_done),
                                              ("ss", exp_done)] +
                                             (bw(c, avkey) if first_av else []))
                                      if ki == 0 else [],
                                      inc="tt" if ki == len(kts) - 1 else None)
                        mw = [("tt", qkc)]
                        for ki, kt in enumerate(kts):
                            msk = None
                            if kt == qt - 4:
                                msk = "mask_m4"
                            elif kt == qt:
                                msk = "mask_m0"
                            if msk:
                                mcnt = vin(c, "tensor_tensor",
                                           reg[:, ki * P:(ki + 1) * P],
                                           reg[:, ki * P:(ki + 1) * P],
                                           c.cst[msk][:], ALU.add, waits=mw)
                                mw = [("vv", mcnt)]
                        ecnt = sin(c, "activation", exps[:, 0:nk * P],
                                   reg[:, 0:nk * P], AF.Exp,
                                   bias=negb, scale=1.0,
                                   waits=mw + [("vv", nbc), ("tt", exps_read)])
                        exp_done = ecnt
                        for ki, kt in enumerate(kts):
                            avc = mmv(c, avp[:, (qt % 4) * P:(qt % 4 + 1) * P],
                                      v65t[kt][:, 65 * hh:65 * hh + 65],
                                      exps[:, ki * P:(ki + 1) * P],
                                      start=first_av,
                                      stop=(qt % 4 == 3 and ki == len(kts) - 1),
                                      waits=[("ss", ecnt), vready] +
                                            (bw(c, avkey) if first_av else []),
                                      inc="tt" if ki == len(kts) - 1 else None)
                            first_av = False
                        exps_read = avc
                    v1 = vin(c, "tensor_copy", rr, avp[64:65, :],
                             waits=[("tt", avc)])
                    vin(c, "reciprocal", rr, rr)
                    v3 = vin(c, "tensor_copy", rr16, rr)
                    bcr = mmv(c, c.ps_f[0:64, 0:QG], c.cst["ones_row"][0:1, 0:64],
                              rr16, True, True, waits=[("vv", v3)] + bw(c, "f"),
                              inc="tt")
                    v4 = vin(c, "tensor_copy", bcq[0:64, :], c.ps_f[0:64, 0:QG],
                             waits=[("tt", bcr)])
                    br(c, "f", "vv", v4)
                    wsl = slice(w * QG, (w + 1) * QG)
                    if hh == 0:
                        av_done[0] = vin(c, "tensor_tensor", ofm[pp][0:64, wsl],
                                         avp[0:64, :], bcq[0:64, :], ALU.mult,
                                         waits=[("vv", v4), ("bd", odd_dma)])
                        br(c, "c", "vv", av_done[0])
                    else:
                        st = vin(c, "tensor_tensor", stage64[0:64, :],
                                 avp[0:64, :], bcq[0:64, :], ALU.mult,
                                 waits=[("vv", v4), ("bd", odd_dma)])
                        av_done[1] = st
                        br(c, "e", "vv", st)
                        odd_dma = dma(c, "sync", ofm[pp][64:128, wsl],
                                      stage64[0:64, :], waits=[("vv", st)],
                                      inc="bd")
                        pr.fence("sync", "bd")
            v65_read = pr.now("tt")
        c.l0f_bd = odd_dma
        first = [("wd", wo_cnt), ("bd", odd_dma), ("ss", exp_done),
                 ("vv", max(av_done))]
        xev = []
        for g in range(NG):
            gsl = slice(g * QG, (g + 1) * QG)
            ev = []
            for oc in range(NCH):
                reg = [c.ps_a, c.ps_b][oc % 2]
                ps = reg[:, (oc // 2 % 2) * QG:(oc // 2 % 2) * QG + QG]
                cnt = None
                for di in range(4):
                    cnt = mmv(c, ps, wo_res[di][:, oc * P:(oc + 1) * P],
                              ofm[di][:, gsl],
                              start=(di == 0), stop=(di == 3),
                              waits=(first + ([("vv", ev[oc - 4])] if oc >= 4
                                              else [])) if di == 0 else [],
                              inc="tt" if di == 3 else None)
                ev.append(vin(c, "tensor_tensor", xdat(oc, g), ps, xdat(oc, g),
                              ALU.add, waits=[("tt", cnt)]))
            first = [("vv", max(ev))]
            br(c, "a", "vv", max(ev))
            br(c, "b", "vv", max(ev))
            xev.append(max(ev))
        xw = [("vv", max(xev))]
        c.l0f_wread = pr.now("tt")
        ab.pop(); af.pop()
        pr.barrier()

        ab.push(); af.push()
        xiln = [ab.take(S, f"f_xiln{i}") for i in range(NCH)]
        hbuf = [ab.take(QG, f"f_hb{j}") for j in range(2)]
        yacc = [ab.take(S, f"f_ya{i}") for i in range(NCH)]
        yaccv = [y.rearrange("p (g t) -> p g t", t=QG) for y in yacc]
        rows = {k: af.take(512, k)[0:1, :] for k in ("mu", "rstd", "tmp")}
        mubc16 = ab.take(QG, "f_mubc16")
        ha = mubc16
        mur16 = ab.take(512, "f_mur16")[0:1, :]
        rstdbc = af.take(QG, "f_rstdbc")
        tmp1 = af.take(QG, "f_tmp1b")
        xil_ev = []
        for g in range(NG):
            gsl = slice(g * QG, (g + 1) * QG)
            lcnt = ln_rows(c, lambda ch, g=g: xsh(ch, g), rows, xw)
            m16 = vin(c, "tensor_copy", mur16, rows["mu"], waits=[("vv", lcnt)])
            b1 = bc_row16(c, mur16, QG, mubc16, waits=[("vv", m16)])
            b2 = bc_rowf(c, rows["rstd"], QG, rstdbc, waits=[("vv", lcnt)])
            for ch in range(NCH):
                vin(c, "tensor_tensor", tmp1, xsh(ch, g), mubc16, ALU.subtract,
                    waits=[("vv", b2), ("vv", b1)])
                vin(c, "tensor_tensor", tmp1, tmp1, rstdbc, ALU.mult)
                xil_ev.append(vin(
                    c, "tensor_scalar", xiln[ch][:, gsl], tmp1,
                    c.cst[lw_ + "lng_c"][:, ch:ch + 1],
                    c.cst[lw_ + "lnb_c"][:, ch:ch + 1], ALU.mult, ALU.add))
        xil_ready = ("vv", max(xil_ev))
        JG = [(j0, min(2, 22 - j0)) for j0 in range(0, 22, 2)]
        w1a_s = [ab.take(256, f"f_w1a{i}") for i in range(NCH)]
        w1g_s = [ab.take(256, f"f_w1g{i}") for i in range(NCH)]
        w2_s = [ab.take(D, f"f_w2{i}") for i in range(2)]
        wread = 0
        ha_ev = 0
        hg_ss = 0
        yev = {}
        for gi, (j0, jn) in enumerate(JG):
            wcnts = []
            for di in range(NCH):
                wcnts.append(load_w(c, w1a_s[di][:, 0:jn * P],
                                    c.dram[lw_ + "w1a"][di * P:(di + 1) * P,
                                                        j0 * P:(j0 + jn) * P],
                                    waits=[("tt", wread)] if wread else []))
                wcnts.append(load_w(c, w1g_s[di][:, 0:jn * P],
                                    c.dram[lw_ + "w1g"][di * P:(di + 1) * P,
                                                        j0 * P:(j0 + jn) * P]))
            for j in range(jn):
                wcnts.append(load_w(c, w2_s[j],
                                    c.dram[lw_ + "w2"][(j0 + j) * P:(j0 + j + 1) * P, :]))
            pr.fence("gpsimd", "wd")
            wready = ("wd", max(wcnts))
            for g in range(NG):
                gsl = slice(g * QG, (g + 1) * QG)
                hev = []
                for j in range(jn):
                    jj = j0 + j
                    aps = c.ps_a[:, 0:QG]
                    gps = c.ps_a[:, QG:2 * QG]
                    w0 = [wready, xil_ready, ("vv", ha_ev), ("ss", hg_ss)] +                         bw(c, "a")
                    acnt = gcnt = None
                    for di in range(NCH):
                        acnt = mmv(c, aps, w1a_s[di][:, j * P:(j + 1) * P],
                                   xiln[di][:, gsl], start=(di == 0),
                                   stop=(di == NCH - 1),
                                   waits=w0 if di == 0 else [],
                                   inc="tt" if di == NCH - 1 else None)
                    for di in range(NCH):
                        gcnt = mmv(c, gps, w1g_s[di][:, j * P:(j + 1) * P],
                                   xiln[di][:, gsl], start=(di == 0),
                                   stop=(di == NCH - 1),
                                   waits=[] if di else [("ss", hg_ss)],
                                   inc="tt" if di == NCH - 1 else None)
                    ha_ev = vin(c, "tensor_scalar", ha, aps,
                                c.cst[lw_ + "b1a_c"][:, jj:jj + 1], None,
                                ALU.add, waits=[("tt", acnt)])
                    hg_ss = sin(c, "activation", hbuf[j][:], gps, AF.Gelu,
                                bias=c.cst[lw_ + "b1g_c"][:, jj:jj + 1],
                                waits=[("tt", gcnt)])
                    hev.append(vin(c, "tensor_tensor", hbuf[j][:], ha,
                                   hbuf[j][:], ALU.mult,
                                   waits=[("vv", ha_ev), ("ss", hg_ss)]))
                br(c, "a", "vv", max(hev))
                for oc in range(NCH):
                    ps = c.ps_b[:, (oc % 2) * QG:(oc % 2) * QG + QG]
                    cnt = None
                    slot_prev = c.w2slot.get(oc % 2, 0)
                    for j in range(jn):
                        cnt = mmv(c, ps, w2_s[j][:, oc * P:(oc + 1) * P],
                                  hbuf[j][:], start=(j == 0), stop=(j == jn - 1),
                                  waits=([("vv", max(hev))] +
                                         ([("vv", yev[(g, oc)])]
                                          if (g, oc) in yev else []) +
                                         ([("vv", slot_prev)] if slot_prev else []) +
                                         (bw(c, "b") if gi == 0 and oc < 2 else []))
                                  if j == 0 else [],
                                  inc="tt" if j == jn - 1 else None)
                    if gi == 0:
                        yev[(g, oc)] = vin(c, "tensor_copy", yaccv[oc][:, g, :],
                                           ps, waits=[("tt", cnt)])
                    else:
                        yev[(g, oc)] = vin(c, "tensor_tensor", yaccv[oc][:, g, :],
                                           ps, yaccv[oc][:, g, :], ALU.add,
                                           waits=[("tt", cnt)])
                    c.w2slot[oc % 2] = yev[(g, oc)]
            wread = pr.now("tt")
        xev = []
        for g in range(NG):
            for oc in range(NCH):
                xev.append(vin(
                    c, "scalar_tensor_tensor", xdat(oc, g), yaccv[oc][:, g, :],
                    c.cst[lw_ + "b2_c"][:, oc:oc + 1], xdat(oc, g),
                    ALU.add, ALU.add, waits=[("vv", yev[(g, oc)])]))
        xw = [("vv", max(xev))]
        c.l0f_wread = pr.now("tt")
        ab.pop(); af.pop()
        pr.barrier()
    return xw


def emit_l1f(c, xln, xln_ready):
    pr, ab, af = c.pr, c.ab, c.af
    ab.push(); af.push()
    WQKV = c.dram["l1f_wqkv"]
    ofm = [ab.take(S, f"n_ofm{i}") for i in range(NCH)]
    rowA = ab.take(S, "n_rowA")[0:1, :]
    rowB = ab.take(S, "n_rowB")[0:1, :]
    ab.push()
    kpair = ab.take(S, "n_kpair")
    vfm = ab.take(S, "n_vfm")
    v65p = ab.take(NT * 130, "n_v65p")
    v65n = ab.take(NT * 130, "n_v65n")
    v65pt = [v65p[:, kt * 130:kt * 130 + 130] for kt in range(NT)]
    v65nt = [v65n[:, kt * 130:kt * 130 + 130] for kt in range(NT)]
    v65ph = v65p.rearrange("p (t h e) -> p t h e", h=2, e=65)
    v65nh = v65n.rearrange("p (t h e) -> p t h e", h=2, e=65)
    qpair = ab.take(QG, "n_qpair")
    qfull = ab.take(S, "n_qfull")
    exps = [ab.take(1024, "n_exps0"), ab.take(1024, "n_exps1")]
    wpair = [ab.take(384, f"n_wp{i}") for i in range(NCH)]
    sqs = [ab.take(QG, "n_sqs0"), ab.take(QG, "n_sqs1")]
    stage64 = ab.take(QG, "n_st64")
    stageB = ab.take(QG, "n_stB")
    rr16 = ab.take(512, "n_rr16")[0:2, :]
    rr16b = ab.take(512, "n_rr16b")[0:1, :]
    bcq = ab.take(QG, "n_bcq")
    bcn = ab.take(QG, "n_bcn")
    qgbig = ab.take(16, "n_qgbig")
    gh16 = ab.take(64, "n_gh16")
    ghT = ab.take(16, "n_ghT")
    mposc = af.take(16, "n_mposc")
    mnegc = af.take(16, "n_mnegc")
    rr = af.take(QG, "n_rr")[0:2, :]
    rrb = af.take(QG, "n_rrb")[0:1, :]
    negb = af.take(4, "n_negb")[:, 0:1]
    gq2 = af.take(4, "n_gq2")[0:1, 0:1]
    k2p = af.take(4, "n_k2p")[0:1, 0:1]
    t11 = af.take(4, "n_t11")[0:1, 0:1]
    qgn = af.take(64, "n_qgn")
    tau = af.take(4, "n_tau")[0:1, 0:1]
    ccst = af.take(4, "n_cc")[0:1, 0:2]
    af.push()
    dot2 = af.take(QG, "n_dot2")[0:2, :]
    simrow = af.take(2048, "n_sim")[0:1, :]
    simcol = af.take(16, "n_simcol")

    qg = c.cst["l1f_qg"]
    v0 = vin(c, "tensor_tensor", qgn[0:16, 0:HD], qg[:], qg[:], ALU.mult,
             waits=c.ready)
    v1 = vin(c, "reduce_sum", qgn[0:16, 0:1], qgn[0:16, 0:HD], axis=AX.X)
    s0 = sin(c, "activation", qgn[0:16, 0:1], qgn[0:16, 0:1], AF.Ln,
             waits=[("vv", v1)])
    s1 = sin(c, "activation", qgn[0:16, 0:1], qgn[0:16, 0:1], AF.Exp,
             scale=-0.5)
    v3 = vin(c, "tensor_scalar", gh16[0:16, 0:HD], qg[:], qgn[0:16, 0:1],
             None, ALU.mult, waits=[("ss", s1)])
    tq = ins(c, "tensor", "transpose", c.ps_d[0:64, 0:16], gh16[0:16, 0:HD],
             c.cst["identity"][0:16, 0:16], waits=[("vv", v3)] + bw(c, "d"),
             inc="tt")
    v4 = vin(c, "tensor_copy", ghT[0:64, 0:16], c.ps_d[0:64, 0:16],
             waits=[("tt", tq)])
    br(c, "d", "vv", v4)
    qgz = vin(c, "memset", qgbig[:], 0.0)
    ghT2 = ghT[0:64, 0:16].rearrange("p (a b) -> p a b", b=2)
    qgb2lo = qgbig[0:64, :].rearrange("p (a b) -> p a b", b=2)
    qgb2hi = qgbig[64:128, :].rearrange("p (a b) -> p a b", b=2)
    ev_ = vin(c, "tensor_copy", qgb2lo[:, :, 0:1], ghT2[:, :, 0:1],
              waits=[("vv", v4), ("vv", qgz)])
    od_ = dma(c, "sync", qgb2hi[:, :, 1:2], ghT2[:, :, 1:2],
              waits=[("vv", ev_)], inc="bd", allow_slow_non_contiguous=True)
    pr.fence("sync", "bd")
    qg_ready = ("bd", od_)

    wpair_read = 0
    gq_ev = 0
    for pair in range(NH // 2):
        h0 = 2 * pair
        wcnts = [load_w(c, wpair[di][:, 0:128],
                        WQKV[di * P:(di + 1) * P, h0 * HD:h0 * HD + 128],
                        waits=[("tt", wpair_read)] if wpair_read else [])
                 for di in range(NCH)]
        pr.fence("gpsimd", "wd")
        wready = ("wd", max(wcnts))
        qev = []
        for g in range(NG):
            gsl = slice(g * QG, (g + 1) * QG)
            key = "a" if g % 2 == 0 else "b"
            ps = [c.ps_a, c.ps_b][g % 2][:, 0:QG]
            w0 = [wready] + xln_ready + bw(c, key) +                 ([("vv", qev[g - 2])] if g >= 2 else [])
            cnt = None
            for di in range(NCH):
                cnt = mmv(c, ps, wpair[di][:, 0:128], xln[di][:, gsl],
                          start=(di == 0), stop=(di == NCH - 1),
                          waits=w0 if di == 0 else [],
                          inc="tt" if di == NCH - 1 else None)
            qev.append(vin(c, "tensor_copy", qfull[:, gsl], ps,
                           waits=[("tt", cnt)]))
            br(c, key, "vv", qev[-1])
        for g in range(NG):
            gsl = slice(g * QG, (g + 1) * QG)
            scnt = sin(c, "activation", sqs[g % 2][:], qfull[:, gsl],
                       AF.Square, waits=[("vv", qev[g])])
            nmm = mmv(c, c.ps_e[0:2, 0:QG], c.cst["headmask16"][:, 0:2],
                      sqs[g % 2][:], True, True,
                      waits=[("ss", scnt)] + bw(c, "e"), inc="tt")
            br(c, "e", "tt", nmm)
            bmm = mmv(c, c.ps_e[0:1, QG:2 * QG] if False else
                      c.ps_c[0:1, 0:QG], c.cst["ones_col"][:],
                      sqs[g % 2][:], True, True,
                      waits=[("ss", scnt)] + bw(c, "c"), inc="tt")
            br(c, "c", "tt", bmm)
            mx = vin(c, "reduce_max", t11, c.ps_c[0:1, 0:QG], axis=AX.X,
                     waits=[("tt", bmm), ("ss", getattr(c, "t11_ss", 0))])
            br(c, "c", "vv", mx)
            if pair == 0 and g == 0:
                gq_ev = vin(c, "tensor_copy", gq2, t11)
            else:
                gq_ev = vin(c, "tensor_tensor", gq2, gq2, t11, ALU.max)
            dmm = mmv(c, c.ps_f[0:2, 0:QG], qgbig[:, h0:h0 + 2],
                      qfull[:, gsl], True, True,
                      waits=[qg_ready, ("vv", qev[g])] + bw(c, "f"), inc="tt")
            lcn = sin(c, "activation", rr[0:2, :], c.ps_e[0:2, 0:QG], AF.Ln,
                      waits=[("tt", nmm)])
            ecn = sin(c, "activation", rr[0:2, :], rr[0:2, :], AF.Exp,
                      scale=-0.5)
            br(c, "e", "ss", ecn)
            dcn = vin(c, "tensor_tensor", dot2[0:2, :],
                      c.ps_f[0:2, 0:QG], rr[0:2, :], ALU.mult,
                      waits=[("ss", ecn), ("tt", dmm)])
            br(c, "f", "vv", dcn)
            d16 = vin(c, "tensor_copy", sqs[g % 2][0:2, :], dot2[0:2, :])
            smm = mmv(c, c.ps_c[0:1, 0:QG], c.cst["ones_col"][0:2, :],
                      sqs[g % 2][0:2, :], True, True,
                      waits=[("vv", d16)] + bw(c, "c"), inc="tt")
            if pair == 0:
                sev = vin(c, "tensor_scalar", simrow[0:1, gsl],
                          c.ps_c[0:1, 0:QG], 1.0 / 16.0, None, ALU.mult,
                          waits=[("tt", smm)])
            else:
                sev = vin(c, "scalar_tensor_tensor", simrow[0:1, gsl],
                          c.ps_c[0:1, 0:QG], 1.0 / 16.0, simrow[0:1, gsl],
                          ALU.mult, ALU.add, waits=[("tt", smm)])
            br(c, "c", "vv", sev)
        wpair_read = pr.now("tt")

    mn1 = vin(c, "tensor_reduce", ccst[0:1, 0:1], simrow[0:1, 5:S], AX.X,
              ALU.min)
    vin(c, "tensor_scalar", ccst[0:1, 0:1], ccst[0:1, 0:1], -1.0, None,
        ALU.mult)
    mx1 = vin(c, "reduce_max", ccst[0:1, 1:2], simrow[0:1, 5:S], axis=AX.X)
    cin = dma(c, "gpsimd", c.cc_in.ap(), ccst[0:1, 0:2], waits=[("vv", mx1)],
              inc="wd")
    pr.fence("gpsimd", "wd")
    ccc = pr.on("gpsimd", lambda e: e.collective_compute(
        "AllReduce", ALU.max, ins=[c.cc_in.ap()], outs=[c.cc_out.ap()],
        replica_groups=[list(range(8))]), waits=[("wd", cin)], inc="gg")
    cout = dma(c, "gpsimd", ccst[0:1, 0:2], c.cc_out.ap(),
               waits=[("gg", ccc)], inc="wd")
    pr.fence("gpsimd", "wd")
    v5 = vin(c, "tensor_scalar", tau, ccst[0:1, 1:2], 0.9, None, ALU.mult,
             waits=[("wd", cout)])
    v6 = vin(c, "scalar_tensor_tensor", tau, ccst[0:1, 0:1], 0.1, tau,
             ALU.mult, ALU.subtract)
    v7 = vin(c, "tensor_scalar", tau, tau, -1.0, None, ALU.mult)
    sd = vin(c, "tensor_scalar", simrow[0:1, :], simrow[0:1, :], tau[0:1, 0:1],
             None, ALU.subtract, waits=[("vv", v7)])
    trs = []
    for kt in range(NT):
        tcn = mmv(c, c.ps_f[:, kt:kt + 1],
                  simrow[0:1, kt * P:(kt + 1) * P],
                  c.cst["ones_row_f"][0:1, 0:1], True, True,
                  waits=[("vv", sd)] + (bw(c, "f") if kt == 0 else []),
                  inc="tt")
        trs.append(tcn)
    v8 = vin(c, "tensor_copy", simcol[:, 0:16], c.ps_f[:, 0:16],
             waits=[("tt", trs[-1])])
    br(c, "f", "vv", v8)
    p1 = vin(c, "tensor_single_scalar", mposc[:, 0:16], simcol[:, 0:16], 0.0,
             ALU.is_gt)
    p2 = vin(c, "memset", mposc[0:5, 0:1], 1.0)
    p3 = vin(c, "tensor_scalar", mnegc[:, 0:16], mposc[:, 0:16], -1.0, 1.0,
             ALU.mult, ALU.add)
    p4 = vin(c, "memset", mnegc[0:5, 0:1], 1.0)
    r1 = vin(c, "tensor_single_scalar", rowA[0:1, :], simrow[0:1, :], 0.0,
             ALU.is_gt, waits=[("vv", sd)])
    r2 = vin(c, "tensor_scalar", rowB[0:1, :], rowA[0:1, :], -1.0, 1.0,
             ALU.mult, ALU.add)
    r3 = vin(c, "memset", rowA[0:1, 0:5], 0.5)
    r4 = vin(c, "memset", rowB[0:1, 0:5], 0.5)
    mask_ready = pr.now("vv")
    af.pop()

    odd_dma = pr.now("bd")
    av_done = [0, 0]
    exp_done = [0, 0]
    v65_read = 0
    kv_ss = 0
    for pair in range(NH // 2):
        h0 = 2 * pair
        wcnts = []
        for di in range(NCH):
            src = WQKV[di * P:(di + 1) * P, :]
            for slot, base in ((0, h0 * HD), (1, D + h0 * HD),
                               (2, 2 * D + h0 * HD)):
                wcnts.append(load_w(c, wpair[di][:, slot * 128:slot * 128 + 128],
                                    src[:, base:base + 128],
                                    waits=[("tt", wpair_read)]))
        pr.fence("gpsimd", "wd")
        wready = ("wd", max(wcnts))
        kev = []
        for g in range(NG):
            gsl = slice(g * QG, (g + 1) * QG)
            key = "a" if g % 2 == 0 else "b"
            ps = [c.ps_a, c.ps_b][g % 2][:, 0:QG]
            w0 = [wready, ("ss", exp_done[g % 2]), ("ss", kv_ss)] +                 xln_ready + bw(c, key) +                 ([("vv", kev[g - 2])] if g >= 2 else [])
            cnt = None
            for di in range(NCH):
                cnt = mmv(c, ps, wpair[di][:, 128:256], xln[di][:, gsl],
                          start=(di == 0), stop=(di == NCH - 1),
                          waits=w0 if di == 0 else [],
                          inc="tt" if di == NCH - 1 else None)
            kev.append(vin(c, "tensor_copy", kpair[:, gsl], ps,
                           waits=[("tt", cnt)]))
        k2mm = 0
        for g in range(NG):
            gsl = slice(g * QG, (g + 1) * QG)
            scnt = sin(c, "activation", sqs[g % 2][:], kpair[:, gsl],
                       AF.Square, waits=[("vv", kev[g])] +
                       ([("tt", k2mm)] if k2mm else []))
            kv_ss = scnt
            k2mm = mmv(c, c.ps_c[0:1, 0:QG], c.cst["ones_col"][:],
                       sqs[g % 2][:], True, True,
                       waits=[("ss", scnt)] + bw(c, "c"), inc="tt")
            mx = vin(c, "reduce_max", t11, c.ps_c[0:1, 0:QG], axis=AX.X,
                     waits=[("tt", k2mm), ("ss", getattr(c, "t11_ss", 0))])
            br(c, "c", "vv", mx)
            if g == 0:
                kc_ev = vin(c, "tensor_copy", k2p, t11)
            else:
                kc_ev = vin(c, "tensor_tensor", k2p, k2p, t11, ALU.max)
        b0 = vin(c, "tensor_tensor", t11, gq2, k2p, ALU.mult,
                 waits=[("vv", kc_ev), ("vv", gq_ev)])
        b1 = sin(c, "activation", t11, t11, AF.Ln, waits=[("vv", b0)])
        b2 = sin(c, "activation", t11, t11, AF.Exp, scale=0.5)
        c.t11_ss = b2
        b3 = mmv(c, c.ps_f[:, 0:1], c.cst["ones_row_f"][0:1, :], t11, True,
                 True, waits=[("ss", b2)] + bw(c, "f"), inc="tt")
        nbc = vin(c, "tensor_scalar", negb, c.ps_f[:, 0:1], -0.125 * 1.01,
                  None, ALU.mult, waits=[("tt", b3)])
        br(c, "f", "vv", nbc)
        vev = []
        for g in range(NG):
            gsl = slice(g * QG, (g + 1) * QG)
            key = "a" if g % 2 == 0 else "b"
            ps = [c.ps_a, c.ps_b][g % 2][:, 0:QG]
            w0 = [wready, ("ss", exp_done[g % 2]), ("ss", kv_ss)] + bw(c, key) +                 ([("vv", vev[g - 2])] if g >= 2 else [])
            cnt = None
            for di in range(NCH):
                cnt = mmv(c, ps, wpair[di][:, 256:384], xln[di][:, gsl],
                          start=(di == 0), stop=(di == NCH - 1),
                          waits=w0 if di == 0 else [],
                          inc="tt" if di == NCH - 1 else None)
            vev.append(vin(c, "tensor_copy", vfm[:, gsl], ps,
                           waits=[("tt", cnt)]))
        mposb = mposc[:, 0:16].unsqueeze(2).broadcast_to([P, 16, 2])
        mnegb = mnegc[:, 0:16].unsqueeze(2).broadcast_to([P, 16, 2])
        vone = vin(c, "tensor_copy",
                   v65ph[:, :, :, 64:65].rearrange("p a b e -> p a (b e)"),
                   mposb, waits=[("vv", mask_ready), ("tt", v65_read)])
        none1 = vin(c, "tensor_copy",
                    v65nh[:, :, :, 64:65].rearrange("p a b e -> p a (b e)"),
                    mnegb, waits=[("vv", mask_ready), ("tt", v65_read)])
        vtr = []
        for kt in range(NT):
            tcnt = ins(c, "tensor", "transpose", c.ps_d[:, 0:P],
                       vfm[:, kt * P:(kt + 1) * P], c.cst["identity"][:],
                       waits=[("vv", vev[kt // 4]), ("tt", v65_read)] +
                             bw(c, "d") +
                             ([("vv", vtr[kt - 1])] if kt >= 1 else []),
                       inc="tt")
            vp = vin(c, "tensor_scalar",
                     v65ph[:, kt, :, 0:64],
                     c.ps_d[:, 0:P].rearrange("p (h e) -> p h e", e=64),
                     mposc[:, kt:kt + 1], None, ALU.mult,
                     waits=[("tt", tcnt), ("vv", mask_ready)])
            vn = vin(c, "tensor_scalar",
                     v65nh[:, kt, :, 0:64],
                     c.ps_d[:, 0:P].rearrange("p (h e) -> p h e", e=64),
                     mnegc[:, kt:kt + 1], None, ALU.mult)
            vtr.append(vn)
        br(c, "d", "vv", max(vtr))
        vready = ("vv", max(vtr + [vone, none1]))
        for g in range(NG):
            gsl = slice(g * QG, (g + 1) * QG)
            psq = c.ps_c[:, 0:QG]
            w0 = [wready] + xln_ready + bw(c, "c")
            cnt = None
            for di in range(NCH):
                cnt = mmv(c, psq, wpair[di][:, 0:128], xln[di][:, gsl],
                          start=(di == 0), stop=(di == NCH - 1),
                          waits=w0 if di == 0 else [],
                          inc="tt" if di == NCH - 1 else None)
            qev = vin(c, "tensor_copy", qpair[:], psq, waits=[("tt", cnt)])
            br(c, "c", "vv", qev)
            for hh in range(2):
                hp = slice(hh * HD, (hh + 1) * HD)
                avp = [c.ps_c, c.ps_e][hh][0:65, 0:QG]
                for ri in range(8):
                    k0, k1 = 2 * ri, 2 * ri + 2
                    sreg = [c.ps_a, c.ps_b][ri % 2]
                    qkc = None
                    for kt in range(k0, k1):
                        j = kt - k0
                        qkc = mmv(c, sreg[:, j * QG:(j + 1) * QG],
                                  kpair[hp, kt * P:(kt + 1) * P],
                                  qpair[hp, :], True, True,
                                  waits=([("vv", qev),
                                          ("ss", exp_done[ri % 2])] +
                                         (bw(c, "c", "e") if ri == 0 else []) +
                                         ([vready] if ri == 0 else []))
                                  if kt == k0 else [],
                                  inc="tt" if kt == k1 - 1 else None)
                    ecnt = sin(c, "activation", exps[ri % 2][:],
                               sreg[:, 0:1024], AF.Exp,
                               bias=negb, scale=0.125,
                               waits=[("tt", qkc), ("vv", nbc)])
                    exp_done[ri % 2] = ecnt
                    for kt in range(k0, k1):
                        j = kt - k0
                        mmv(c, c.ps_c[0:65, 0:QG], v65pt[kt][:, 65 * hh:65 * hh + 65],
                            exps[ri % 2][:, j * QG:(j + 1) * QG],
                            start=(kt == 0), stop=(kt == NT - 1),
                            waits=[("ss", ecnt)] +
                                  ([("vv", av_done[0])] if kt == 0 else []),
                            inc=None)
                        avc = mmv(c, c.ps_e[0:65, 0:QG],
                                  v65nt[kt][:, 65 * hh:65 * hh + 65],
                                  exps[ri % 2][:, j * QG:(j + 1) * QG],
                                  start=(kt == 0), stop=(kt == NT - 1),
                                  waits=([("vv", av_done[1])] if kt == 0 else []),
                                  inc="tt" if kt == NT - 1 else None)
                v1 = vin(c, "tensor_copy", rr[0:1, :], c.ps_c[64:65, 0:QG],
                         waits=[("tt", avc)])
                v2 = vin(c, "tensor_copy", rrb, c.ps_e[64:65, 0:QG])
                vin(c, "reciprocal", rr[0:1, :], rr[0:1, :])
                vin(c, "reciprocal", rrb, rrb)
                v3 = vin(c, "tensor_tensor", rr[0:1, :], rr[0:1, :],
                         rowA[0:1, gsl], ALU.mult)
                v4 = vin(c, "tensor_tensor", rrb, rrb,
                         rowB[0:1, gsl], ALU.mult)
                v5 = vin(c, "tensor_copy", rr16[0:1, :], rr[0:1, :])
                v5b = vin(c, "tensor_copy", rr16b, rrb)
                bcr = mmv(c, c.ps_f[0:64, 0:QG], c.cst["ones_row"][0:1, 0:64],
                          rr16[0:1, :], True, True,
                          waits=[("vv", v5b)] + bw(c, "f"), inc="tt")
                bcr2 = mmv(c, c.ps_f[64:128, 0:QG],
                           c.cst["ones_row"][0:1, 0:64],
                           rr16b, True, True, inc="tt")
                v6 = vin(c, "tensor_copy", bcq[0:64, :], c.ps_f[0:64, 0:QG],
                         waits=[("tt", bcr2)])
                v7 = vin(c, "tensor_copy", bcn[0:64, :], c.ps_f[64:128, 0:QG])
                br(c, "f", "vv", v7)
                tgt = stage64 if hh == 1 else None
                u1 = vin(c, "tensor_tensor", stageB[0:64, :],
                         c.ps_e[0:64, 0:QG], bcn[0:64, :], ALU.mult,
                         waits=[("vv", v7), ("bd", odd_dma)])
                av_done[1] = u1
                br(c, "e", "vv", u1)
                u2 = vin(c, "tensor_tensor", stage64[0:64, :],
                         c.ps_c[0:64, 0:QG], bcq[0:64, :], ALU.mult)
                av_done[0] = u2
                br(c, "c", "vv", u2)
                if hh == 0:
                    fin = vin(c, "tensor_tensor", ofm[pair][0:64, gsl],
                              stage64[0:64, :], stageB[0:64, :], ALU.add)
                else:
                    fin = vin(c, "tensor_tensor", stage64[0:64, :],
                              stage64[0:64, :], stageB[0:64, :], ALU.add)
                    odd_dma = dma(c, "sync", ofm[pair][64:128, gsl],
                                  stage64[0:64, :], waits=[("vv", fin)],
                                  inc="bd")
                    pr.fence("sync", "bd")
        v65_read = pr.now("tt")
        wpair_read = pr.now("tt")

    ab.pop()
    wout = [ab.take(D, f"n_wo{i}") for i in range(NCH)]
    yb = [ab.take(QG, f"n_yb{i}") for i in range(NCH)]
    wout_cnt = max(load_w(c, wout[di],
                          c.dram["l1f_wproj"][di * P:(di + 1) * P, :],
                          waits=[("tt", wpair_read)])
                   for di in range(NCH))
    pr.fence("gpsimd", "wd")

    first = [("wd", wout_cnt), ("bd", odd_dma),
             ("ss", exp_done[0]), ("ss", exp_done[1]),
             ("vv", max(av_done))]
    for g in range(NG):
        gsl = slice(g * QG, (g + 1) * QG)
        ev = []
        for oc in range(NCH):
            reg = [c.ps_a, c.ps_b][oc % 2]
            ps = reg[:, (oc // 2 % 2) * QG:(oc // 2 % 2) * QG + QG]
            cnt = None
            for di in range(NCH):
                cnt = mmv(c, ps, wout[di][:, oc * P:(oc + 1) * P],
                          ofm[di][:, gsl],
                          start=(di == 0), stop=(di == NCH - 1),
                          waits=(first + bw(c, "a", "b") +
                                 ([("vv", ev[oc - 4])] if oc >= 4 else []))
                          if di == 0 else [],
                          inc="tt" if di == NCH - 1 else None)
            ev.append(vin(c, "tensor_scalar", yb[oc][:], ps,
                          c.cst["l1f_bproj_c"][:, oc:oc + 1], None, ALU.add,
                          waits=[("tt", cnt)]))
        first = [("vv", max(ev))]
        br(c, "a", "vv", max(ev))
        br(c, "b", "vv", max(ev))
        post_ln(c, yb, g, [("vv", max(ev))])
    ab.pop(); af.pop()


def emit_l1a(c, xln, xln_ready):
    pr, ab, af = c.pr, c.ab, c.af
    ab.push(); af.push()
    ofm = [ab.take(S, f"a_ofm{i}") for i in range(NCH)]
    ab.push()
    kpair = ab.take(S, "a_kpair")
    vfm = ab.take(S, "a_vfm")
    v65 = ab.take(NT * 130, "a_v65")
    v65t = [v65[:, kt * 130:kt * 130 + 130] for kt in range(NT)]
    v65h = v65.rearrange("p (t h e) -> p t h e", h=2, e=65)
    qpair = ab.take(QG, "a_qpair")
    exps = [ab.take(1024, "a_exps0"), ab.take(1024, "a_exps1")]
    wpair = [ab.take(384, f"a_wp{i}") for i in range(NCH)]
    sqs = [ab.take(QG, "a_sqs0"), ab.take(QG, "a_sqs1")]
    stage64 = ab.take(QG, "a_st64")
    rr16 = ab.take(512, "a_rr16")[0:1, :]
    bcq = ab.take(QG, "a_bcq")
    attn_g = ab.take(4 * 384, "a_attn")
    dbuf = ab.take(256, "a_dbuf")
    dtb = ab.take(256, "a_dtb")
    lrsel = ab.take(QG, "a_lrsel")
    edgeLR = ab.take(64, "a_edge")
    rve16 = ab.take(64, "a_rve16")[0:2, :]
    zr65 = ab.take(68, "a_zr65")
    qforb = ab.take(2052, "a_qforb")
    ncol = ab.take(4, "a_ncol")[:, 0:1]
    ncol2 = [ab.take(4, "a_nc0")[:, 0:1], ab.take(4, "a_nc1")[:, 0:1]]
    gqb = ab.take(4, "a_gqb")[0:2, 0:1]
    mixt = [ab.take(P, f"a_mx{j}") for j in range(NCH)]
    mprod = [ab.take(P, "a_mpL"), ab.take(P, "a_mpR")]
    mcb = [ab.take(4, "a_mcbL"), ab.take(4, "a_mcbR")]
    mrow = [ab.take(P, "a_mrL"), ab.take(P, "a_mrR")]
    yb = None
    af.push()
    band_g = af.take(4 * 384, "a_band")
    rr = af.take(QG, "a_rr")[0:1, :]
    psb = af.take(260, "a_psb")
    pdiff2 = af.take(512, "a_pd2")
    bias2 = af.take(4, "a_bias2")
    q2col = af.take(4, "a_q2")[0:2, 0:1]
    k2col = af.take(4, "a_k2")[0:2, 0:1]
    rk2 = af.take(4, "a_rk2")[0:2, 0:1]
    tcol = af.take(4, "a_tc")[0:2, 0:1]
    macc = [af.take(4, "a_maL"), af.take(4, "a_maR")]
    drow = af.take(QG, "a_drow")[0:1, :]
    dcol = af.take(4, "a_dcol")[:, 0:1]
    gq2 = af.take(4, "a_gq2")[0:2, 0:1]
    gqt = af.take(4, "a_gqt")[0:2, 0:1]
    qf2col = af.take(4, "a_qf2")[0:2, 0:1]
    t2col = af.take(4, "a_t2c")[0:2, 0:1]

    WQ, WK, WV = c.dram["l1a_wq"], c.dram["l1a_wk"], c.dram["l1a_wv"]

    relk = c.cst["l1a_relkT"]
    sq0 = sin(c, "activation", sqs[0][0:64, 0:257], relk[0:64, :], AF.Square,
              waits=c.ready)
    rmm = mmv(c, c.ps_f[0:2, 0:257], c.cst["ones_2col"][0:64, 0:2],
              sqs[0][0:64, 0:257], True, True,
              waits=[("ss", sq0)] + bw(c, "f"), inc="tt")
    rmx = vin(c, "reduce_max", rk2, c.ps_f[0:2, 0:257], axis=AX.X,
              waits=[("tt", rmm)])
    br(c, "f", "vv", rmx)
    e16 = vin(c, "tensor_copy", rve16[0:2, 0:HD], c.cst["l1a_relv_edge"][:],
              waits=c.ready)
    zrm = vin(c, "memset", zr65[0:1, 0:65], 0.0, waits=c.ready)
    emm = mmv(c, c.ps_f[0:32, 0:HD], c.cst["edge2"][:], rve16[0:2, 0:HD],
              True, True, waits=[("vv", e16)] + bw(c, "f"), inc="tt")
    ecv = vin(c, "tensor_copy", edgeLR[0:32, 0:HD], c.ps_f[0:32, 0:HD],
              waits=[("tt", emm)])
    br(c, "f", "vv", ecv)

    odd_dma = pr.now("bd")
    av_done = 0
    exp_done = [0, 0]
    v65_read = 0
    kv_ss = 0
    wpair_read = 0
    c.scr_tt = 0
    c.l1a_ce = {"a0": 0, "a1": 0, "t0": 0, "t1": 0, "m0": 0, "m1": 0}
    c.prev_bg = 0
    c.attn_rd = 0
    c.dbuf_rd = 0
    c.dmm_prev = 0

    mxc = max(load_w(c, mixt[j], c.dram["l1a_mix"][j * P:(j + 1) * P, :],
                     waits=c.ready) for j in range(NCH))
    pr.fence("gpsimd", "wd")
    qsend = {}
    AQI = c.dram["aq_in"].tensor
    AQO = c.dram["aq_out"].tensor
    AVI = c.dram["av_in"].tensor
    AVO = c.dram["av_out"].tensor
    nmm_prev = 0
    ncol_ev = 0
    for pair in range(NH // 2):
        h0 = 2 * pair
        wc = [load_w(c, wpair[di][:, 0:P],
                     WQ[di * P:(di + 1) * P, h0 * HD:h0 * HD + P],
                     waits=[("tt", wpair_read)] if wpair_read else [])
              for di in range(NCH)]
        pr.fence("gpsimd", "wd")
        wready = ("wd", max(wc))
        qfm = ofm[pair]
        qevs = []
        mx_ev = 0
        for g in range(NG):
            gsl = slice(g * QG, (g + 1) * QG)
            key = "a" if g % 2 == 0 else "b"
            ps = [c.ps_a, c.ps_b][g % 2][:, 0:QG]
            w0 = [wready] + xln_ready + bw(c, key)
            cnt = None
            for di in range(NCH):
                cnt = mmv(c, ps, wpair[di][:, 0:P], xln[di][:, gsl],
                          start=(di == 0), stop=(di == NCH - 1),
                          waits=w0 if di == 0 else [],
                          inc="tt" if di == NCH - 1 else None)
            ev = vin(c, "tensor_scalar", qfm[:, gsl], ps,
                     c.cst["l1a_bq_c"][:, pair:pair + 1], None, ALU.add,
                     waits=[("tt", cnt)])
            br(c, key, "vv", ev)
            qevs.append(ev)
            scr = exps[g % 2][:, 0:QG]
            sq = sin(c, "activation", scr, qfm[:, gsl], AF.Square,
                     waits=[("vv", ev)] +
                           ([("tt", nmm_prev)] if nmm_prev else []))
            nmm_prev = mmv(c, c.ps_e[0:2, 0:QG], c.cst["headmask16"][:, 0:2],
                           scr, True, True,
                           waits=[("ss", sq)] + bw(c, "e"), inc="tt")
            if g == 0:
                mx_ev = vin(c, "reduce_max", gq2, c.ps_e[0:2, 0:QG],
                            axis=AX.X, waits=[("tt", nmm_prev)])
            else:
                vin(c, "reduce_max", gqt, c.ps_e[0:2, 0:QG], axis=AX.X,
                    waits=[("tt", nmm_prev)])
                mx_ev = vin(c, "tensor_tensor", gq2, gq2, gqt, ALU.max)
            br(c, "e", "vv", mx_ev)
        gbe = vin(c, "tensor_copy", gqb, gq2, waits=[("vv", mx_ev)])
        pmm2 = mmv(c, c.ps_f[:, 0:1], c.cst["pairsel2"][:], gqb,
                   True, True, waits=[("vv", gbe)] + bw(c, "f"), inc="tt")
        ncol_ev = vin(c, "tensor_copy", ncol, c.ps_f[:, 0:1],
                      waits=[("tt", pmm2)] +
                            ([("bd", qsend[pair - 1])] if pair else []))
        br(c, "f", "vv", ncol_ev)
        wpair_read = pr.now("tt")
        sd1 = dma(c, "sync",
                  bass.AP(AQI, pair * P * 2049, [[2049, P], [1, 2048]]),
                  qfm[:], waits=[("vv", max(qevs))], inc="bd")
        sd2 = dma(c, "sync",
                  bass.AP(AQI, pair * P * 2049 + 2048, [[2049, P], [1, 1]]),
                  ncol, waits=[("vv", ncol_ev)], inc="bd",
                  allow_slow_non_contiguous=True)
        pr.fence("sync", "bd")
        qsend[pair] = sd2
    wpair_read = pr.now("tt")
    agq = pr.on("gpsimd", lambda e: e.collective_compute(
        "AllGather", ALU.bypass, ins=[c.dram["aq_in"]],
        outs=[c.dram["aq_out"]], replica_groups=[list(range(8))]),
        waits=[("bd", qsend[NCH - 1])], inc="gg")
    agq_cnt = agq

    qforb_rd = 0
    for pair in range(NH // 2):
        h0 = 2 * pair
        qstk2 = [kpair, vfm]
        tiles = [c.ps_a[:, 0:QG], c.ps_a[:, QG:2 * QG],
                 c.ps_b[:, 0:QG], c.ps_b[:, QG:2 * QG]]
        mixlast = 0
        mpar = {0: 0, 1: 0}
        for jj in range(NCH):
            buf = qstk2[jj % 2]
            lw = dma(c, "sync", buf[:, 0:2048],
                     bass.AP(AQO, (pair * 8 * P + jj * P) * 2049,
                             [[2049, P], [1, 2048]]),
                     waits=[("gg", agq_cnt)] +
                           ([("tt", mpar[jj % 2])] if mpar[jj % 2]
                            else ([("tt", wpair_read)] if wpair_read else [])),
                     inc="bd")
            ln_ = dma(c, "sync", ncol2[jj % 2],
                      bass.AP(AQO, (pair * 8 * P + jj * P) * 2049 + 2048,
                              [[2049, P], [1, 1]]),
                      waits=[("gg", agq_cnt)] +
                            ([("tt", mpar[jj % 2])] if mpar[jj % 2] else []),
                      inc="bd", allow_slow_non_contiguous=True)
            pr.fence("sync", "bd")
            w0 = ([("bd", ln_), ("wd", mxc)] +
                  ([("tt", qforb_rd)] if (jj == 0 and qforb_rd) else []) +
                  ([("ss", exp_done[0]), ("ss", exp_done[1])] +
                   bw(c, "a") + bw(c, "b") + bw(c, "e") if jj == 0 else []))
            for ct in range(4):
                mmv(c, tiles[ct], mixt[jj], buf[:, ct * QG:(ct + 1) * QG],
                    start=(jj == 0), stop=(jj == NCH - 1),
                    waits=w0 if ct == 0 else [],
                    inc="tt" if (jj == NCH - 1 and ct == 3) else None)
            mixlast = mmv(c, c.ps_e[:, 0:1], mixt[jj], ncol2[jj % 2],
                          start=(jj == 0), stop=(jj == NCH - 1), inc="tt")
            mpar[jj % 2] = mixlast
        qf_ev = []
        for ct in range(4):
            qf_ev.append(vin(c, "tensor_copy", qforb[:, ct * QG:(ct + 1) * QG],
                             tiles[ct], waits=[("tt", mixlast)]))
        qf_ev.append(vin(c, "tensor_copy", qforb[:, 2048:2049],
                         c.ps_e[:, 0:1]))
        qfor_ready = max(qf_ev)
        br(c, "a", "vv", qfor_ready); br(c, "b", "vv", qfor_ready)
        br(c, "e", "vv", qfor_ready)
        nmm2 = mmv(c, c.ps_f[0:2, 0:1], c.cst["headmask16"][:, 0:2],
                   qforb[:, 2048:2049], True, True,
                   waits=[("vv", qfor_ready)] + bw(c, "f"), inc="tt")
        qfv = vin(c, "tensor_scalar", qf2col, c.ps_f[0:2, 0:1], 1.0 / 64.0,
                  None, ALU.mult, waits=[("tt", nmm2)])
        br(c, "f", "vv", qfv)
        wcnts = []
        for di in range(NCH):
            for slot, wsrc in ((0, WQ), (1, WK), (2, WV)):
                wcnts.append(load_w(
                    c, wpair[di][:, slot * P:(slot + 1) * P],
                    wsrc[di * P:(di + 1) * P, h0 * HD:h0 * HD + P],
                    waits=[("tt", wpair_read)] if wpair_read else []))
        pr.fence("gpsimd", "wd")
        wready = ("wd", max(wcnts))
        kev = []
        for g in range(NG):
            gsl = slice(g * QG, (g + 1) * QG)
            key = "a" if g % 2 == 0 else "b"
            ps = [c.ps_a, c.ps_b][g % 2][:, 0:QG]
            w0 = [wready, ("ss", exp_done[g % 2]), ("ss", kv_ss)] +                 xln_ready + bw(c, key) +                 ([("vv", kev[g - 2])] if g >= 2 else [])
            cnt = None
            for di in range(NCH):
                cnt = mmv(c, ps, wpair[di][:, 128:256], xln[di][:, gsl],
                          start=(di == 0), stop=(di == NCH - 1),
                          waits=w0 if di == 0 else [],
                          inc="tt" if di == NCH - 1 else None)
            kev.append(vin(c, "tensor_scalar", kpair[:, gsl], ps,
                           c.cst["l1a_bk_c"][:, pair:pair + 1], None, ALU.add,
                           waits=[("tt", cnt), ("tt", mixlast)]))
        nrm_mm = 0
        for g in range(NG):
            gsl = slice(g * QG, (g + 1) * QG)
            scnt = sin(c, "activation", sqs[g % 2][:], kpair[:, gsl],
                       AF.Square, waits=[("vv", kev[g])] +
                       ([("tt", nrm_mm)] if g >= 2 else []))
            kv_ss = scnt
            nrm_mm = mmv(c, c.ps_e[0:2, 0:QG], c.cst["headmask16"][:, 0:2],
                         sqs[g % 2][:], start=(g == 0), stop=(g == NG - 1),
                         waits=[("ss", scnt)] + (bw(c, "e") if g == 0 else []),
                         inc="tt")
            if g == 0:
                br(c, "e", "tt", nrm_mm)
        kn = vin(c, "reduce_max", k2col, c.ps_e[0:2, 0:QG], axis=AX.X,
                 waits=[("tt", nrm_mm)])
        br(c, "e", "vv", kn)
        vev = []
        for g in range(NG):
            gsl = slice(g * QG, (g + 1) * QG)
            key = "a" if g % 2 == 0 else "b"
            ps = [c.ps_a, c.ps_b][g % 2][:, 0:QG]
            w0 = [wready, ("ss", exp_done[g % 2]), ("ss", kv_ss)] + bw(c, key) +                 ([("vv", vev[g - 2])] if g >= 2 else [])
            cnt = None
            for di in range(NCH):
                cnt = mmv(c, ps, wpair[di][:, 256:384], xln[di][:, gsl],
                          start=(di == 0), stop=(di == NCH - 1),
                          waits=w0 if di == 0 else [],
                          inc="tt" if di == NCH - 1 else None)
            vev.append(vin(c, "tensor_scalar", vfm[:, gsl], ps,
                           c.cst["l1a_bv_c"][:, pair:pair + 1], None, ALU.add,
                           waits=[("tt", cnt), ("tt", mixlast)]))
        vone = vin(c, "memset", v65h[:, :, :, 64:65], 1.0,
                   waits=[("tt", v65_read)])
        vtr = []
        for kt in range(NT):
            tcnt = ins(c, "tensor", "transpose", c.ps_d[:, 0:P],
                       vfm[:, kt * P:(kt + 1) * P], c.cst["identity"][:],
                       waits=[("vv", vev[kt // 4]), ("tt", v65_read)] +
                             bw(c, "d") +
                             ([("vv", vtr[kt - 1])] if kt >= 1 else []),
                       inc="tt")
            vtr.append(vin(c, "tensor_copy", v65h[:, kt, :, 0:64],
                           c.ps_d[:, 0:P].rearrange("p (h e) -> p h e", e=64),
                           waits=[("tt", tcnt)]))
        br(c, "d", "vv", max(vtr))
        vready = ("vv", max(vtr + [vone]))

        for g in range(NG):
            gsl = slice(g * QG, (g + 1) * QG)
            psq = c.ps_c[:, 0:QG]
            w0 = [wready] + xln_ready + bw(c, "c")
            cnt = None
            for di in range(NCH):
                cnt = mmv(c, psq, wpair[di][:, 0:128], xln[di][:, gsl],
                          start=(di == 0), stop=(di == NCH - 1),
                          waits=w0 if di == 0 else [],
                          inc="tt" if di == NCH - 1 else None)
            qev = vin(c, "tensor_scalar", qpair[:], psq,
                      c.cst["l1a_bq_c"][:, pair:pair + 1], None, ALU.add,
                      waits=[("tt", cnt), ("ss", c.qpair_ss)])
            br(c, "c", "vv", qev)
            scnt = sin(c, "activation", sqs[0][:], qpair[:], AF.Square,
                       waits=[("vv", qev), ("tt", nrm_mm)])
            c.qpair_ss = scnt
            qmm = mmv(c, c.ps_e[0:2, 0:QG], c.cst["headmask16"][:, 0:2],
                      sqs[0][:], True, True,
                      waits=[("ss", scnt)] + bw(c, "e"), inc="tt")
            qn = vin(c, "reduce_max", q2col, c.ps_e[0:2, 0:QG], axis=AX.X,
                     waits=[("tt", qmm)])
            br(c, "e", "vv", qn)
            lq = sin(c, "activation", q2col, q2col, AF.Ln,
                     waits=[("vv", qn), ("vv", kn), ("vv", qfv),
                            ("ss", c.bias_ss)])
            sin(c, "activation", q2col, q2col, AF.Exp, scale=0.5)
            sin(c, "activation", k2col, k2col, AF.Ln)
            sin(c, "activation", k2col, k2col, AF.Exp, scale=0.5)
            sin(c, "activation", rk2, rk2, AF.Ln)
            sin(c, "activation", rk2, rk2, AF.Exp, scale=0.5)
            sin(c, "activation", qf2col, qf2col, AF.Ln)
            sq_ = sin(c, "activation", qf2col, qf2col, AF.Exp, scale=0.5)
            v0 = vin(c, "tensor_scalar", tcol, rk2, 2.0, None, ALU.mult,
                     waits=[("ss", sq_)])
            vin(c, "tensor_tensor", tcol, tcol, qf2col, ALU.mult)
            vin(c, "tensor_tensor", t2col, q2col, k2col, ALU.mult)
            v2 = vin(c, "tensor_tensor", tcol, tcol, t2col, ALU.add)
            vin(c, "tensor_tensor", k2col, k2col, k2col, ALU.mult)
            vin(c, "tensor_tensor", qf2col, qf2col, qf2col, ALU.mult)
            v3 = vin(c, "tensor_tensor", rk2, rk2, rk2, ALU.mult)
            tmm = mmv(c, c.ps_f[0:1, 0:2], tcol, c.cst["idrow16"][0:2, 0:2],
                      True, True, waits=[("vv", v2)] + bw(c, "f"), inc="tt")
            trw = vin(c, "tensor_copy", rr[0:1, 0:2], c.ps_f[0:1, 0:2],
                      waits=[("tt", tmm)])
            bmm = mmv(c, c.ps_f[:, 0:2], c.cst["ones_row_f"][0:1, :],
                      rr[0:1, 0:2], True, True, waits=[("vv", trw)], inc="tt")
            nbc = vin(c, "tensor_scalar", bias2[:, 0:2], c.ps_f[:, 0:2],
                      -0.125 * 1.01, None, ALU.mult, waits=[("tt", bmm)])
            br(c, "f", "vv", nbc)
            c.bias_ss = pr.now("ss")
            for hh in range(2):
                h = h0 + hh
                hp = slice(hh * HD, (hh + 1) * HD)
                avp = c.ps_c[0:65, 0:QG]
                pb_ev = {}
                d_ev = {}
                for qi4 in range(4):
                    qt = g * 4 + qi4
                    pmm = mmv(c, c.ps_e[:, 0:257],
                              qforb[hp, qt * P:(qt + 1) * P], relk[hp, :],
                              True, True,
                              waits=bw(c, "e") + [("vv", qfor_ready)],
                              inc="tt")
                    pev = vin(c, "tensor_copy", psb[:, 0:257], c.ps_e[:, 0:257],
                              waits=[("tt", pmm)])
                    br(c, "e", "vv", pev)
                    z0 = vin(c, "memset", pdiff2[:, 0:128], 0.0,
                             waits=[("bd", c.prev_bg)] if c.prev_bg else [])
                    b1 = vin(c, "tensor_scalar", pdiff2[:, 128:383],
                             bass.AP(psb.tensor, psb.offset + 255,
                                     [[psb.ap[0][0], P], [-1, 255]]),
                             psb[:, 256:257], None, ALU.subtract,
                             waits=[("vv", pev)])
                    dc_ = vin(c, "tensor_tensor", dcol[:, 0:1], psb[:, 0:1],
                              psb[:, 256:257], ALU.subtract,
                              waits=[("tt", c.dmm_prev)] if c.dmm_prev else [])
                    b2 = vin(c, "tensor_copy", pdiff2[:, 383:512],
                             dcol[:, 0:1].broadcast_to([P, 129]))
                    bg = dma(c, "sync", band_g[:, qi4 * 384:(qi4 + 1) * 384],
                             bass.AP(pdiff2.tensor, pdiff2.offset + 127,
                                     [[c.af.words - 1, P], [1, 384]]),
                             waits=[("vv", b2), ("vv", b1), ("vv", z0),
                                    ("tt", c.scr_tt)],
                             inc="bd")
                    pr.fence("sync", "bd")
                    pb_ev[qi4] = bg
                    c.prev_bg = bg
                    dmm2 = mmv(c, c.ps_f[0:1, 0:P], dcol[:, 0:1],
                               c.cst["identity_f"][:], True, True,
                               waits=[("vv", dc_)] + bw(c, "f"), inc="tt")
                    c.dmm_prev = dmm2
                    d_ev[qi4] = vin(c, "tensor_copy",
                                    drow[0:1, qi4 * P:(qi4 + 1) * P],
                                    c.ps_f[0:1, 0:P], waits=[("tt", dmm2)])
                    br(c, "f", "vv", d_ev[qi4])
                dr_ready = ("vv", max(d_ev.values()))
                zats = []
                ag_guard = [("bd", c.attn_rd)] if c.attn_rd else []
                if g == 0:
                    zats.append(vin(c, "memset", attn_g[:, 0:128], 0.0,
                                    waits=ag_guard))
                if g == NG - 1:
                    zats.append(vin(c, "memset",
                                    attn_g[:, 3 * 384 + 256:4 * 384], 0.0,
                                    waits=ag_guard))
                first_av = True
                lr_first = True
                atr_ev = {}
                for ri in range(8):
                    k0 = 2 * ri
                    for kt in (k0, k0 + 1):
                        j = kt - k0
                        sreg = [c.ps_a, c.ps_b][ri % 2]
                        seg = sreg[:, j * QG:(j + 1) * QG]
                        writers = [("qk", None, None)]
                        for qi4 in range(4):
                            qt = g * 4 + qi4
                            s_ = kt - qt + 1
                            if 0 <= s_ <= 2:
                                writers.append(("inj", qi4, s_))
                        for qi4 in range(4):
                            qt = g * 4 + qi4
                            if kt >= qt + 2:
                                writers.append(("delta", qi4, None))
                        for wi, (kind, qi4, s_) in enumerate(writers):
                            st_ = (wi == 0)
                            sp_ = (wi == len(writers) - 1)
                            if kind == "inj":
                                ins(c, "tensor", "matmul",
                                    seg[:, qi4 * P:(qi4 + 1) * P],
                                    band_g[:, qi4 * 384 + s_ * P:
                                           qi4 * 384 + (s_ + 1) * P],
                                    c.cst["identity_f"][:],
                                    start=st_, stop=sp_, is_transpose=True,
                                    waits=[("bd", pb_ev[qi4])] +
                                          ([("ss", exp_done[ri % 2])]
                                           if wi == 0 else []),
                                    inc="tt")
                            elif kind == "qk":
                                mmv(c, seg, kpair[hp, kt * P:(kt + 1) * P],
                                    qpair[hp, :], start=st_, stop=sp_,
                                    waits=([("vv", qev),
                                            ("ss", exp_done[ri % 2])]
                                           if wi == 0 else []),
                                    inc="tt")
                            else:
                                mmv(c, seg[:, qi4 * P:(qi4 + 1) * P],
                                    c.cst["ones_row_f"][0:1, :],
                                    drow[0:1, qi4 * P:(qi4 + 1) * P],
                                    start=st_, stop=sp_,
                                    waits=[dr_ready],
                                    inc="tt")
                    qkc = pr.now("tt")
                    ecnt = sin(c, "activation", exps[ri % 2][:],
                               sreg[:, 0:1024], AF.Exp,
                               bias=bias2[:, hh:hh + 1], scale=0.125,
                               waits=[("tt", qkc), ("vv", nbc)])
                    exp_done[ri % 2] = ecnt
                    for kt in (k0, k0 + 1):
                        j = kt - k0
                        mmv(c, avp, v65t[kt][:, 65 * hh:65 * hh + 65],
                            exps[ri % 2][:, j * QG:(j + 1) * QG],
                            start=first_av, stop=(kt == NT - 1),
                            waits=[("ss", ecnt), vready] +
                                  (bw(c, "c") if first_av else []),
                            inc=None)
                        mmv(c, c.ps_e[0:32, 0:QG],
                            c.cst["lr_step"][:, kt * 32:(kt + 1) * 32],
                            exps[ri % 2][:, j * QG:(j + 1) * QG],
                            start=lr_first, stop=(kt == NT - 1),
                            waits=(bw(c, "e") if lr_first else []),
                            inc="tt" if kt == NT - 1 else None)
                        first_av = False
                        lr_first = False
                        for qi4 in range(4):
                            qt = g * 4 + qi4
                            s_ = kt - qt + 1
                            if 0 <= s_ <= 2:
                                tcn = ins(c, "tensor", "transpose",
                                          c.ps_d[:, 0:P],
                                          exps[ri % 2][:, j * QG + qi4 * P:
                                                        j * QG + (qi4 + 1) * P],
                                          c.cst["identity"][:],
                                          waits=[("ss", ecnt)] + bw(c, "d"),
                                          inc="tt")
                                atr_ev[(qi4, s_)] = vin(
                                    c, "tensor_copy",
                                    attn_g[:, qi4 * 384 + s_ * P:
                                           qi4 * 384 + (s_ + 1) * P],
                                    c.ps_d[:, 0:P],
                                    waits=[("tt", tcn)] +
                                          ([("bd", c.attn_rd)]
                                           if c.attn_rd else []))
                                br(c, "d", "vv", atr_ev[(qi4, s_)])
                lr_mm = pr.now("tt")
                v1 = vin(c, "tensor_copy", rr, avp[64:65, :],
                         waits=[("tt", lr_mm)])
                vin(c, "reciprocal", rr, rr)
                v3 = vin(c, "tensor_copy", rr16, rr)
                bcr = mmv(c, c.ps_f[0:64, 0:QG], c.cst["ones_row"][0:1, 0:64],
                          rr16, True, True, waits=[("vv", v3)] + bw(c, "f"),
                          inc="tt")
                v4 = vin(c, "tensor_copy", bcq[0:64, :], c.ps_f[0:64, 0:QG],
                         waits=[("tt", bcr)])
                br(c, "f", "vv", v4)
                if hh == 0:
                    adn = vin(c, "tensor_tensor", ofm[pair][0:64, gsl],
                              avp[0:64, :], bcq[0:64, :], ALU.mult,
                              waits=[("vv", v4), ("bd", odd_dma),
                                     ("bd", qsend[pair])])
                else:
                    adn = vin(c, "tensor_tensor", stage64[0:64, :],
                              avp[0:64, :], bcq[0:64, :], ALU.mult,
                              waits=[("vv", v4), ("bd", odd_dma)])
                    odd_dma = dma(c, "sync", ofm[pair][64:128, gsl],
                                  stage64[0:64, :], waits=[("vv", adn)],
                                  inc="bd")
                    pr.fence("sync", "bd")
                br(c, "c", "vv", adn)
                mmv(c, c.ps_c[0:64, 0:QG], zr65[0:1, 0:64],
                    c.cst["lr_step"][0:1, 0:QG], start=True, stop=False,
                    waits=bw(c, "c") + [("vv", zrm)], inc=None)
                for qi4 in range(4):
                    qt = g * 4 + qi4
                    aw = [("vv", v) for (q_, s_), v in atr_ev.items()
                          if q_ == qi4] + [("vv", z) for z in zats]
                    dg = dma(c, "sync", dbuf[:, 0:255],
                             bass.AP(attn_g.tensor,
                                     attn_g.offset + qi4 * 384 + 1,
                                     [[c.ab.words + 1, P], [1, 255]]),
                             waits=aw + ([("tt", c.dbuf_rd)]
                                         if c.dbuf_rd else []),
                             inc="bd")
                    pr.fence("sync", "bd")
                    tr1 = ins(c, "tensor", "transpose", c.ps_d[:, 0:P],
                              dbuf[:, 0:128], c.cst["identity"][:],
                              waits=[("bd", dg)] + bw(c, "d"), inc="tt")
                    ev1 = vin(c, "tensor_copy", dtb[:, 0:128],
                              c.ps_d[:, 0:P], waits=[("tt", tr1)])
                    tr2 = ins(c, "tensor", "transpose", c.ps_d[0:127, 0:P],
                              dbuf[:, 128:255], c.cst["identity"][:],
                              waits=[("vv", ev1)], inc="tt")
                    ev2 = vin(c, "tensor_copy", dtb[0:127, 128:256],
                              c.ps_d[0:127, 0:P], waits=[("tt", tr2)])
                    br(c, "d", "vv", ev2)
                    c.dbuf_rd = tr2
                    c.attn_rd = dg
                    qsl = slice(qi4 * P, (qi4 + 1) * P)
                    mmv(c, avp[0:64, qsl], c.cst["l1a_relv_mid"][:, 0:64],
                        dtb[:, 0:128], start=False, stop=False,
                        waits=[("vv", ev2)], inc=None)
                    mmv(c, avp[0:64, qsl], c.cst["l1a_relv_mid"][0:127, 64:128],
                        dtb[0:127, 128:256], start=False, stop=False,
                        inc=None)
                    ce = c.l1a_ce
                    base = qi4 * 384
                    for si, (cb, msk, lt) in enumerate(
                            ((1, "tri", rve16[0:1, 0:HD]),
                             (256, "trin", c.cst["l1a_rv0"][0:1, 0:HD]))):
                        mp_ = vin(c, "tensor_tensor", mprod[si][:, 0:127],
                                  attn_g[:, base + cb:base + cb + 127],
                                  c.cst[msk][:, 0:127], ALU.mult,
                                  waits=aw + [("ss", ce[f"a{si}"])])
                        ac_ = sin(c, "activation", mprod[si][:, 0:127],
                                  mprod[si][:, 0:127], AF.Identity,
                                  accum_out=macc[si][:, 0:1],
                                  waits=[("vv", mp_)])
                        ce[f"a{si}"] = ac_
                        cb_ = vin(c, "tensor_copy", mcb[si][:, 0:1],
                                  macc[si][:, 0:1],
                                  waits=[("ss", ac_), ("tt", ce[f"t{si}"])])
                        t_ = ins(c, "tensor", "transpose", c.ps_d[0:1, 0:P],
                                 mcb[si][:, 0:1], c.cst["identity"][:],
                                 waits=[("vv", cb_)] + bw(c, "d"), inc="tt")
                        ce[f"t{si}"] = t_
                        rc_ = vin(c, "tensor_copy", mrow[si][0:1, :],
                                  c.ps_d[0:1, 0:P],
                                  waits=[("tt", t_), ("tt", ce[f"m{si}"])])
                        br(c, "d", "vv", rc_)
                        mm_ = mmv(c, avp[0:64, qsl],
                                  lt, mrow[si][0:1, :],
                                  start=False, stop=False,
                                  waits=[("vv", rc_), ("vv", e16)], inc="tt")
                        ce[f"m{si}"] = mm_
                lsl = vin(c, "tensor_tensor", lrsel[0:32, :],
                          c.ps_e[0:32, 0:QG],
                          c.cst["selmask"][:, gsl], ALU.mult,
                          waits=[("tt", lr_mm)])
                br(c, "e", "vv", lsl)
                edm = mmv(c, avp[0:64, :], edgeLR[0:32, 0:HD],
                          lrsel[0:32, :], start=False, stop=True,
                          waits=[("vv", lsl), ("vv", ecv)], inc="tt")
                wv = vin(c, "tensor_tensor", stage64[0:64, :],
                         c.ps_c[0:64, 0:QG], bcq[0:64, :], ALU.mult,
                         waits=[("tt", edm), ("bd", odd_dma)])
                odd_dma = dma(c, "sync",
                              bass.AP(AVI, (pair * P + hh * HD) * S + g * QG,
                                      [[S, HD], [1, QG]]),
                              stage64[0:64, :], waits=[("vv", wv)], inc="bd")
                pr.fence("sync", "bd")
                av_done = wv
                br(c, "c", "vv", wv)
                c.scr_tt = pr.now("tt")
        v65_read = pr.now("tt")
        wpair_read = pr.now("tt")

    agv = pr.on("gpsimd", lambda e: e.collective_compute(
        "AllGather", ALU.bypass, ins=[c.dram["av_in"]],
        outs=[c.dram["av_out"]], replica_groups=[list(range(8))]),
        waits=[("bd", odd_dma)], inc="gg")
    w2add = 0
    mixlast = 0
    cpar = {0: 0, 1: 0}
    for pair in (range(NH // 2) if not getattr(Ctx, "SKIPC", False) else []):
        qstk2 = [kpair, vfm]
        tiles = [c.ps_a[:, 0:QG], c.ps_a[:, QG:2 * QG],
                 c.ps_b[:, 0:QG], c.ps_b[:, QG:2 * QG]]
        for jj in range(NCH):
            buf = qstk2[jj % 2]
            lw = dma(c, "sync", buf[:, 0:2048],
                     bass.AP(AVO, (pair * 8 * P + jj * P) * S,
                             [[S, P], [1, S]]),
                     waits=[("gg", agv)] +
                           ([("tt", cpar[jj % 2])] if cpar[jj % 2]
                            else [("tt", wpair_read)]),
                     inc="bd")
            pr.fence("sync", "bd")
            w0 = [("bd", lw), ("wd", mxc), ("ss", exp_done[0]),
                  ("ss", exp_done[1])] +                  (bw(c, "a") + bw(c, "b") if jj == 0 else [])
            for ct in range(4):
                r_ = mmv(c, tiles[ct], mixt[jj],
                         buf[:, ct * QG:(ct + 1) * QG],
                         start=(jj == 0), stop=(jj == NCH - 1),
                         waits=w0 if ct == 0 else [],
                         inc="tt" if ct == 3 else None)
                if ct == 3:
                    mixlast = r_
                    cpar[jj % 2] = r_
        evs = [vin(c, "tensor_tensor", ofm[pair][:, ct * QG:(ct + 1) * QG],
                   ofm[pair][:, ct * QG:(ct + 1) * QG], tiles[ct], ALU.add,
                   waits=[("tt", mixlast), ("bd", odd_dma)])
               for ct in range(4)]
        w2add = max(evs)
        br(c, "a", "vv", w2add); br(c, "b", "vv", w2add)
    wpair_read = pr.now("tt")

    ab.pop(); af.pop()
    wout = [ab.take(D, f"a_wo{i}") for i in range(NCH)]
    yb = [ab.take(QG, f"a_yb{i}") for i in range(NCH)]
    wout_cnt = max(load_w(c, wout[di],
                          c.dram["l1a_wo"][di * P:(di + 1) * P, :],
                          waits=[("tt", wpair_read)])
                   for di in range(NCH))
    pr.fence("gpsimd", "wd")
    first = [("wd", wout_cnt), ("bd", odd_dma), ("ss", exp_done[0]),
             ("ss", exp_done[1]), ("vv", av_done), ("vv", w2add)]
    for g in range(NG):
        gsl = slice(g * QG, (g + 1) * QG)
        ev = []
        for oc in range(NCH):
            reg = [c.ps_a, c.ps_b][oc % 2]
            ps = reg[:, (oc // 2 % 2) * QG:(oc // 2 % 2) * QG + QG]
            cnt = None
            for di in range(NCH):
                cnt = mmv(c, ps, wout[di][:, oc * P:(oc + 1) * P],
                          ofm[di][:, gsl],
                          start=(di == 0), stop=(di == NCH - 1),
                          waits=(first + bw(c, "a", "b") +
                                 ([("vv", ev[oc - 4])] if oc >= 4 else []))
                          if di == 0 else [],
                          inc="tt" if di == NCH - 1 else None)
            ev.append(vin(c, "tensor_scalar", yb[oc][:], ps,
                          c.cst["l1a_bo_c"][:, oc:oc + 1], None, ALU.add,
                          waits=[("tt", cnt)]))
        first = [("vv", max(ev))]
        br(c, "a", "vv", max(ev))
        br(c, "b", "vv", max(ev))
        post_ln(c, yb, g, [("vv", max(ev))])
    ab.pop(); af.pop()


def build_chain(c, blocks):
    c.w2slot = {}
    if "L0A" in blocks:
        c.ab.push()
        xln = [c.ab.take(S, f"xln{i}") for i in range(NCH)]
        xcnt = pre_ln(c, xln, c.ready)
        emit_l0a(c, xln, [("vv", xcnt)])
        c.ab.pop()
        c.pr.barrier()
    if "L0F" in blocks:
        c.ab.push()
        xi = [c.ab.take(S + 8, f"xi{i}") for i in range(NCH)]
        gz = [vin(c, "memset", xi[i][:, 0:1], 0.0) for i in range(NCH)]
        xcnt = pre_ln(c, [t[:, 1:1 + S] for t in xi],
                      c.ready + [("vv", max(gz))])
        xw = emit_l0f(c, xi, [("vv", xcnt)])
        for g in range(NG):
            post_ln(c, [xi[ch][:, 1 + g * QG:1 + (g + 1) * QG]
                        for ch in range(NCH)], g, xw)
        c.ab.pop()
        c.pr.barrier()
    if "L1A" in blocks:
        c.qpair_ss = 0
        c.bias_ss = 0
        c.ab.push()
        xln = [c.ab.take(S, f"axln{i}") for i in range(NCH)]
        xcnt = pre_ln(c, xln, c.ready)
        emit_l1a(c, xln, [("vv", xcnt)])
        c.ab.pop()
        c.pr.barrier()
    if "L1F" in blocks:
        c.ab.push()
        xln = [c.ab.take(S, f"nxln{i}") for i in range(NCH)]
        xcnt = pre_ln(c, xln, c.ready)
        emit_l1f(c, xln, [("vv", xcnt)])
        c.ab.pop()
        c.pr.barrier()


def build_program(blocks=BLOCKS):
    nc = bass.Bass("TRN2", target_bir_lowering=False)
    es = contextlib.ExitStack()
    nc._keepalive = es
    c = Ctx()
    c.nc, c.es = nc, es
    dram = {}

    def din(name, shape, dtype=F32):
        dram[name] = nc.dram_tensor(name, list(shape), dtype,
                                    kind="ExternalInput").ap()

    din("x_in", [D, S])
    din("l0a_wqkv", [D, 3 * D]); din("l0a_bqkv_c", [P, 24])
    din("l0a_wout", [D, D]); din("l0a_bout_c", [P, 8])
    for l in range(2):
        din(f"l0f{l}_wqkv", [D, 1536]); din(f"l0f{l}_wo", [512, D])
        din(f"l0f{l}_w1a", [D, HIDP]); din(f"l0f{l}_w1g", [D, HIDP])
        din(f"l0f{l}_w2", [HIDP, D])
        din(f"l0f{l}_b1a_c", [P, 22]); din(f"l0f{l}_b1g_c", [P, 22])
        din(f"l0f{l}_b2_c", [P, 8]); din(f"l0f{l}_lng_c", [P, 8])
        din(f"l0f{l}_lnb_c", [P, 8])
        din(f"l0f{l}_qscale", [1, HD]); din(f"l0f{l}_kscale", [1, HD])
    for w in ("wq", "wk", "wv", "wo"):
        din(f"l1a_{w}", [D, D])
    din("l1a_bq_c", [P, 8]); din("l1a_bk_c", [P, 8]); din("l1a_bv_c", [P, 8])
    din("l1a_bo_c", [P, 8])
    din("l1a_relkT", [P, 257]); din("l1a_relv_mid", [P, P])
    din("l1a_relv_edge", [2, HD]); din("l1a_rv0", [1, HD])
    din("tri", [P, P], BF16); din("trin", [P, P], BF16)
    din("l1a_mix", [8 * P, P])
    for nm, shp in (("aq_in", [8 * P, 2049]), ("aq_out", [64 * P, 2049]),
                    ("av_in", [8 * P, S]), ("av_out", [64 * P, S])):
        kw = {"addr_space": "Shared"} if nm.endswith("_out") else {}
        dram[nm] = nc.dram_tensor(nm, shp, BF16, kind="Internal", **kw).ap()
    din("l1f_wqkv", [D, 3 * D]); din("l1f_wproj", [D, D])
    din("l1f_bproj_c", [P, 8]); din("l1f_qg", [NH, HD])
    din("identity", [P, P], BF16)
    din("ones_row", [1, P], BF16); din("ones_row_f", [1, P])
    din("ones_col", [P, 1], BF16)
    din("idrow16", [16, 16])
    din("headmask16", [P, NCH * NH], BF16)
    din("mask_m4", [P, P], BF16); din("mask_m0", [P, P], BF16)
    din("lr_step", [P, NT * 32], BF16); din("selmask", [32, S], BF16)
    din("rot_qc", [S, HD]); din("rot_qs", [S, HD])
    din("rot_kc", [S, HD]); din("rot_ks", [S, HD])
    din("edge2", [2, 32], BF16)
    din("c2p65", [16, 1])
    din("pairsel2", [2, P], BF16)
    din("ones_2col", [P, 2], BF16)
    din("identity_f", [P, P])
    din("rotperm", [P, P], BF16)
    for nme in ("fm_qc", "fm_qs", "fm_kc", "fm_ks"):
        din(nme, [P, S], BF16)
    for l in range(2):
        din(f"l0f{l}_qscale_c", [P, 1])
        din(f"l0f{l}_kscale_c", [P, 1])
    c.dram = dram
    c.y_out = nc.dram_tensor("y_out", [D, S], F32, kind="ExternalOutput").ap()
    c.dbg_out = nc.dram_tensor("dbg_out", [P, 8192], F32, kind="ExternalOutput").ap()
    c.DBG = Ctx.DBG
    c.cc_in = nc.dram_tensor("cc_in", [1, 2], F32, kind="Internal")
    c.cc_out = nc.dram_tensor("cc_out", [1, 2], F32, kind="Internal",
                              addr_space="Shared")

    pr = Prog(nc, es)
    c.pr = pr
    for s in ("wd", "cd", "tt", "vv", "ss", "gg", "so", "ph", "bd"):
        pr.sem(s)

    c.xs = [es.enter_context(nc.sbuf_tensor(f"x{i}", [P, S + 8], F32))
            for i in range(NCH)]
    c.XD = [t[:, 1:1 + S] for t in c.xs]

    cst = {}
    c.cst = cst

    def sbc(name, shape, dtype=F32, cast=False):
        t = es.enter_context(nc.sbuf_tensor("c_" + name, list(shape), dtype))
        cst[name] = t
        if cast:
            pr.on("gpsimd", lambda e, t=t, n=name: e.dma_start(
                out=t[:], in_=dram[n]), inc="wd", inc_n=16)
        else:
            pr.on("sync", lambda e, t=t, n=name: e.dma_start(
                out=t[:], in_=dram[n]), inc="cd", inc_n=16)

    for n, sh, dt in [
        ("identity", [P, P], BF16), ("ones_row", [1, P], BF16),
        ("ones_row_f", [1, P], F32), ("ones_col", [P, 1], BF16),
        ("idrow16", [16, 16], F32), ("headmask16", [P, NCH * NH], BF16),
        ("mask_m4", [P, P], BF16), ("mask_m0", [P, P], BF16),
        ("lr_step", [P, NT * 32], BF16), ("selmask", [32, S], BF16),
        ("edge2", [2, 32], BF16),
        ("l0a_bqkv_c", [P, 24], F32), ("l0a_bout_c", [P, 8], F32),
        ("l1a_bq_c", [P, 8], F32), ("l1a_bk_c", [P, 8], F32),
        ("l1a_bv_c", [P, 8], F32), ("l1a_bo_c", [P, 8], F32),
        ("l1a_relv_edge", [2, HD], F32),
        ("tri", [P, P], BF16), ("trin", [P, P], BF16),
        ("l1f_bproj_c", [P, 8], F32), ("l1f_qg", [NH, HD], F32),
        ("c2p65", [16, 1], F32),
        ("pairsel2", [2, P], BF16), ("rotperm", [P, P], BF16),
        ("ones_2col", [P, 2], BF16), ("identity_f", [P, P], F32),
    ]:
        sbc(n, sh, dt)
    for l in range(2):
        sbc(f"l0f{l}_b1a_c", [P, 22]); sbc(f"l0f{l}_b1g_c", [P, 22])
        sbc(f"l0f{l}_b2_c", [P, 8]); sbc(f"l0f{l}_lng_c", [P, 8])
        sbc(f"l0f{l}_lnb_c", [P, 8])
        sbc(f"l0f{l}_qscale", [1, HD]); sbc(f"l0f{l}_kscale", [1, HD])
        sbc(f"l0f{l}_qscale_c", [P, 1]); sbc(f"l0f{l}_kscale_c", [P, 1])
    sbc("l1a_relkT", [P, 257], BF16, cast=True)
    sbc("l1a_relv_mid", [P, P], BF16, cast=True)
    sbc("l1a_rv0", [1, HD], BF16, cast=True)
    for ch in range(NCH):
        pr.on("sync", lambda e, ch=ch: e.dma_start(
            out=c.XD[ch], in_=dram["x_in"][ch * P:(ch + 1) * P, :]),
            inc="cd", inc_n=16)
    pr.fence("sync", "cd")
    pr.fence("gpsimd", "wd")
    c.ready = [("cd", pr.now("cd")), ("wd", pr.now("wd"))]

    c.ab = Arena(nc, es, "arena_b16", BF16, 58880)
    c.af = Arena(nc, es, "arena_f32", F32, 4288)
    c.lnsq = [c.ab.take(QG, "lnsq0"), c.ab.take(QG, "lnsq1")]
    c.nbcol = c.af.take(4, "nbcol")
    c.nbrow = c.af.take(16, "nbrow")

    c.ps_a = es.enter_context(nc.psum_tensor("ps_a", [P, 1024], F32))
    c.ps_b = es.enter_context(nc.psum_tensor("ps_b", [P, 1024], F32))
    c.ps_c = es.enter_context(nc.psum_tensor("ps_c", [P, 512], F32))
    c.ps_e = es.enter_context(nc.psum_tensor("ps_e", [P, 512], F32))
    c.ps_f = es.enter_context(nc.psum_tensor("ps_f", [P, 512], F32))
    c.ps_d = es.enter_context(nc.psum_tensor("ps_d", [P, 1024], BF16))

    c.bank = {k: ("vv", 0) for k in ("a", "b", "c", "e", "f", "d", "nb")}
    c.bias_read = 0
    c.kpair_ss = 0
    c.qpair_ss = 0

    build_chain(c, blocks)

    for ch in range(NCH):
        pr.on("sync", lambda e, ch=ch: e.dma_start(
            out=c.y_out[ch * P:(ch + 1) * P, :], in_=c.XD[ch]),
            waits=[("ph", pr.now("ph")), ("vv", pr.now("vv")),
                   ("ss", pr.now("ss")), ("tt", pr.now("tt"))],
            inc="so", inc_n=16)
    pr.on("sync", lambda e: e.nop(), waits=[("so", pr.now("so"))])

    pr.check()
    print(f"arena peaks: ab={c.ab.peak}/{c.ab.words} af={c.af.peak}/{c.af.words}",
          flush=True)
    nc._ctx = c

    with nc.Block() as block:
        for ename in ENGINES:
            def mk(ename):
                def body(eng):
                    for fn in pr.q[ename]:
                        fn(eng)
                return body
            getattr(block, ename)(mk(ename))
    return nc


_NC_CACHE = {}


def kernel(**inputs):
    maps = stage_inputs(inputs)
    key = tuple(BLOCKS)
    if key not in _NC_CACHE:
        _NC_CACHE[key] = build_program(BLOCKS)
    nc = _NC_CACHE[key]
    res = run_bass_kernel_spmd(nc, maps, core_ids=list(range(8)))
    global _DBG_RES
    _DBG_RES = [np.asarray(r["dbg_out"], np.float32) for r in res.results] \
        if Ctx.DBG else []
    return np.stack([np.asarray(r["y_out"], np.float32).T for r in res.results])


_DBG_RES = []
